# version 09
# baseline (speedup 1.0000x reference)
import numpy as np
import concourse.bass as bass
import concourse.mybir as mybir
from concourse.bass_utils import run_bass_kernel_spmd
from contextlib import ExitStack

F32 = mybir.dt.float32
BF16 = mybir.dt.bfloat16
I32 = mybir.dt.int32
AF = mybir.ActivationFunctionType
ALU = mybir.AluOpType
PI = float(np.pi)

D, DC = 1024, 8
FF, FC = 4096, 32
H, DH = 16, 64
T = 512
NA, NB = 8192, 4096
NTA, NTB = NA // T, NB // T
SCH = T // 128
EPS_LN = 1e-5
EPS_Z = 1e-6

SB_BASE = 16512
SB_TOP = 229344


class Lay:
    def __init__(self, nc, base):
        self.nc = nc
        self.off = base
        self.end = base

    def al(self, name, shape, dtype):
        per = int(np.prod(shape[1:])) * mybir.dt.size(dtype)
        self.off = -(-self.off // 32) * 32
        h = self.nc.alloc_sbuf_tensor_at(name, list(shape), dtype, offset=self.off)
        self.off += per
        self.end = max(self.end, self.off)
        return h


class KB:
    ENGS = ("t", "v", "s", "g", "y")

    def __init__(self, debug=False):
        self.debug = debug
        self.nc = bass.Bass("TRN2", target_bir_lowering=False)
        self.qs = {e: [] for e in self.ENGS}
        self.cnt = {k: 0 for k in ("p", "v", "s", "g", "gs", "gw", "gc", "gq", "gx", "ge", "yw", "yw2", "yo", "od")}
        self.obs = {e: {} for e in self.ENGS}
        self.semh = {}

    def op(self, eng, fn, inc=None, n=1, fence=False):
        if fence != "never" and inc in ("v", "s", "g"):
            fence = True
        if fence is True and inc is not None:
            self.qs[eng].append(fn)
            if eng == "v":
                ffn = lambda e: e.tensor_copy(self.fen_v[0:1, 0:8], self.fen_v[0:1, 8:16])
            elif eng == "s":
                ffn = lambda e: e.activation(self.fen_s[0:1, 0:8], self.fen_s[0:1, 8:16], AF.Copy, bias=0.0, scale=1.0)
            else:
                ffn = lambda e: e.tensor_copy(self.fen_g[0:1, 0:8], self.fen_g[0:1, 8:16])
            return self.op(eng, ffn, inc=inc, n=n, fence="never")
        if inc is None:
            self.qs[eng].append(fn)
            return None
        self.cnt[inc] += n
        val = self.cnt[inc]
        sh = self.semh

        def wrapped(e, fn=fn, inc=inc, n=n):
            fn(e).then_inc(sh[inc], n)

        self.qs[eng].append(wrapped)
        if inc in ("p", "v", "s", "g"):
            ob = self.obs[eng]
            ob[inc] = max(ob.get(inc, 0), val)
        return (inc, val)

    def vfence(self):
        tok = self.op("v", lambda e: e.tensor_copy(self.fen_v[0:1, 0:8], self.fen_v[0:1, 8:16]), inc="v", fence="never")
        sh = self.semh
        self.qs["v"].append(lambda e, v=tok[1]: e.wait_ge(sh["v"], v))
        return tok

    def tap(self, name, src_ap, tok, shape, dtype):
        if not self.debug:
            return None
        d = self.nc.dram_tensor("dbg_" + name, list(shape), dtype, kind="ExternalOutput").ap()
        self.need("y", tok)
        return self.op("y", lambda e, d=d, s=src_ap: e.dma_start(out=d, in_=s), inc="od", n=16)

    def need(self, eng, *tokens):
        sh = self.semh
        for tok in tokens:
            if tok is None:
                continue
            if isinstance(tok, list):
                self.need(eng, *tok)
                continue
            sem, val = tok
            ob = self.obs[eng]
            if ob.get(sem, 0) >= val:
                continue
            ob[sem] = val
            self.qs[eng].append(lambda e, sem=sem, val=val: e.wait_ge(sh[sem], val))

    def alloc(self):
        nc = self.nc
        self.xA = nc.dram_tensor("xA", [D, NA], F32, kind="ExternalInput").ap()
        self.wqt = nc.dram_tensor("wqt", [128, DC * D], F32, kind="ExternalInput").ap()
        self.wkt = nc.dram_tensor("wkt", [128, DC * D], F32, kind="ExternalInput").ap()
        self.wvt = nc.dram_tensor("wvt", [128, DC * D], F32, kind="ExternalInput").ap()
        self.wot = nc.dram_tensor("wot", [128, DC * D], F32, kind="ExternalInput").ap()
        self.w1p = nc.dram_tensor("w1p", [DC, 128, FF], F32, kind="ExternalInput").ap()
        self.w2p = nc.dram_tensor("w2p", [DC, 128, FF], F32, kind="ExternalInput").ap()
        self.bkr = nc.dram_tensor("bkr", [1, D], F32, kind="ExternalInput").ap()
        self.bvr = nc.dram_tensor("bvr", [1, D], F32, kind="ExternalInput").ap()
        self.bqc_i = nc.dram_tensor("bqc_i", [128, DC], F32, kind="ExternalInput").ap()
        self.boc_i = nc.dram_tensor("boc_i", [128, DC], F32, kind="ExternalInput").ap()
        self.b1c_i = nc.dram_tensor("b1c_i", [128, FC], F32, kind="ExternalInput").ap()
        self.b1r_i = nc.dram_tensor("b1r_i", [1, FF], F32, kind="ExternalInput").ap()
        self.b2c_i = nc.dram_tensor("b2c_i", [128, DC], F32, kind="ExternalInput").ap()
        self.th_i = nc.dram_tensor("th_i", [128, FC], F32, kind="ExternalInput").ap()
        self.wr_i = nc.dram_tensor("wr_i", [128, FC], F32, kind="ExternalInput").ap()
        self.wi_i = nc.dram_tensor("wi_i", [128, FC], F32, kind="ExternalInput").ap()
        self.ones_i = nc.dram_tensor("ones_i", [128, 128], F32, kind="ExternalInput").ap()
        self.sel_i = nc.dram_tensor("sel_i", [16, DC * 128], F32, kind="ExternalInput").ap()
        self.nonce_i = nc.dram_tensor("nonce_i", [1, 2], I32, kind="ExternalInput").ap()
        self.side_i = nc.dram_tensor("side_i", [1, 2], I32, kind="ExternalInput").ap()
        self.out_d = nc.dram_tensor("out", [D, NB], F32, kind="ExternalOutput").ap()
        self.w1bf = nc.dram_tensor("w1bf", [DC, 128, FF], BF16)
        self.w2bf = nc.dram_tensor("w2bf", [DC, 128, FF], BF16)
        self.shpay = nc.dram_tensor("shpay", [2, 128, 520], F32, addr_space="Shared")
        self.shflag = nc.dram_tensor("shflag", [2, 16], I32, addr_space="Shared")

        st = Lay(nc, SB_BASE)
        self.xr = st.al("xr", [128, DC * NB], BF16)
        self.kv_acc = st.al("kv_acc", [128, 520], F32)
        self.kv_peer = st.al("kv_peer", [128, 520], F32)
        self.kv_bf = st.al("kv_bf", [128, 520], BF16)
        self.bd_bf = st.al("bd_bf", [128, 128], BF16)
        self.ones_bf = st.al("ones_bf", [128, 128], BF16)
        self.bqc = st.al("bqc", [128, DC], F32)
        self.nbqc = st.al("nbqc", [128, DC], F32)
        self.boc = st.al("boc", [128, DC], F32)
        self.b1c = st.al("b1c", [128, FC], F32)
        self.b1s1 = st.al("b1s1", [128, FC], F32)
        self.b1s2 = st.al("b1s2", [128, FC], F32)
        self.b2c = st.al("b2c", [128, DC], F32)
        self.qA = st.al("qA", [128, FC], F32)
        self.qB = st.al("qB", [128, FC], F32)
        self.sel_bf = st.al("sel_bf", [16, DC * 128], BF16)
        self.a_row = st.al("a_row", [1, NB], BF16)
        self.b_row = st.al("b_row", [1, NB], BF16)
        self.nonce_sb = st.al("nonce_sb", [1, 16], I32)
        self.side_sb = st.al("side_sb", [1, 2], I32)
        self.rowm = st.al("rowm", [1, 512], F32)
        self.rowv = st.al("rowv", [1, 512], F32)
        self.rowt = st.al("rowt", [1, 512], F32)
        self.b1r = st.al("b1r", [1, FF], BF16)
        self.pihalf = st.al("pihalf", [128, 1], F32)
        self.ones_row = st.al("ones_row", [1, 512], BF16)
        self.rowa = st.al("rowa", [1, 512], BF16)
        self.rowb = st.al("rowb", [1, 512], BF16)
        self.qs1 = st.al("qs1", [128, FC], F32)
        self.qs2 = st.al("qs2", [128, FC], F32)
        self.qs3 = st.al("qs3", [128, FC], F32)
        self.th_c = st.al("th_c", [128, FC], F32)
        self.wr_c = st.al("wr_c", [128, FC], F32)
        self.wi_c = st.al("wi_c", [128, FC], F32)
        self.fen_v = st.al("fen_v", [1, 16], F32)
        self.fen_s = st.al("fen_s", [1, 16], F32)
        self.fen_g = st.al("fen_g", [1, 16], F32)
        rb = -(-st.off // 64) * 64

        A = Lay(nc, rb)
        self.wkt_bf = A.al("wkt_bf", [128, DC * D], BF16)
        self.wvt_bf = A.al("wvt_bf", [128, DC * D], BF16)
        self.xa0 = A.al("xa0", [128, DC * T], BF16)
        self.xa1 = A.al("xa1", [128, DC * T], BF16)
        self.ln1A = A.al("ln1A", [128, DC * T], BF16)
        self.k_sb = A.al("k_sb", [128, D], BF16)
        self.v_sb = A.al("v_sb", [128, H * 65], BF16)
        self.ab_sbA = A.al("ab_sbA", [128, 1024], F32)
        self.bkb = A.al("bkb", [128, D], F32)
        self.bvb = A.al("bvb", [128, D], F32)
        self.kpre = A.al("kpre", [128, D], F32)
        self.m0_sb = A.al("m0_sb", [128, D], F32)
        self.e_sbA = A.al("e_sbA", [128, D], F32)
        self.xsq = [A.al("xsq0", [128, T], BF16), A.al("xsq1", [128, T], BF16)]
        self.ptmpA = A.al("ptmpA", [128, T], BF16)
        self.bkr_bf = A.al("bkr_bf", [1, D], BF16)
        self.bvr_bf = A.al("bvr_bf", [1, D], BF16)

        B = Lay(nc, rb)
        self.wqt_bf = B.al("wqt_bf", [128, DC * D], BF16)
        self.wot_bf = B.al("wot_bf", [128, DC * D], BF16)
        self.ln1B = B.al("ln1B", [128, DC * T], BF16)
        self.qraw = B.al("qraw", [128, DC * T], BF16)
        self.qz = B.al("qz", [128, DC * T], BF16)
        self.attn_sb = B.al("attn_sb", [128, DC * T], BF16)
        self.y_sb = B.al("y_sb", [128, DC * T], BF16)
        self.ab_sbB = B.al("ab_sbB", [128, 1024], F32)
        self.relu1 = B.al("relu1", [128, T], F32)
        self.e_sbB = [B.al("e_sbB0", [128, T], F32), B.al("e_sbB1", [128, T], F32)]
        self.p2_sb = [B.al("p2_sb0", [128, T], F32), B.al("p2_sb1", [128, T], F32)]
        self.ysq = [B.al("ysq0", [128, T], BF16), B.al("ysq1", [128, T], BF16)]
        self.t1_sbB = B.al("t1_sbB", [128, T], BF16)
        self.t2_sbB = B.al("t2_sbB", [128, T], BF16)
        self.zden = B.al("zden", [16, T], F32)
        self.zbuf = B.al("zbuf", [16, T], F32)
        self.zbf = B.al("zbf", [16, T], BF16)
        self.ptmpB = B.al("ptmpB", [128, T], BF16)

        C = Lay(nc, rb)
        self.w1b = [C.al("w1b0", [128, FF], BF16), C.al("w1b1", [128, FF], BF16)]
        self.w2b = [C.al("w2b0", [128, FF], BF16), C.al("w2b1", [128, FF], BF16)]
        self.ln2C = C.al("ln2C", [128, DC * T], BF16)
        self.hidden = C.al("hidden", [128, FC * T], BF16)
        self.fout = C.al("fout", [128, DC * T], BF16)
        self.ab_sbC = C.al("ab_sbC", [128, 1024], F32)
        self.sd_sb = [C.al("sd_sb0", [128, T], BF16), C.al("sd_sb1", [128, T], BF16)]
        self.cd_sb = [C.al("cd_sb0", [128, T], BF16), C.al("cd_sb1", [128, T], BF16)]
        self.t12_sb = [C.al("t12_sb0", [128, T], BF16), C.al("t12_sb1", [128, T], BF16)]
        self.t1_sbC = C.al("t1_sbC", [128, T], BF16)
        self.t2f_sb = C.al("t2f_sb", [128, T], BF16)
        self.u_sb = [C.al("u_sb0", [128, T], BF16), C.al("u_sb1", [128, T], BF16)]
        self.fsq = [C.al("fsq0", [128, T], BF16), C.al("fsq1", [128, T], BF16)]
        self.outs = [C.al("outs0", [128, T], F32), C.al("outs1", [128, T], F32)]
        self.ptmpC = C.al("ptmpC", [128, T], BF16)

        assert max(A.end, B.end, C.end) <= SB_TOP, (st.off, A.end, B.end, C.end)

        self.kban = nc.alloc_psum_tensor("kban", [128, 1024], F32)
        self.vban = nc.alloc_psum_tensor("vban", [128, 1024], F32)
        self.sban = nc.alloc_psum_tensor("sban", [128, 512], F32)
        self.tban = nc.alloc_psum_tensor("tban", [128, 512], F32)
        self.kvA = nc.alloc_psum_tensor("kvA", [128, 512], F32)
        self.kvB = nc.alloc_psum_tensor("kvB", [128, 512], F32)

    def build(self):
        self.alloc()
        with ExitStack() as stack:
            E = stack.enter_context
            for name in ("p", "v", "s", "g", "gs", "gw", "gc", "gq", "gx", "ge", "yw", "yw2", "yo", "od"):
                self.semh[name] = E(self.nc.semaphore(name))
            block = E(self.nc.Block())
            self.emit()
            for eng, sect in (("t", block.tensor), ("v", block.vector), ("s", block.scalar),
                              ("g", block.gpsimd), ("y", block.sync)):
                def body(e, eng=eng):
                    for fn in self.qs[eng]:
                        fn(e)
                sect(body)
        return self.nc

    def emit(self):
        self.setup()
        st = self.phase_a()
        st = self.exchange(st)
        st = self.phase_b(st)
        self.phase_c(st)

    def setup(self):
        op, need = self.op, self.need
        toks = {}
        for nm, dst, src in (
            ("ones", self.ones_bf, self.ones_i), ("sel", self.sel_bf, self.sel_i),
            ("bqc", self.bqc, self.bqc_i), ("boc", self.boc, self.boc_i),
            ("b1c", self.b1c, self.b1c_i), ("b2c", self.b2c, self.b2c_i),
            ("th", self.th_c, self.th_i), ("wr", self.wr_c, self.wr_i),
            ("wi", self.wi_c, self.wi_i),
            ("bkr", self.bkr_bf, self.bkr), ("bvr", self.bvr_bf, self.bvr),
            ("b1r", self.b1r, self.b1r_i),
        ):
            toks[nm] = op("g", lambda e, d=dst, s=src: e.dma_start(out=d[:, :], in_=s), inc="gs", n=16)
        toks["nonce"] = op("g", lambda e: e.dma_start(out=self.nonce_sb[0:1, 0:2], in_=self.nonce_i), inc="gs", n=16)
        toks["side"] = op("g", lambda e: e.dma_start(out=self.side_sb[0:1, :], in_=self.side_i), inc="gs", n=16)
        tok_gs_all = ("gs", self.cnt["gs"])
        for nm in toks:
            toks[nm] = tok_gs_all
        self.tok_excfg = [tok_gs_all]

        self.xa = [self.xa0, self.xa1]
        self.tok_xa = [self.load_xa(0), self.load_xa(1)]

        self.tok_wk, self.tok_wv = [], []
        for w_bf, w_f, lst in ((self.wkt_bf, self.wkt, self.tok_wk), (self.wvt_bf, self.wvt, self.tok_wv)):
            for q4 in range(4):
                c0, c1 = q4 * (DC * D // 4), (q4 + 1) * (DC * D // 4)
                lst.append(op("g", lambda e, d=w_bf, s=w_f, c0=c0, c1=c1:
                              e.dma_start(out=d[:, c0:c1], in_=s[:, c0:c1], max_dma_last_dim=8192),
                              inc="gw", n=16))
        tok_gw_all = ("gw", self.cnt["gw"])
        self.tok_wk = [tok_gw_all]
        self.tok_wv = [tok_gw_all]
        self.big_casts = [("w1", j) for j in range(DC)] + [("w2", j) for j in range(DC)]
        self.tok_w1bf = [None] * DC
        self.tok_w2bf = [None] * DC

        self.tok_vones = op("v", lambda e: e.memset(self.v_sb[:, :], 1.0), inc="v")
        op("v", lambda e: e.memset(self.ones_row[0:1, :], 1.0), inc="v")
        op("v", lambda e: e.memset(self.pihalf[:, 0:1], PI / 2), inc="v")

        need("v", toks["th"], toks["wr"], toks["wi"], toks["b1c"], toks["bqc"])
        v = lambda fn: op("v", fn, inc="v")
        v(lambda e: e.tensor_scalar(self.qs1[:, :], self.th_c[:, :], 1.0 / (2 * PI), None, ALU.mult, ALU.bypass))
        self.vfence()
        qs2_i = self.qs2.bitcast(I32)
        v(lambda e, q=qs2_i: e.tensor_copy(q[:, :], self.qs1[:, :]))
        self.vfence()
        v(lambda e, q=qs2_i: e.tensor_copy(self.qs1[:, :], q[:, :]))
        self.vfence()
        tok_thred = v(lambda e: e.scalar_tensor_tensor(self.qs1[:, :], self.qs1[:, :], -2 * PI, self.th_c[:, :], ALU.mult, ALU.add))
        need("s", tok_thred)
        self.op("s", lambda e: e.activation(self.qs2[:, :], self.qs1[:, :], AF.Sin, bias=0.0, scale=1.0), inc="s")
        tok_s2 = self.op("s", lambda e: e.activation(self.qs3[:, :], self.qs1[:, :], AF.Sin, bias=0.0, scale=0.5), inc="s")
        need("v", tok_s2)
        v(lambda e: e.tensor_tensor(self.qs1[:, :], self.qs3[:, :], self.qs3[:, :], ALU.mult))
        v(lambda e: e.tensor_scalar(self.qs1[:, :], self.qs1[:, :], -2.0, 1.0, ALU.mult, ALU.add))
        self.vfence()
        v(lambda e: e.tensor_tensor(self.qs3[:, :], self.qs1[:, :], self.wr_c[:, :], ALU.mult))
        v(lambda e: e.tensor_tensor(self.th_c[:, :], self.qs2[:, :], self.wi_c[:, :], ALU.mult))
        v(lambda e: e.tensor_tensor(self.qs3[:, :], self.qs3[:, :], self.th_c[:, :], ALU.add))
        v(lambda e: e.tensor_tensor(self.qs3[:, :], self.qs3[:, :], self.wi_c[:, :], ALU.mult))
        v(lambda e: e.tensor_scalar(self.qs3[:, :], self.qs3[:, :], 0.1, None, ALU.mult, ALU.bypass))
        self.vfence()
        v(lambda e: e.tensor_tensor(self.qA[:, :], self.qs3[:, :], self.qs2[:, :], ALU.mult))
        v(lambda e: e.tensor_tensor(self.qB[:, :], self.qs3[:, :], self.qs1[:, :], ALU.mult))
        v(lambda e: e.tensor_scalar(self.nbqc[:, :], self.bqc[:, :], -1.0, None, ALU.mult, ALU.bypass))

        need("t", self.tok_vones, toks["ones"], toks["bkr"], toks["bvr"])
        op("t", lambda e: e.matmul(self.sban[:, :], self.ones_bf[0:1, 0:128], self.bkr_bf[0:1, 0:512], start=True, stop=True), inc="p")
        tk = op("t", lambda e: e.matmul(self.tban[:, :], self.ones_bf[0:1, 0:128], self.bkr_bf[0:1, 512:1024], start=True, stop=True), inc="p")
        need("s", tk)
        self.op("s", lambda e: e.activation(self.bkb[:, 0:512], self.sban[:, :], AF.Copy, bias=0.0, scale=1.0), inc="s")
        tok_kb = self.op("s", lambda e: e.activation(self.bkb[:, 512:1024], self.tban[:, :], AF.Copy, bias=0.0, scale=1.0), inc="s")
        need("t", tok_kb)
        op("t", lambda e: e.matmul(self.sban[:, :], self.ones_bf[0:1, 0:128], self.bvr_bf[0:1, 0:512], start=True, stop=True), inc="p")
        tv = op("t", lambda e: e.matmul(self.tban[:, :], self.ones_bf[0:1, 0:128], self.bvr_bf[0:1, 512:1024], start=True, stop=True), inc="p")
        need("s", tv)
        self.op("s", lambda e: e.activation(self.bvb[:, 0:512], self.sban[:, :], AF.Copy, bias=0.0, scale=1.0), inc="s")
        self.tok_setup_banks = self.op("s", lambda e: e.activation(self.bvb[:, 512:1024], self.tban[:, :], AF.Copy, bias=0.0, scale=1.0), inc="s")

    def load_xa(self, t):
        par = t % 2
        src = self.xA[:, t * T:(t + 1) * T].rearrange("(cc p) s -> p cc s", p=128)
        dst = self.xa[par][:, :].rearrange("p (cc s) -> p cc s", s=T)
        sh = self.semh
        if t >= 2:
            prior = self.cnt["gx"]
            self.qs["g"].append(lambda e, v=prior: e.wait_ge(sh["gx"], v))
        return self.op("g", lambda e, d=dst, s=src: e.dma_start(out=d, in_=s, max_dma_last_dim=8192), inc="gx", n=16)

    def emit_big_cast(self):
        if not self.big_casts:
            return
        which, j = self.big_casts.pop(0)
        w_bf, w_f = (self.w1bf, self.w1p) if which == "w1" else (self.w2bf, self.w2p)
        self.op("g", lambda e, d=w_bf, s=w_f, j=j:
                e.dma_start(out=d.ap()[j, :, :], in_=s[j, :, :], max_dma_last_dim=8192), inc="gc", n=16)
        if not self.big_casts:
            tok = ("gc", self.cnt["gc"])
            self.tok_w1bf = [tok] * DC
            self.tok_w2bf = [tok] * DC

    def ln_rows(self, sum_ap, sq_ap, stat_tok, rows_free_tok):
        op, need = self.op, self.need
        need("v", stat_tok, rows_free_tok)
        op("v", lambda e: e.tensor_scalar(self.rowm[0:1, :], sum_ap, 1.0 / D, None, ALU.mult, ALU.bypass))
        op("v", lambda e: e.tensor_scalar(self.rowv[0:1, :], sq_ap, 1.0 / D, None, ALU.mult, ALU.bypass))
        op("v", lambda e: e.tensor_tensor(self.rowt[0:1, :], self.rowm[0:1, :], self.rowm[0:1, :], ALU.mult))
        op("v", lambda e: e.tensor_tensor(self.rowv[0:1, :], self.rowv[0:1, :], self.rowt[0:1, :], ALU.subtract))
        tok_v = op("v", lambda e: e.tensor_scalar(self.rowv[0:1, :], self.rowv[0:1, :], EPS_LN, None, ALU.add, ALU.bypass), inc="v", fence=True)
        need("s", tok_v)
        op("s", lambda e: e.activation(self.rowt[0:1, :], self.rowv[0:1, :], AF.Ln, bias=0.0, scale=1.0))
        tok_a = op("s", lambda e: e.activation(self.rowa[0:1, :], self.rowt[0:1, :], AF.Exp, bias=0.0, scale=-0.5), inc="s", fence=True)
        need("v", tok_a)
        tok_b = op("v", lambda e: e.scalar_tensor_tensor(self.rowb[0:1, :], self.rowm[0:1, :], -1.0, self.rowa[0:1, :], ALU.mult, ALU.mult), inc="v", fence=True)
        return tok_b

    def bcast_ab(self, a_ps, b_ps, rows_tok, bank_free_tok):
        op, need = self.op, self.need
        need("t", rows_tok, bank_free_tok)
        op("t", lambda e: e.matmul(a_ps, self.ones_bf[0:1, 0:128], self.rowa[0:1, :], start=True, stop=True))
        return op("t", lambda e: e.matmul(b_ps, self.ones_bf[0:1, 0:128], self.rowb[0:1, :], start=True, stop=True), inc="p")

    def evac_ab(self, a_ps, b_ps, ab_sb, pe_tok, consumer_tok):
        op, need = self.op, self.need
        need("s", pe_tok, consumer_tok)
        op("s", lambda e: e.activation(ab_sb[:, 0:512], a_ps, AF.Copy, bias=0.0, scale=1.0))
        return op("s", lambda e: e.activation(ab_sb[:, 512:1024], b_ps, AF.Copy, bias=0.0, scale=1.0), inc="s", fence=True)

    def phase_a(self):
        op, need = self.op, self.need
        tok_stats_free = self.tok_setup_banks
        tok_ab_cons = None
        tok_rows_free = None
        tok_kban_free = None
        tok_vban_free = None
        tok_ksb_free = None
        tok_vsb_free = self.tok_vones
        tok_ln1_free = None
        tok_eA_free = None
        tok_xa_free = [None, None]
        kv_started = [False, False]
        tok_kv_last = None
        tok_ln1 = None

        for t in range(NTA):
            par = t % 2
            tok_x = self.tok_xa[t]
            self.emit_big_cast()
            if t >= NTA - 4:
                while self.big_casts:
                    self.emit_big_cast()

            xa = self.xa[par]
            need("t", tok_x, tok_stats_free)
            for cc in range(DC):
                op("t", lambda e, cc=cc, xa=xa: e.matmul(
                    self.sban[0:1, :], self.ones_bf[:, 0:1], xa[:, cc * T:(cc + 1) * T],
                    start=(cc == 0), stop=(cc == DC - 1)),
                    inc=("p" if cc == DC - 1 else None))
            tok_sq_last = None
            tok_stat = None
            statq = [None, None]
            for cc in range(DC):
                xq = self.xsq[cc % 2]
                need("s", tok_x, statq[cc % 2])
                tok_sq_last = self.op("s", lambda e, cc=cc, xa=xa, xq=xq: e.activation(
                    xq[:, :], xa[:, cc * T:(cc + 1) * T], AF.Square, bias=0.0, scale=1.0), inc="s", fence=True)
                need("t", tok_sq_last)
                tok_stat = op("t", lambda e, cc=cc, xq=xq: e.matmul(
                    self.tban[0:1, :], self.ones_bf[:, 0:1], xq[:, :],
                    start=(cc == 0), stop=(cc == DC - 1)), inc="p")
                statq[cc % 2] = tok_stat
            need("v", tok_rows_free)
            rows_tok = self.ln_rows(self.sban[0:1, 0:512], self.tban[0:1, 0:512], tok_stat, tok_rows_free)
            tok_xr_last = None
            if t < NTB:
                op("v", lambda e, t=t: e.tensor_copy(self.a_row[0:1, t * T:(t + 1) * T], self.rowa[0:1, :]))
                rows_tok = op("v", lambda e, t=t: e.tensor_copy(self.b_row[0:1, t * T:(t + 1) * T], self.rowb[0:1, :]), inc="v", fence=True)
                need("v", tok_x)
                for cc in range(DC):
                    tok_xr_last = op("v", lambda e, t=t, cc=cc, xa=xa: e.tensor_copy(
                        self.xr[:, cc * NB + t * T: cc * NB + (t + 1) * T], xa[:, cc * T:(cc + 1) * T]),
                        inc=("v" if cc == DC - 1 else None), fence=(cc == DC - 1))
            tok_rows_free = self.bcast_ab(self.sban[:, :], self.tban[:, :], rows_tok, None)
            tok_ab = self.evac_ab(self.sban[:, :], self.tban[:, :], self.ab_sbA, tok_rows_free, tok_ab_cons)
            tok_stats_free = tok_ab
            need("g", tok_ab, tok_ln1_free)
            for cc in range(DC):
                op("g", lambda e, cc=cc, xa=xa: e.tensor_tensor(
                    self.ptmpA[:, :], xa[:, cc * T:(cc + 1) * T], self.ab_sbA[:, 0:512], ALU.mult), inc="g")
                fence_ln1 = (cc == DC - 1)
                tok_ln1 = op("g", lambda e, cc=cc: e.tensor_tensor(
                    self.ln1A[:, cc * T:(cc + 1) * T], self.ptmpA[:, :], self.ab_sbA[:, 512:1024], ALU.add), inc="g", fence=fence_ln1)
            tok_ab_cons = tok_ln1
            tok_xa_free[par] = [tok_sq_last, tok_xr_last, tok_ln1]
            tap_ln1A = self.tap("ln1A", self.ln1A[:, :], tok_ln1, [128, DC * T], BF16) if t == 0 else None

            for sc in range(SCH):
                need("t", tok_ln1, tok_kban_free, self.tok_wk[-1])
                tok_kp = None
                for half in range(2):
                    for cc in range(DC):
                        tok_kp = op("t", lambda e, sc=sc, cc=cc, half=half: e.matmul(
                            self.kban[:, half * 512:(half + 1) * 512],
                            self.ln1A[:, cc * T + sc * 128: cc * T + (sc + 1) * 128],
                            self.wkt_bf[:, cc * D + half * 512: cc * D + (half + 1) * 512],
                            start=(cc == 0), stop=(cc == DC - 1)),
                            inc=("p" if (half == 1 and cc == DC - 1) else None))
                need("t", tok_vban_free, self.tok_wv[-1])
                tok_vp = None
                for half in range(2):
                    for cc in range(DC):
                        tok_vp = op("t", lambda e, sc=sc, cc=cc, half=half: e.matmul(
                            self.vban[:, half * 512:(half + 1) * 512],
                            self.ln1A[:, cc * T + sc * 128: cc * T + (sc + 1) * 128],
                            self.wvt_bf[:, cc * D + half * 512: cc * D + (half + 1) * 512],
                            start=(cc == 0), stop=(cc == DC - 1)),
                            inc=("p" if (half == 1 and cc == DC - 1) else None))
                need("v", tok_kp, tok_ksb_free)
                op("v", lambda e: e.scalar_tensor_tensor(self.kpre[:, :], self.kban[:, :], 0.0, self.bkb[:, :], ALU.add, ALU.add))
                tok_m0 = op("v", lambda e: e.tensor_scalar(self.m0_sb[:, :], self.kpre[:, :], 0.0, None, ALU.min, ALU.bypass), inc="v", fence=True)
                tok_kban_free = tok_m0
                need("s", tok_m0, tok_eA_free)
                tok_e = self.op("s", lambda e: e.activation(self.e_sbA[:, :], self.m0_sb[:, :], AF.Exp, bias=0.0, scale=1.0), inc="s")
                need("v", tok_e)
                tok_k = op("v", lambda e: e.scalar_tensor_tensor(self.k_sb[:, :], self.kpre[:, :], 0.0, self.e_sbA[:, :], ALU.max, ALU.add), inc="v", fence=True)
                tok_eA_free = tok_k
                need("v", tok_vp, tok_vsb_free)
                tok_v = None
                for half in range(2):
                    src = self.vban[:, half * 512:(half + 1) * 512].rearrange("p (h m) -> p h m", m=64)
                    bsl = self.bvb[:, half * 512:(half + 1) * 512].rearrange("p (h m) -> p h m", m=64)
                    dst = self.v_sb[:, half * 520:(half + 1) * 520].rearrange("p (h m) -> p h m", m=65)[:, :, 0:64]
                    tok_v = op("v", lambda e, s=src, b=bsl, d=dst: e.scalar_tensor_tensor(
                        d, s, 0.0, b, ALU.add, ALU.add), inc=("v" if half == 1 else None), fence=(half == 1))
                tok_vban_free = tok_v
                need("t", tok_k, tok_v)
                for h in range(H):
                    q = h // 2
                    bank = self.kvA if q < 4 else self.kvB
                    qq = q % 4
                    side = 0 if q < 4 else 1
                    first = not kv_started[side]
                    kv_started[side] = True
                    last = (t == NTA - 1 and sc == SCH - 1 and h == (7 if side == 0 else 15))
                    tok_kv_last = op("t", lambda e, h=h, qq=qq, bank=bank, first=first, last=last: e.matmul(
                        bank[(h % 2) * 64:(h % 2) * 64 + 64, qq * 65: qq * 65 + 65],
                        self.k_sb[:, h * DH:(h + 1) * DH],
                        self.v_sb[:, h * 65: h * 65 + 65],
                        start=first, stop=last), inc=("p" if h == H - 1 else None))
                if t == 3 and sc == 0:
                    tap_k = self.tap("ksb", self.k_sb[:, :], tok_k, [128, D], BF16)
                    tap_v = self.tap("vsb", self.v_sb[:, :], tok_v, [128, H * 65], BF16)
                elif t == 0 and sc == 0:
                    tap_k = self.tap("ksb0", self.k_sb[:, :], tok_k, [128, D], BF16)
                    tap_v = self.tap("vsb0", self.v_sb[:, :], tok_v, [128, H * 65], BF16)
                else:
                    tap_k = tap_v = None
                tok_ksb_free = [tok_kv_last, tap_k]
                tok_vsb_free = [tok_kv_last, tap_v]
            tok_ln1_free = [tok_kv_last, tap_ln1A]
            if t + 2 < NTA:
                need("g", tok_xa_free[par])
                self.tok_xa.append(self.load_xa(t + 2))
        return dict(kv=tok_kv_last, kban=tok_kban_free, vban=tok_vban_free,
                    stats=tok_stats_free, rows=tok_rows_free, ab_cons=tok_ln1)

    def exchange(self, st):
        op, need = self.op, self.need
        need("v", st["kv"])
        op("v", lambda e: e.tensor_copy(self.kv_acc[:, 0:260], self.kvA[:, 0:260]), inc="v")
        tok_evac = op("v", lambda e: e.tensor_copy(self.kv_acc[:, 260:520], self.kvB[:, 0:260]), inc="v", fence=True)

        need("g", tok_evac, *self.tok_excfg)
        sh = self.semh

        def gseq(e):
            with e.register("myslot") as myslot, e.register("peerslot") as peerslot:
                e.load(myslot, self.side_sb[0:1, 0:1])
                e.load(peerslot, self.side_sb[0:1, 1:2])
                with e.If_eq(myslot, 0):
                    e.dma_start(out=self.shpay.ap()[0, :, :], in_=self.kv_acc[:, :]).then_inc(sh["ge"], 16)
                with e.Else():
                    e.dma_start(out=self.shpay.ap()[1, :, :], in_=self.kv_acc[:, :]).then_inc(sh["ge"], 16)
                e.wait_ge(sh["ge"], 16)
                with e.If_eq(myslot, 0):
                    e.dma_start(out=self.shflag.ap()[0:1, 0:2], in_=self.nonce_sb[0:1, 0:2]).then_inc(sh["ge"], 16)
                with e.Else():
                    e.dma_start(out=self.shflag.ap()[1:2, 0:2], in_=self.nonce_sb[0:1, 0:2]).then_inc(sh["ge"], 16)
                e.wait_ge(sh["ge"], 32)
                with (
                    e.register("f0") as f0,
                    e.register("f1") as f1,
                    e.register("n0") as n0,
                    e.register("n1") as n1,
                    e.register("bad") as bad,
                ):
                    e.load(n0, self.nonce_sb[0:1, 0:1])
                    e.load(n1, self.nonce_sb[0:1, 1:2])
                    e.reg_mov(bad, 1)
                    with e.While(bad):
                        with e.If_eq(peerslot, 0):
                            e.load(f0, self.shflag.ap()[0:1, 0:1])
                            e.load(f1, self.shflag.ap()[0:1, 1:2])
                        with e.Else():
                            e.load(f0, self.shflag.ap()[1:2, 0:1])
                            e.load(f1, self.shflag.ap()[1:2, 1:2])
                        e.reg_sub(f0, f0, n0)
                        e.reg_sub(f1, f1, n1)
                        e.reg_alu(bad, f0, f1, ALU.bitwise_or)
                with e.If_eq(peerslot, 0):
                    e.dma_start(out=self.kv_peer[:, :], in_=self.shpay.ap()[0, :, :]).then_inc(sh["ge"], 16)
                with e.Else():
                    e.dma_start(out=self.kv_peer[:, :], in_=self.shpay.ap()[1, :, :]).then_inc(sh["ge"], 16)
                e.wait_ge(sh["ge"], 48)

        op("g", lambda e: e.dma_start(out=self.wqt_bf[:, :], in_=self.wqt, max_dma_last_dim=8192), inc="gq", n=16)
        op("g", lambda e: e.dma_start(out=self.wot_bf[:, :], in_=self.wot, max_dma_last_dim=8192), inc="gq", n=16)
        self.tok_wq = ("gq", self.cnt["gq"])
        self.tok_wo = ("gq", self.cnt["gq"])
        self.qs["g"].append(gseq)
        self.cnt["ge"] = 48
        tok_peer = ("ge", 48)

        tap_kvown = self.tap("kvown", self.kv_acc[:, :], tok_evac, [128, 520], F32)
        self.tap("arow", self.a_row[0:1, :], tok_evac, [1, NB], BF16)
        self.tap("brow", self.b_row[0:1, :], tok_evac, [1, NB], BF16)
        need("v", tok_peer, tap_kvown)
        op("v", lambda e: e.tensor_tensor(self.kv_acc[:, :], self.kv_acc[:, :], self.kv_peer[:, :], ALU.add), inc="v")
        self.tok_kvbf = op("v", lambda e: e.tensor_copy(self.kv_bf[:, :], self.kv_acc[:, :]), inc="v", fence=True)
        op("v", lambda e: e.memset(self.bd_bf[:, :], 0.0), inc="v")
        self.vfence()
        tok_bd = None
        for h in range(H):
            cc = h // 2
            r0 = (h % 2) * 64
            tok_bd = op("v", lambda e, h=h, cc=cc, r0=r0: e.tensor_copy(
                self.bd_bf[r0:r0 + 64, cc * 16 + h: cc * 16 + h + 1],
                self.kv_acc[r0:r0 + 64, cc * 65 + 64: cc * 65 + 65]), inc="v", fence=(h == H - 1))
        self.tap("kvbf", self.kv_bf[:, :], self.tok_kvbf, [128, 520], BF16)
        self.tap("bdbf", self.bd_bf[:, :], tok_bd, [128, 128], BF16)
        st["bd"] = tok_bd
        st["kv_evac"] = tok_evac
        return st

    def phase_b(self, st):
        op, need = self.op, self.need
        tok_sban_free = st["stats"]
        tok_rows_free = st["rows"]
        tok_ab_cons = st["ab_cons"]
        tok_kban_free = [st["kban"], st["kban"]]
        tok_vban_free = st["vban"]
        tok_attn_bank_free = st["kv_evac"]
        tok_qraw_free = None
        tok_qz_free = None
        tok_attn_sb_free = None
        tok_y_free = None
        tok_ln1B_free = None
        tok_x1 = None

        for t in range(NTB):
            need("t", tok_sban_free)
            op("t", lambda e, t=t: e.matmul(self.sban[:, :], self.ones_bf[0:1, 0:128],
                                            self.a_row[0:1, t * T:(t + 1) * T], start=True, stop=True), inc="p")
            pe_ab = op("t", lambda e, t=t: e.matmul(self.tban[:, :], self.ones_bf[0:1, 0:128],
                                                    self.b_row[0:1, t * T:(t + 1) * T], start=True, stop=True), inc="p")
            tok_ab = self.evac_ab(self.sban[:, :], self.tban[:, :], self.ab_sbB, pe_ab, tok_ab_cons)
            tok_sban_free = tok_ab
            need("g", tok_ab, tok_ln1B_free)
            tok_ln1 = None
            for cc in range(DC):
                op("g", lambda e, cc=cc, t=t: e.tensor_tensor(
                    self.ptmpB[:, :], self.xr[:, cc * NB + t * T: cc * NB + (t + 1) * T],
                    self.ab_sbB[:, 0:512], ALU.mult))
                tok_ln1 = op("g", lambda e, cc=cc: e.tensor_tensor(
                    self.ln1B[:, cc * T:(cc + 1) * T], self.ptmpB[:, :], self.ab_sbB[:, 512:1024], ALU.add),
                    inc=("g" if cc == DC - 1 else None), fence=(cc == DC - 1))
            tok_ab_cons = tok_ln1
            tap_ln1B = self.tap("ln1B", self.ln1B[:, :], tok_ln1, [128, DC * T], BF16) if t == 0 else None
            if tap_ln1B is not None:
                tok_ab_cons = [tok_ln1, tap_ln1B]

            tok_q = None
            tok_q_half = [None, None]
            for cc in range(DC):
                half = cc % 2
                need("t", self.tok_wq, tok_ln1, tok_kban_free[half], tok_qraw_free)
                tok_qp = None
                for dd in range(DC):
                    tok_qp = op("t", lambda e, cc=cc, dd=dd, half=half: e.matmul(
                        self.kban[:, half * 512:(half + 1) * 512],
                        self.wqt_bf[:, dd * D + cc * 128: dd * D + cc * 128 + 128],
                        self.ln1B[:, dd * T:(dd + 1) * T],
                        start=(dd == 0), stop=(dd == DC - 1)),
                        inc=("p" if dd == DC - 1 else None))
                need("s", tok_qp, tok_q_half[half])
                ps = self.kban[:, half * 512:(half + 1) * 512]
                self.op("s", lambda e, ps=ps, cc=cc: e.activation(
                    self.relu1[:, :], ps, AF.Relu, bias=self.nbqc[:, cc:cc + 1], scale=-1.0))
                self.op("s", lambda e, half=half: e.activation(
                    self.e_sbB[half][:, :], self.relu1[:, :], AF.Exp, bias=0.0, scale=-1.0))
                tok_p2 = self.op("s", lambda e, ps=ps, cc=cc, half=half: e.activation(
                    self.p2_sb[half][:, :], ps, AF.Relu, bias=self.bqc[:, cc:cc + 1], scale=1.0), inc="s")
                tok_kban_free[half] = tok_p2
                need("v", tok_p2)
                tok_q = op("v", lambda e, cc=cc, half=half: e.tensor_tensor(
                    self.qraw[:, cc * T:(cc + 1) * T], self.p2_sb[half][:, :], self.e_sbB[half][:, :], ALU.add), inc="v", fence=True)
                tok_q_half[half] = tok_q
            tap_qraw = self.tap("qraw", self.qraw[:, :], tok_q, [128, DC * T], BF16) if t == 0 else None
            tok_ln1B_free = [tok_q, tap_ln1B]

            need("t", st["bd"], tok_q, tok_vban_free)
            tok_den = None
            for cc in range(DC):
                tok_den = op("t", lambda e, cc=cc: e.matmul(
                    self.vban[0:16, 0:512], self.bd_bf[:, cc * 16:(cc + 1) * 16],
                    self.qraw[:, cc * T:(cc + 1) * T], start=(cc == 0), stop=(cc == DC - 1)),
                    inc=("p" if cc == DC - 1 else None))
            need("v", tok_den)
            op("v", lambda e: e.tensor_scalar(self.zden[0:16, :], self.vban[0:16, 0:512], EPS_Z, None, ALU.add, ALU.bypass))
            op("v", lambda e: e.reciprocal(self.zbuf[0:16, :], self.zden[0:16, :]), inc="v")
            self.vfence()
            tok_z = op("v", lambda e: e.tensor_copy(self.zbf[0:16, :], self.zbuf[0:16, :]), inc="v", fence=True)
            tap_z = self.tap("zbf", self.zbf[0:16, :], tok_z, [16, T], BF16) if t == 0 else None
            tok_zb_free = [tok_z, tap_z]

            tok_ae = None
            for cc in range(DC):
                need("t", tok_z, tok_zb_free, tok_qz_free)
                tok_zb = op("t", lambda e, cc=cc: e.matmul(
                    self.vban[:, 512:1024], self.sel_bf[0:16, cc * 128:(cc + 1) * 128],
                    self.zbf[0:16, :], start=True, stop=True), inc="p")
                need("v", tok_zb)
                tok_qz = op("v", lambda e, cc=cc: e.tensor_tensor(
                    self.qz[:, cc * T:(cc + 1) * T], self.qraw[:, cc * T:(cc + 1) * T],
                    self.vban[:, 512:1024], ALU.mult), inc="v", fence=True)
                tok_zb_free = tok_qz
                need("t", tok_qz, self.tok_kvbf, tok_attn_bank_free)
                op("t", lambda e, cc=cc: e.matmul(
                    self.kvA[0:64, 0:512], self.kv_bf[0:64, cc * 65: cc * 65 + 64],
                    self.qz[0:64, cc * T:(cc + 1) * T], start=True, stop=True))
                tok_at = op("t", lambda e, cc=cc: e.matmul(
                    self.kvB[0:64, 0:512], self.kv_bf[64:128, cc * 65: cc * 65 + 64],
                    self.qz[64:128, cc * T:(cc + 1) * T], start=True, stop=True), inc="p")
                need("s", tok_at, tok_attn_sb_free)
                self.op("s", lambda e, cc=cc: e.activation(
                    self.attn_sb[0:64, cc * T:(cc + 1) * T], self.kvA[0:64, 0:512], AF.Copy, bias=0.0, scale=1.0))
                tok_ae = self.op("s", lambda e, cc=cc: e.activation(
                    self.attn_sb[64:128, cc * T:(cc + 1) * T], self.kvB[0:64, 0:512], AF.Copy, bias=0.0, scale=1.0), inc="s", fence=True)
                tok_attn_bank_free = tok_ae
            tap_attn = self.tap("attnsb", self.attn_sb[:, :], tok_ae, [128, DC * T], BF16) if t == 0 else None
            tok_qz_free = tok_ae
            tok_qraw_free = [tok_ae, tap_qraw]

            tok_ye = None
            ysq_q = [None, None]
            tok_stats_last = None
            tok_woj = None
            for j in range(DC):
                bank = (self.sban, self.tban)[j % 2]
                need("t", self.tok_wo, tok_ae, tok_sban_free)
                for cc in range(DC):
                    tok_woj = op("t", lambda e, j=j, cc=cc, bank=bank: e.matmul(
                        bank[:, :], self.wot_bf[:, cc * D + j * 128: cc * D + j * 128 + 128],
                        self.attn_sb[:, cc * T:(cc + 1) * T],
                        start=(cc == 0), stop=(cc == DC - 1)),
                        inc=("p" if cc == DC - 1 else None))
                need("s", tok_woj, tok_y_free)
                tok_ye = self.op("s", lambda e, j=j, bank=bank: e.activation(
                    self.y_sb[:, j * T:(j + 1) * T], bank[:, :], AF.Identity, bias=self.boc[:, j:j + 1], scale=1.0), inc="s", fence=True)
                tok_sban_free = tok_ye
                need("t", tok_ye)
                op("t", lambda e, j=j: e.matmul(
                    self.vban[0:1, 0:512], self.ones_bf[:, 0:1], self.y_sb[:, j * T:(j + 1) * T],
                    start=(j == 0), stop=(j == DC - 1)), inc="p")
                yq = self.ysq[j % 2]
                need("g", tok_ye, ysq_q[j % 2])
                tok_yq = op("g", lambda e, j=j, yq=yq: e.tensor_tensor(
                    yq[:, :], self.y_sb[:, j * T:(j + 1) * T], self.y_sb[:, j * T:(j + 1) * T], ALU.mult), inc="g", fence=True)
                need("t", tok_yq)
                tok_stats_last = op("t", lambda e, j=j, yq=yq: e.matmul(
                    self.vban[0:1, 512:1024], self.ones_bf[:, 0:1], yq[:, :],
                    start=(j == 0), stop=(j == DC - 1)), inc="p")
                ysq_q[j % 2] = tok_stats_last
            rows_tok = self.ln_rows(self.vban[0:1, 0:512], self.vban[0:1, 512:1024], tok_stats_last, tok_rows_free)
            tok_rows_free = self.bcast_ab(self.vban[:, 0:512], self.vban[:, 512:1024], rows_tok, None)
            tok_ab2 = self.evac_ab(self.vban[:, 0:512], self.vban[:, 512:1024], self.ab_sbB, tok_rows_free, tok_ab_cons)
            tok_vban_free = tok_ab2
            need("v", tok_ab2, tok_ye)
            for cc in range(DC):
                op("v", lambda e, cc=cc: e.tensor_tensor(
                    self.t1_sbB[:, :], self.y_sb[:, cc * T:(cc + 1) * T], self.ab_sbB[:, 0:512], ALU.mult))
                op("v", lambda e: e.tensor_tensor(
                    self.t2_sbB[:, :], self.t1_sbB[:, :], self.ab_sbB[:, 512:1024], ALU.add))
                tok_x1 = op("v", lambda e, cc=cc, t=t: e.tensor_tensor(
                    self.xr[:, cc * NB + t * T: cc * NB + (t + 1) * T],
                    self.xr[:, cc * NB + t * T: cc * NB + (t + 1) * T], self.t2_sbB[:, :], ALU.add),
                    inc=("v" if cc == DC - 1 else None), fence=(cc == DC - 1))
            tap_y = self.tap("ysb", self.y_sb[:, :], tok_ye, [128, DC * T], BF16) if t == 0 else None
            tok_ab_cons = tok_x1
            tok_y_free = [tok_x1, tap_y]
            tok_attn_sb_free = [tok_woj, tap_attn]
        self.tap("xr", self.xr[:, :], tok_x1, [128, DC * NB], BF16)
        return dict(b_done=tok_x1, sban=tok_sban_free, vban=tok_vban_free,
                    kban=tok_kban_free, rows=tok_rows_free, ab_cons=tok_ab_cons,
                    ln3=st["kv_evac"], attn_bank=tok_attn_bank_free)

    def phase_c(self, st):
        op, need = self.op, self.need
        tok_bdone = st["b_done"]
        tok_sban_free = st["sban"]
        tok_rows_free = st["rows"]
        tok_ab_cons = st["ab_cons"]
        tok_kban_free = [st["kban"][0], st["kban"][1]]
        tok_vban_free = [st["vban"], st["vban"]]
        tok_ln3_bank_free = st["attn_bank"]
        tok_fsq_free = None
        tok_hid_w2_done = None
        tok_fout_cons = None
        tok_outs_free = [None, None]
        tok_sdcd_free = [None, None]
        tok_out_last = None
        fsq_q = [None, None]

        sh = self.semh

        def load_w1(fb, par):
            need("y", self.tok_w1bf[fb], tok_bdone)
            prior = self.cnt["yw"]
            if prior >= 16:
                self.qs["y"].append(lambda e, v=prior: e.wait_ge(sh["yw"], v))
            return self.op("y", lambda e, fb=fb, par=par: e.dma_start(
                out=self.w1b[par][:, :], in_=self.w1bf.ap()[fb, :, :]), inc="yw", n=16)

        def load_w2(j, par):
            need("y", self.tok_w2bf[j], tok_bdone)
            prior = self.cnt["yw2"]
            if prior >= 16:
                self.qs["y"].append(lambda e, v=prior: e.wait_ge(sh["yw2"], v))
            return self.op("y", lambda e, j=j, par=par: e.dma_start(
                out=self.w2b[par][:, :], in_=self.w2bf.ap()[j, :, :]), inc="yw2", n=16)

        tok_w1 = [load_w1(0, 0), load_w1(1, 1)]
        tok_w2 = [load_w2(0, 0), load_w2(1, 1)]

        def ln2_front(t, gate_extra=None):
            nonlocal tok_sban_free, tok_rows_free, tok_ab_cons, tok_fsq_free
            need("t", tok_sban_free, tok_bdone, gate_extra)
            for cc in range(DC):
                op("t", lambda e, cc=cc, t=t: e.matmul(
                    self.sban[0:1, :], self.ones_bf[:, 0:1],
                    self.xr[:, cc * NB + t * T: cc * NB + (t + 1) * T],
                    start=(cc == 0), stop=(cc == DC - 1)),
                    inc=("p" if cc == DC - 1 else None))
            tok_stat = None
            for cc in range(DC):
                fq = self.fsq[cc % 2]
                need("g", tok_bdone, fsq_q[cc % 2], tok_fsq_free)
                tok_fq = op("g", lambda e, cc=cc, t=t, fq=fq: e.tensor_tensor(
                    fq[:, :], self.xr[:, cc * NB + t * T: cc * NB + (t + 1) * T],
                    self.xr[:, cc * NB + t * T: cc * NB + (t + 1) * T], ALU.mult), inc="g", fence=True)
                need("t", tok_fq)
                tok_stat = op("t", lambda e, cc=cc, fq=fq: e.matmul(
                    self.tban[0:1, :], self.ones_bf[:, 0:1], fq[:, :],
                    start=(cc == 0), stop=(cc == DC - 1)),
                    inc=("p" if cc == DC - 1 else None))
                fsq_q[cc % 2] = tok_stat
            rows_tok = self.ln_rows(self.sban[0:1, 0:512], self.tban[0:1, 0:512], tok_stat, tok_rows_free)
            tok_rows_free = self.bcast_ab(self.sban[:, :], self.tban[:, :], rows_tok, None)
            tok_ab = self.evac_ab(self.sban[:, :], self.tban[:, :], self.ab_sbC, tok_rows_free, tok_ab_cons)
            tok_sban_free = tok_ab
            need("g", tok_ab)
            tok_ln2 = None
            for cc in range(DC):
                op("g", lambda e, cc=cc, t=t: e.tensor_tensor(
                    self.ptmpC[:, :], self.xr[:, cc * NB + t * T: cc * NB + (t + 1) * T],
                    self.ab_sbC[:, 0:512], ALU.mult))
                tok_ln2 = op("g", lambda e, cc=cc: e.tensor_tensor(
                    self.ln2C[:, cc * T:(cc + 1) * T], self.ptmpC[:, :], self.ab_sbC[:, 512:1024], ALU.add),
                    inc=("g" if cc == DC - 1 else None), fence=(cc == DC - 1))
            tok_ab_cons = tok_ln2
            return tok_ln2

        tok_ln2_tile = ln2_front(0)

        for t in range(NTB):
            tok_ln2 = tok_ln2_tile
            tok_hid = None
            for j in range(FC):
                fb = j // 4
                par = fb % 2
                half = j % 2
                if j % 4 == 0:
                    need("t", tok_w1[par])
                need("t", tok_ln2, tok_kban_free[half])
                ps = self.kban[:, half * 512:(half + 1) * 512]
                op("t", lambda e, j=j, ps=ps: e.matmul(
                    ps, self.b1r[0:1, j * 128:(j + 1) * 128], self.ones_row[0:1, :],
                    start=True, stop=False))
                tok_w1m = None
                for cc in range(DC):
                    tok_w1m = op("t", lambda e, j=j, cc=cc, half=half, par=par, ps=ps: e.matmul(
                        ps,
                        self.w1b[par][:, cc * 512 + (j % 4) * 128: cc * 512 + (j % 4) * 128 + 128],
                        self.ln2C[:, cc * T:(cc + 1) * T],
                        start=False, stop=(cc == DC - 1)),
                        inc=("p" if cc == DC - 1 else None))
                if j % 4 == 3:
                    nxt = fb + 2
                    if nxt < DC:
                        need("y", tok_w1m)
                        tok_w1[nxt % 2] = load_w1(nxt, nxt % 2)
                    elif t + 1 < NTB and nxt - DC < 2:
                        need("y", tok_w1m)
                        tok_w1[nxt % 2] = load_w1(nxt - DC, nxt % 2)
                sp = j % 2
                need("s", tok_w1m, tok_sdcd_free[sp])
                self.op("s", lambda e, ps=ps, sp=sp: e.activation(
                    self.sd_sb[sp][:, :], ps, AF.Sin, bias=0.0, scale=0.1))
                tok_h = self.op("s", lambda e, ps=ps, sp=sp: e.activation(
                    self.cd_sb[sp][:, :], ps, AF.Sin, bias=self.pihalf[:, 0:1], scale=0.1), inc="s", fence=True)
                need("v", tok_h, tok_hid_w2_done if j < 2 else None)
                tok_t12 = op("v", lambda e, j=j, sp=sp, ps=ps: e.scalar_tensor_tensor(
                    self.t12_sb[sp][:, :], self.sd_sb[sp][:, :], self.qB[:, j:j + 1], ps, ALU.mult, ALU.add), inc="v")
                tok_kban_free[half] = tok_t12
                tok_u = op("v", lambda e, j=j, sp=sp: e.scalar_tensor_tensor(
                    self.u_sb[sp][:, :], self.cd_sb[sp][:, :], self.qA[:, j:j + 1], self.t12_sb[sp][:, :], ALU.mult, ALU.add), inc="v", fence=True)
                tok_sdcd_free[sp] = tok_u
                need("g", tok_u)
                tok_hid = op("g", lambda e, j=j, sp=sp: e.tensor_scalar(
                    self.hidden[:, j * T:(j + 1) * T], self.u_sb[sp][:, :], 0.0, None, ALU.max, ALU.bypass),
                    inc="g", fence=True)

            if t + 1 < NTB:
                tok_ln2_tile = ln2_front(t + 1, gate_extra=tok_hid)

            tok_stats_last = None
            tok_fe = None
            tok_w2m = None
            for j in range(DC):
                par = j % 2
                need("t", tok_w2[par], tok_hid, tok_vban_free[par])
                for i in range(FC):
                    tok_w2m = op("t", lambda e, j=j, i=i, par=par: e.matmul(
                        self.vban[:, par * 512:(par + 1) * 512],
                        self.w2b[par][:, i * 128:(i + 1) * 128],
                        self.hidden[:, i * T:(i + 1) * T],
                        start=(i == 0), stop=(i == FC - 1)),
                        inc=("p" if i == FC - 1 else None))
                nxt = j + 2
                if nxt < DC:
                    need("y", tok_w2m)
                    tok_w2[nxt % 2] = load_w2(nxt, nxt % 2)
                elif t + 1 < NTB and nxt - DC < 2:
                    need("y", tok_w2m)
                    tok_w2[nxt % 2] = load_w2(nxt - DC, nxt % 2)
                need("s", tok_w2m, tok_fout_cons if j < 2 else None)
                tok_fe = self.op("s", lambda e, j=j, par=par: e.activation(
                    self.fout[:, j * T:(j + 1) * T], self.vban[:, par * 512:(par + 1) * 512],
                    AF.Identity, bias=self.b2c[:, j:j + 1], scale=1.0), inc="s", fence=True)
                tok_vban_free[par] = tok_fe
                need("t", tok_fe, tok_ln3_bank_free if j == 0 else None)
                op("t", lambda e, j=j: e.matmul(
                    self.kvA[0:1, :], self.ones_bf[:, 0:1], self.fout[:, j * T:(j + 1) * T],
                    start=(j == 0), stop=(j == DC - 1)),
                    inc=("p" if j == DC - 1 else None))
                fq = self.fsq[j % 2]
                need("g", tok_fe, fsq_q[j % 2])
                tok_fq = op("g", lambda e, j=j, fq=fq: e.tensor_tensor(
                    fq[:, :], self.fout[:, j * T:(j + 1) * T], self.fout[:, j * T:(j + 1) * T], ALU.mult), inc="g", fence=True)
                need("t", tok_fq)
                tok_stats_last = op("t", lambda e, j=j, fq=fq: e.matmul(
                    self.kvB[0:1, :], self.ones_bf[:, 0:1], fq[:, :],
                    start=(j == 0), stop=(j == DC - 1)),
                    inc=("p" if j == DC - 1 else None))
                fsq_q[j % 2] = tok_stats_last
            tok_hid_w2_done = tok_w2m
            rows_tok = self.ln_rows(self.kvA[0:1, 0:512], self.kvB[0:1, 0:512], tok_stats_last, tok_rows_free)
            tok_rows_free = self.bcast_ab(self.kvA[:, :], self.kvB[:, :], rows_tok, None)
            tok_ab3 = self.evac_ab(self.kvA[:, :], self.kvB[:, :], self.ab_sbC, tok_rows_free, tok_ab_cons)
            tok_ln3_bank_free = tok_ab3
            need("v", tok_ab3, tok_fe)
            tok_of = None
            for cc in range(DC):
                par = cc % 2
                need("v", tok_outs_free[par])
                op("v", lambda e, cc=cc: e.tensor_tensor(
                    self.t1_sbC[:, :], self.fout[:, cc * T:(cc + 1) * T], self.ab_sbC[:, 0:512], ALU.mult))
                op("v", lambda e: e.tensor_tensor(
                    self.t2f_sb[:, :], self.t1_sbC[:, :], self.ab_sbC[:, 512:1024], ALU.add))
                tok_of = op("v", lambda e, cc=cc, t=t, par=par: e.tensor_tensor(
                    self.outs[par][:, :], self.xr[:, cc * NB + t * T: cc * NB + (t + 1) * T],
                    self.t2f_sb[:, :], ALU.add), inc="v", fence=True)
                need("y", tok_of)
                prior = self.cnt["yo"]
                if prior >= 16:
                    self.qs["y"].append(lambda e, v=prior: e.wait_ge(sh["yo"], v))
                tok_out_last = self.op("y", lambda e, cc=cc, t=t, par=par: e.dma_start(
                    out=self.out_d[cc * 128:(cc + 1) * 128, t * T:(t + 1) * T], in_=self.outs[par][:, :]), inc="yo", n=16)
                tok_outs_free[par] = tok_out_last
            tok_fout_cons = tok_of
            tok_ab_cons = tok_of

        need("y", tok_out_last)


_BUILT = None


def _get_built():
    global _BUILT
    if _BUILT is None:
        kb = KB()
        nc = kb.build()
        _BUILT = (kb, nc)
    return _BUILT


def _prepT(W):
    return np.ascontiguousarray(W.T.reshape(DC, 128, D).transpose(1, 0, 2).reshape(128, DC * D))


def kernel(**inputs):
    x = np.asarray(inputs["L0_attn_hidden_states"], np.float32)
    Wq = np.asarray(inputs["L0_attn_Wq"], np.float32)
    bq = np.asarray(inputs["L0_attn_bq"], np.float32)
    Wk = np.asarray(inputs["L0_attn_Wk"], np.float32)
    bk = np.asarray(inputs["L0_attn_bk"], np.float32)
    Wv = np.asarray(inputs["L0_attn_Wv"], np.float32)
    bv = np.asarray(inputs["L0_attn_bv"], np.float32)
    Wo = np.asarray(inputs["L0_attn_Wo"], np.float32)
    bo = np.asarray(inputs["L0_attn_bo"], np.float32)
    W1 = np.asarray(inputs["L0_ffn_W1"], np.float32)
    b1 = np.asarray(inputs["L0_ffn_b1"], np.float32)
    th = np.asarray(inputs["L0_ffn_quantum_phases"], np.float32)
    wr = np.asarray(inputs["L0_ffn_quantum_weights_real"], np.float32)
    wi = np.asarray(inputs["L0_ffn_quantum_weights_imag"], np.float32)
    W2 = np.asarray(inputs["L0_ffn_W2"], np.float32)
    b2 = np.asarray(inputs["L0_ffn_b2"], np.float32)

    wqt, wkt, wvt, wot = _prepT(Wq), _prepT(Wk), _prepT(Wv), _prepT(Wo)
    w1p = np.ascontiguousarray(W1.reshape(DC, 128, 8, 512).transpose(2, 1, 0, 3).reshape(DC, 128, FF))
    w2p = np.ascontiguousarray(W2.reshape(FC, 128, DC, 128).transpose(2, 1, 0, 3).reshape(DC, 128, FF))

    cols = lambda vec, n: np.ascontiguousarray(vec.reshape(n, 128).T)
    sel = np.zeros((16, DC * 128), np.float32)
    for cc in range(DC):
        sel[2 * cc, cc * 128: cc * 128 + 64] = 1.0
        sel[2 * cc + 1, cc * 128 + 64: cc * 128 + 128] = 1.0
    nonce = np.random.randint(1, 2**31 - 1, size=(1, 2)).astype(np.int32)

    kb, nc = _get_built()
    in_maps = []
    for c in range(8):
        b = c // 4
        sA = (c % 2) * NA
        sB = sA + ((c % 4) // 2) * NB
        xT = np.ascontiguousarray(x[b].T)
        rest = sA + NB if sB == sA else sA
        xA = np.concatenate([xT[:, sB:sB + NB], xT[:, rest:rest + NB]], axis=1)
        in_maps.append(dict(
            xA=np.ascontiguousarray(xA),
            wqt=wqt, wkt=wkt, wvt=wvt, wot=wot, w1p=w1p, w2p=w2p,
            bkr=bk.reshape(1, D), bvr=bv.reshape(1, D),
            bqc_i=cols(bq, DC), boc_i=cols(bo, DC), b1c_i=cols(b1, FC), b2c_i=cols(b2, DC),
            b1r_i=b1.reshape(1, FF),
            th_i=cols(th, FC), wr_i=cols(wr, FC), wi_i=cols(wi, FC),
            ones_i=np.ones((128, 128), np.float32),
            sel_i=sel,
            nonce_i=nonce,
            side_i=np.array([[c & 1, 1 - (c & 1)]], np.int32),
        ))
    res = run_bass_kernel_spmd(nc, in_maps, core_ids=list(range(8)))
    out = np.empty((2, 16384, D), np.float32)
    for c in range(8):
        b = c // 4
        sB = (c % 2) * NA + ((c % 4) // 2) * NB
        out[b, sB:sB + NB, :] = res.results[c]["out"].T
    return out


# version 13
# speedup vs baseline: 5.3070x; 5.3070x over previous
import numpy as np
import concourse.bass as bass
import concourse.mybir as mybir
from concourse.bass_utils import run_bass_kernel_spmd
from contextlib import ExitStack

F32 = mybir.dt.float32
BF16 = mybir.dt.bfloat16
I32 = mybir.dt.int32
AF = mybir.ActivationFunctionType
ALU = mybir.AluOpType
PI = float(np.pi)

D, DC = 1024, 8
FF, FC = 4096, 32
H, DH = 16, 64
T = 512
NA, NB = 8192, 4096
NTA, NTB = NA // T, NB // T
SCH = T // 128
EPS_LN = 1e-5
EPS_Z = 1e-6

SB_BASE = 16512
SB_TOP = 229344


class Lay:
    def __init__(self, nc, base):
        self.nc = nc
        self.off = base
        self.end = base

    def al(self, name, shape, dtype):
        per = int(np.prod(shape[1:])) * mybir.dt.size(dtype)
        self.off = -(-self.off // 32) * 32
        h = self.nc.alloc_sbuf_tensor_at(name, list(shape), dtype, offset=self.off)
        self.off += per
        self.end = max(self.end, self.off)
        return h


class KB:
    ENGS = ("t", "v", "s", "g", "y")

    def __init__(self, debug=False):
        self.debug = debug
        self.nc = bass.Bass("TRN2", target_bir_lowering=False)
        self.qs = {e: [] for e in self.ENGS}
        self.cnt = {k: 0 for k in ("p", "v", "s", "g", "gs", "gw", "gc", "gq", "gx", "ge", "yw", "yw2", "yo", "od")}
        self.obs = {e: {} for e in self.ENGS}
        self.semh = {}

    def op(self, eng, fn, inc=None, n=1, fence=False):
        if fence != "never" and inc in ("v", "s", "g"):
            fence = True
        if fence is True and inc is not None:
            self.qs[eng].append(fn)
            if eng == "v":
                ffn = lambda e: e.tensor_copy(self.fen_v[0:1, 0:8], self.fen_v[0:1, 8:16])
            elif eng == "s":
                ffn = lambda e: e.activation(self.fen_s[0:1, 0:8], self.fen_s[0:1, 8:16], AF.Copy, bias=0.0, scale=1.0)
            else:
                ffn = lambda e: e.tensor_copy(self.fen_g[0:1, 0:8], self.fen_g[0:1, 8:16])
            return self.op(eng, ffn, inc=inc, n=n, fence="never")
        if inc is None:
            self.qs[eng].append(fn)
            return None
        self.cnt[inc] += n
        val = self.cnt[inc]
        sh = self.semh

        def wrapped(e, fn=fn, inc=inc, n=n):
            fn(e).then_inc(sh[inc], n)

        self.qs[eng].append(wrapped)
        if inc in ("p", "v", "s", "g"):
            ob = self.obs[eng]
            ob[inc] = max(ob.get(inc, 0), val)
        return (inc, val)

    def vfence(self):
        tok = self.op("v", lambda e: e.tensor_copy(self.fen_v[0:1, 0:8], self.fen_v[0:1, 8:16]), inc="v", fence="never")
        sh = self.semh
        self.qs["v"].append(lambda e, v=tok[1]: e.wait_ge(sh["v"], v))
        return tok

    def tap(self, name, src_ap, tok, shape, dtype):
        if not self.debug:
            return None
        d = self.nc.dram_tensor("dbg_" + name, list(shape), dtype, kind="ExternalOutput").ap()
        self.need("y", tok)
        return self.op("y", lambda e, d=d, s=src_ap: e.dma_start(out=d, in_=s), inc="od", n=16)

    def need(self, eng, *tokens):
        sh = self.semh
        for tok in tokens:
            if tok is None:
                continue
            if isinstance(tok, list):
                self.need(eng, *tok)
                continue
            sem, val = tok
            ob = self.obs[eng]
            if ob.get(sem, 0) >= val:
                continue
            ob[sem] = val
            self.qs[eng].append(lambda e, sem=sem, val=val: e.wait_ge(sh[sem], val))

    def alloc(self):
        nc = self.nc
        self.xA = nc.dram_tensor("xA", [D, NA], F32, kind="ExternalInput").ap()
        self.wqt = nc.dram_tensor("wqt", [128, DC * D], F32, kind="ExternalInput").ap()
        self.wkt = nc.dram_tensor("wkt", [128, DC * D], F32, kind="ExternalInput").ap()
        self.wvt = nc.dram_tensor("wvt", [128, DC * D], F32, kind="ExternalInput").ap()
        self.wot = nc.dram_tensor("wot", [128, DC * D], F32, kind="ExternalInput").ap()
        self.w1p = nc.dram_tensor("w1p", [DC, 128, FF], F32, kind="ExternalInput").ap()
        self.w2p = nc.dram_tensor("w2p", [DC, 128, FF], F32, kind="ExternalInput").ap()
        self.bkr = nc.dram_tensor("bkr", [1, D], F32, kind="ExternalInput").ap()
        self.bvr = nc.dram_tensor("bvr", [1, D], F32, kind="ExternalInput").ap()
        self.bqc_i = nc.dram_tensor("bqc_i", [128, DC], F32, kind="ExternalInput").ap()
        self.boc_i = nc.dram_tensor("boc_i", [128, DC], F32, kind="ExternalInput").ap()
        self.b1c_i = nc.dram_tensor("b1c_i", [128, FC], F32, kind="ExternalInput").ap()
        self.b2c_i = nc.dram_tensor("b2c_i", [128, DC], F32, kind="ExternalInput").ap()
        self.th_i = nc.dram_tensor("th_i", [128, FC], F32, kind="ExternalInput").ap()
        self.wr_i = nc.dram_tensor("wr_i", [128, FC], F32, kind="ExternalInput").ap()
        self.wi_i = nc.dram_tensor("wi_i", [128, FC], F32, kind="ExternalInput").ap()
        self.ones_i = nc.dram_tensor("ones_i", [128, 128], F32, kind="ExternalInput").ap()
        self.sel_i = nc.dram_tensor("sel_i", [16, DC * 128], F32, kind="ExternalInput").ap()
        self.nonce_i = nc.dram_tensor("nonce_i", [1, 2], I32, kind="ExternalInput").ap()
        self.side_i = nc.dram_tensor("side_i", [1, 2], I32, kind="ExternalInput").ap()
        self.out_d = nc.dram_tensor("out", [D, NB], F32, kind="ExternalOutput").ap()
        self.w1bf = nc.dram_tensor("w1bf", [DC, 128, FF], BF16)
        self.w2bf = nc.dram_tensor("w2bf", [DC, 128, FF], BF16)
        self.shpay = nc.dram_tensor("shpay", [2, 128, 520], F32, addr_space="Shared")
        self.shflag = nc.dram_tensor("shflag", [2, 16], I32, addr_space="Shared")

        st = Lay(nc, SB_BASE)
        self.xr = st.al("xr", [128, DC * NB], BF16)
        self.kv_acc = st.al("kv_acc", [128, 520], F32)
        self.kv_peer = st.al("kv_peer", [128, 520], F32)
        self.kv_bf = st.al("kv_bf", [128, 520], BF16)
        self.bd_bf = st.al("bd_bf", [128, 128], BF16)
        self.ones_bf = st.al("ones_bf", [128, 128], BF16)
        self.bqc = st.al("bqc", [128, DC], F32)
        self.nbqc = st.al("nbqc", [128, DC], F32)
        self.boc = st.al("boc", [128, DC], F32)
        self.b1c = st.al("b1c", [128, FC], F32)
        self.b1s1 = st.al("b1s1", [128, FC], F32)
        self.b1s2 = st.al("b1s2", [128, FC], F32)
        self.b2c = st.al("b2c", [128, DC], F32)
        self.qA = st.al("qA", [128, FC], F32)
        self.qB = st.al("qB", [128, FC], F32)
        self.sel_bf = st.al("sel_bf", [16, DC * 128], BF16)
        self.a_row = st.al("a_row", [1, NB], BF16)
        self.b_row = st.al("b_row", [1, NB], BF16)
        self.nonce_sb = st.al("nonce_sb", [1, 16], I32)
        self.side_sb = st.al("side_sb", [1, 2], I32)
        self.rowm = st.al("rowm", [1, 512], F32)
        self.rowv = st.al("rowv", [1, 512], F32)
        self.rowt = st.al("rowt", [1, 512], F32)
        self.rowa = st.al("rowa", [1, 512], BF16)
        self.rowb = st.al("rowb", [1, 512], BF16)
        self.qs1 = st.al("qs1", [128, FC], F32)
        self.qs2 = st.al("qs2", [128, FC], F32)
        self.qs3 = st.al("qs3", [128, FC], F32)
        self.th_c = st.al("th_c", [128, FC], F32)
        self.wr_c = st.al("wr_c", [128, FC], F32)
        self.wi_c = st.al("wi_c", [128, FC], F32)
        self.fen_v = st.al("fen_v", [1, 16], F32)
        self.fen_s = st.al("fen_s", [1, 16], F32)
        self.fen_g = st.al("fen_g", [1, 16], F32)
        rb = -(-st.off // 64) * 64

        A = Lay(nc, rb)
        self.wkt_bf = A.al("wkt_bf", [128, DC * D], BF16)
        self.wvt_bf = A.al("wvt_bf", [128, DC * D], BF16)
        self.xa0 = A.al("xa0", [128, DC * T], BF16)
        self.xa1 = A.al("xa1", [128, DC * T], BF16)
        self.ln1A = [A.al("ln1A0", [128, DC * T], BF16), A.al("ln1A1", [128, DC * T], BF16)]
        self.k_sb = [A.al("k_sb0", [128, D], BF16), A.al("k_sb1", [128, D], BF16)]
        self.v_sb = [A.al("v_sb0", [128, H * 65], BF16), A.al("v_sb1", [128, H * 65], BF16)]
        self.ab_sbA = A.al("ab_sbA", [128, 1024], F32)
        self.bkb = A.al("bkb", [128, D], F32)
        self.bvb = A.al("bvb", [128, D], F32)
        self.kpre = A.al("kpre", [128, D], F32)
        self.m0_sb = A.al("m0_sb", [128, D], F32)
        self.e_sbA = A.al("e_sbA", [128, D], F32)
        self.xsq = A.al("xsq", [128, DC * T], BF16)
        self.ptmpA = A.al("ptmpA", [128, T], BF16)
        self.bkr_bf = A.al("bkr_bf", [1, D], BF16)
        self.bvr_bf = A.al("bvr_bf", [1, D], BF16)

        B = Lay(nc, rb)
        self.wqt_bf = B.al("wqt_bf", [128, DC * D], BF16)
        self.wot_bf = B.al("wot_bf", [128, DC * D], BF16)
        self.ln1B = B.al("ln1B", [128, DC * T], BF16)
        self.qraw = B.al("qraw", [128, DC * T], BF16)
        self.qz = B.al("qz", [128, DC * T], BF16)
        self.attn_sb = B.al("attn_sb", [128, DC * T], BF16)
        self.y_sb = B.al("y_sb", [128, DC * T], BF16)
        self.ab_sbB = B.al("ab_sbB", [128, 1024], F32)
        self.relu1 = B.al("relu1", [128, T], F32)
        self.e_sbB = [B.al("e_sbB0", [128, T], F32), B.al("e_sbB1", [128, T], F32)]
        self.p2_sb = [B.al("p2_sb0", [128, T], F32), B.al("p2_sb1", [128, T], F32)]
        self.ysq = [B.al("ysq0", [128, T], BF16), B.al("ysq1", [128, T], BF16)]
        self.t1_sbB = B.al("t1_sbB", [128, T], BF16)
        self.t2_sbB = B.al("t2_sbB", [128, T], BF16)
        self.zden = B.al("zden", [16, T], F32)
        self.zbuf = B.al("zbuf", [16, T], F32)
        self.zbf = B.al("zbf", [16, T], BF16)
        self.ptmpB = B.al("ptmpB", [128, T], BF16)

        C = Lay(nc, rb)
        self.w1b = [C.al("w1b0", [128, FF], BF16), C.al("w1b1", [128, FF], BF16)]
        self.w2b = [C.al("w2b0", [128, FF], BF16), C.al("w2b1", [128, FF], BF16)]
        self.ln2C = C.al("ln2C", [128, DC * T], BF16)
        self.hidden = C.al("hidden", [128, FC * T], BF16)
        self.fout = C.al("fout", [128, DC * T], BF16)
        self.ab_sbC = C.al("ab_sbC", [128, 1024], F32)
        self.sd_sb = [C.al("sd_sb0", [128, T], BF16), C.al("sd_sb1", [128, T], BF16)]
        self.cd_sb = [C.al("cd_sb0", [128, T], BF16), C.al("cd_sb1", [128, T], BF16)]
        self.hbf_sb = [C.al("hbf0", [128, T], BF16), C.al("hbf1", [128, T], BF16)]
        self.t2_sbC = [C.al("t2_sbC0", [128, T], BF16), C.al("t2_sbC1", [128, T], BF16)]
        self.t12_sb = [C.al("t12_sb0", [128, T], BF16), C.al("t12_sb1", [128, T], BF16)]
        self.t1_sbC = C.al("t1_sbC", [128, T], BF16)
        self.t2f_sb = C.al("t2f_sb", [128, T], BF16)
        self.u_sb = [C.al("u_sb0", [128, T], F32), C.al("u_sb1", [128, T], F32)]
        self.fsq = [C.al("fsq0", [128, T], BF16), C.al("fsq1", [128, T], BF16)]
        self.outs = [C.al("outs0", [128, T], F32), C.al("outs1", [128, T], F32)]
        self.ptmpC = C.al("ptmpC", [128, T], BF16)

        assert max(A.end, B.end, C.end) <= SB_TOP, (st.off, A.end, B.end, C.end)

        self.kban = nc.alloc_psum_tensor("kban", [128, 1024], F32)
        self.vban = nc.alloc_psum_tensor("vban", [128, 1024], F32)
        self.sban = nc.alloc_psum_tensor("sban", [128, 512], F32)
        self.tban = nc.alloc_psum_tensor("tban", [128, 512], F32)
        self.kvA = nc.alloc_psum_tensor("kvA", [128, 512], F32)
        self.kvB = nc.alloc_psum_tensor("kvB", [128, 512], F32)

    def build(self):
        self.alloc()
        with ExitStack() as stack:
            E = stack.enter_context
            for name in ("p", "v", "s", "g", "gs", "gw", "gc", "gq", "gx", "ge", "yw", "yw2", "yo", "od"):
                self.semh[name] = E(self.nc.semaphore(name))
            block = E(self.nc.Block())
            self.emit()
            for eng, sect in (("t", block.tensor), ("v", block.vector), ("s", block.scalar),
                              ("g", block.gpsimd), ("y", block.sync)):
                def body(e, eng=eng):
                    for fn in self.qs[eng]:
                        fn(e)
                sect(body)
        return self.nc

    def emit(self):
        self.setup()
        st = self.phase_a()
        st = self.exchange(st)
        st = self.phase_b(st)
        self.phase_c(st)

    def setup(self):
        op, need = self.op, self.need
        toks = {}
        for nm, dst, src in (
            ("ones", self.ones_bf, self.ones_i), ("sel", self.sel_bf, self.sel_i),
            ("bqc", self.bqc, self.bqc_i), ("boc", self.boc, self.boc_i),
            ("b1c", self.b1c, self.b1c_i), ("b2c", self.b2c, self.b2c_i),
            ("th", self.th_c, self.th_i), ("wr", self.wr_c, self.wr_i),
            ("wi", self.wi_c, self.wi_i),
            ("bkr", self.bkr_bf, self.bkr), ("bvr", self.bvr_bf, self.bvr),
        ):
            toks[nm] = op("g", lambda e, d=dst, s=src: e.dma_start(out=d[:, :], in_=s), inc="gs", n=16)
        toks["nonce"] = op("g", lambda e: e.dma_start(out=self.nonce_sb[0:1, 0:2], in_=self.nonce_i), inc="gs", n=16)
        toks["side"] = op("g", lambda e: e.dma_start(out=self.side_sb[0:1, :], in_=self.side_i), inc="gs", n=16)
        tok_gs_all = ("gs", self.cnt["gs"])
        for nm in toks:
            toks[nm] = tok_gs_all
        self.tok_excfg = [tok_gs_all]

        self.xa = [self.xa0, self.xa1]
        self.tok_xa = [self.load_xa(0), self.load_xa(1)]

        self.tok_wk, self.tok_wv = [], []
        for w_bf, w_f, lst in ((self.wkt_bf, self.wkt, self.tok_wk), (self.wvt_bf, self.wvt, self.tok_wv)):
            for q4 in range(4):
                c0, c1 = q4 * (DC * D // 4), (q4 + 1) * (DC * D // 4)
                lst.append(op("g", lambda e, d=w_bf, s=w_f, c0=c0, c1=c1:
                              e.dma_start(out=d[:, c0:c1], in_=s[:, c0:c1], max_dma_last_dim=8192),
                              inc="gw", n=16))
        tok_gw_all = ("gw", self.cnt["gw"])
        self.tok_wk = [tok_gw_all]
        self.tok_wv = [tok_gw_all]
        self.big_casts = [("w1", j) for j in range(DC)] + [("w2", j) for j in range(DC)]
        self.tok_w1bf = [None] * DC
        self.tok_w2bf = [None] * DC

        op("v", lambda e: e.memset(self.v_sb[0][:, :], 1.0), inc="v")
        self.tok_vones = op("v", lambda e: e.memset(self.v_sb[1][:, :], 1.0), inc="v")

        need("v", toks["th"], toks["wr"], toks["wi"], toks["b1c"], toks["bqc"])
        v = lambda fn: op("v", fn, inc="v")
        v(lambda e: e.tensor_scalar(self.qs1[:, :], self.th_c[:, :], 1.0 / (2 * PI), None, ALU.mult, ALU.bypass))
        self.vfence()
        qs2_i = self.qs2.bitcast(I32)
        v(lambda e, q=qs2_i: e.tensor_copy(q[:, :], self.qs1[:, :]))
        self.vfence()
        v(lambda e, q=qs2_i: e.tensor_copy(self.qs1[:, :], q[:, :]))
        self.vfence()
        tok_thred = v(lambda e: e.scalar_tensor_tensor(self.qs1[:, :], self.qs1[:, :], -2 * PI, self.th_c[:, :], ALU.mult, ALU.add))
        need("s", tok_thred)
        self.op("s", lambda e: e.activation(self.qs2[:, :], self.qs1[:, :], AF.Sin, bias=0.0, scale=1.0), inc="s")
        tok_s2 = self.op("s", lambda e: e.activation(self.qs3[:, :], self.qs1[:, :], AF.Sin, bias=0.0, scale=0.5), inc="s")
        need("v", tok_s2)
        v(lambda e: e.tensor_tensor(self.qs1[:, :], self.qs3[:, :], self.qs3[:, :], ALU.mult))
        v(lambda e: e.tensor_scalar(self.qs1[:, :], self.qs1[:, :], -2.0, 1.0, ALU.mult, ALU.add))
        self.vfence()
        v(lambda e: e.tensor_tensor(self.qs3[:, :], self.qs1[:, :], self.wr_c[:, :], ALU.mult))
        v(lambda e: e.tensor_tensor(self.th_c[:, :], self.qs2[:, :], self.wi_c[:, :], ALU.mult))
        v(lambda e: e.tensor_tensor(self.qs3[:, :], self.qs3[:, :], self.th_c[:, :], ALU.add))
        v(lambda e: e.tensor_tensor(self.qs3[:, :], self.qs3[:, :], self.wi_c[:, :], ALU.mult))
        v(lambda e: e.tensor_scalar(self.qs3[:, :], self.qs3[:, :], 0.1, None, ALU.mult, ALU.bypass))
        self.vfence()
        v(lambda e: e.tensor_tensor(self.qA[:, :], self.qs3[:, :], self.qs2[:, :], ALU.mult))
        v(lambda e: e.tensor_tensor(self.qB[:, :], self.qs3[:, :], self.qs1[:, :], ALU.mult))
        v(lambda e: e.tensor_scalar(self.b1s1[:, :], self.b1c[:, :], 0.1, None, ALU.mult, ALU.bypass))
        self.vfence()
        v(lambda e: e.tensor_scalar(self.b1s2[:, :], self.b1s1[:, :], PI / 2, None, ALU.add, ALU.bypass))
        v(lambda e: e.tensor_scalar(self.nbqc[:, :], self.bqc[:, :], -1.0, None, ALU.mult, ALU.bypass))

        need("t", self.tok_vones, toks["ones"], toks["bkr"], toks["bvr"])
        op("t", lambda e: e.matmul(self.sban[:, :], self.ones_bf[0:1, 0:128], self.bkr_bf[0:1, 0:512], start=True, stop=True), inc="p")
        tk = op("t", lambda e: e.matmul(self.tban[:, :], self.ones_bf[0:1, 0:128], self.bkr_bf[0:1, 512:1024], start=True, stop=True), inc="p")
        need("s", tk)
        self.op("s", lambda e: e.activation(self.bkb[:, 0:512], self.sban[:, :], AF.Copy, bias=0.0, scale=1.0), inc="s")
        tok_kb = self.op("s", lambda e: e.activation(self.bkb[:, 512:1024], self.tban[:, :], AF.Copy, bias=0.0, scale=1.0), inc="s")
        need("t", tok_kb)
        op("t", lambda e: e.matmul(self.sban[:, :], self.ones_bf[0:1, 0:128], self.bvr_bf[0:1, 0:512], start=True, stop=True), inc="p")
        tv = op("t", lambda e: e.matmul(self.tban[:, :], self.ones_bf[0:1, 0:128], self.bvr_bf[0:1, 512:1024], start=True, stop=True), inc="p")
        need("s", tv)
        self.op("s", lambda e: e.activation(self.bvb[:, 0:512], self.sban[:, :], AF.Copy, bias=0.0, scale=1.0), inc="s")
        self.tok_setup_banks = self.op("s", lambda e: e.activation(self.bvb[:, 512:1024], self.tban[:, :], AF.Copy, bias=0.0, scale=1.0), inc="s")

    def load_xa(self, t):
        par = t % 2
        src = self.xA[:, t * T:(t + 1) * T].rearrange("(cc p) s -> p cc s", p=128)
        dst = self.xa[par][:, :].rearrange("p (cc s) -> p cc s", s=T)
        sh = self.semh
        if t >= 2:
            prior = self.cnt["gx"]
            self.qs["g"].append(lambda e, v=prior: e.wait_ge(sh["gx"], v))
        return self.op("g", lambda e, d=dst, s=src: e.dma_start(out=d, in_=s, max_dma_last_dim=8192), inc="gx", n=16)

    def emit_big_cast(self):
        if not self.big_casts:
            return
        which, j = self.big_casts.pop(0)
        w_bf, w_f = (self.w1bf, self.w1p) if which == "w1" else (self.w2bf, self.w2p)
        self.op("g", lambda e, d=w_bf, s=w_f, j=j:
                e.dma_start(out=d.ap()[j, :, :], in_=s[j, :, :], max_dma_last_dim=8192), inc="gc", n=16)
        if not self.big_casts:
            tok = ("gc", self.cnt["gc"])
            self.tok_w1bf = [tok] * DC
            self.tok_w2bf = [tok] * DC

    def ln_rows(self, sum_ap, sq_ap, stat_tok, rows_free_tok):
        op, need = self.op, self.need
        need("v", stat_tok, rows_free_tok)
        op("v", lambda e: e.tensor_scalar(self.rowm[0:1, :], sum_ap, 1.0 / D, None, ALU.mult, ALU.bypass))
        op("v", lambda e: e.tensor_scalar(self.rowv[0:1, :], sq_ap, 1.0 / D, None, ALU.mult, ALU.bypass))
        op("v", lambda e: e.tensor_tensor(self.rowt[0:1, :], self.rowm[0:1, :], self.rowm[0:1, :], ALU.mult))
        op("v", lambda e: e.tensor_tensor(self.rowv[0:1, :], self.rowv[0:1, :], self.rowt[0:1, :], ALU.subtract))
        tok_v = op("v", lambda e: e.tensor_scalar(self.rowv[0:1, :], self.rowv[0:1, :], EPS_LN, None, ALU.add, ALU.bypass), inc="v", fence=True)
        need("s", tok_v)
        op("s", lambda e: e.activation(self.rowt[0:1, :], self.rowv[0:1, :], AF.Ln, bias=0.0, scale=1.0))
        tok_a = op("s", lambda e: e.activation(self.rowa[0:1, :], self.rowt[0:1, :], AF.Exp, bias=0.0, scale=-0.5), inc="s", fence=True)
        need("v", tok_a)
        tok_b = op("v", lambda e: e.scalar_tensor_tensor(self.rowb[0:1, :], self.rowm[0:1, :], -1.0, self.rowa[0:1, :], ALU.mult, ALU.mult), inc="v", fence=True)
        return tok_b

    def bcast_ab(self, a_ps, b_ps, rows_tok, bank_free_tok):
        op, need = self.op, self.need
        need("t", rows_tok, bank_free_tok)
        op("t", lambda e: e.matmul(a_ps, self.ones_bf[0:1, 0:128], self.rowa[0:1, :], start=True, stop=True))
        return op("t", lambda e: e.matmul(b_ps, self.ones_bf[0:1, 0:128], self.rowb[0:1, :], start=True, stop=True), inc="p")

    def evac_ab(self, a_ps, b_ps, ab_sb, pe_tok, consumer_tok):
        op, need = self.op, self.need
        need("s", pe_tok, consumer_tok)
        op("s", lambda e: e.activation(ab_sb[:, 0:512], a_ps, AF.Copy, bias=0.0, scale=1.0))
        return op("s", lambda e: e.activation(ab_sb[:, 512:1024], b_ps, AF.Copy, bias=0.0, scale=1.0), inc="s", fence=True)

    def phase_a(self):
        op, need = self.op, self.need
        S = dict(stats_free=self.tok_setup_banks, ab_cons=None, rows_free=None,
                 kban_free=None, vban_free=None, eA_free=None, kv_last=None,
                 ksb_free=[None, None], vsb_free=[self.tok_vones, self.tok_vones])
        statq = [None]
        rows_pending = {}
        ln1_done = [None, None]
        ln1_free = [None, None]
        xa_free = [None, None]
        kv_started = [False, False]
        kv_pending = []

        def front_sums(t):
            par = t % 2
            xa = self.xa[par]
            need("t", self.tok_xa[t], S["stats_free"])
            for cc in range(DC):
                op("t", lambda e, cc=cc, xa=xa: e.matmul(
                    self.sban[0:1, :], self.ones_bf[:, 0:1], xa[:, cc * T:(cc + 1) * T],
                    start=(cc == 0), stop=(cc == DC - 1)),
                    inc=("p" if cc == DC - 1 else None))

        def front_sq(t):
            par = t % 2
            xa = self.xa[par]
            need("s", self.tok_xa[t], statq[0])
            tok_sq = None
            for cc in range(DC):
                tok_sq = self.op("s", lambda e, cc=cc, xa=xa: e.activation(
                    self.xsq[:, cc * T:(cc + 1) * T], xa[:, cc * T:(cc + 1) * T],
                    AF.Square, bias=0.0, scale=1.0),
                    inc=("s" if cc == DC - 1 else None), fence=(cc == DC - 1))
            xa_free[par] = [tok_sq]
            return tok_sq

        def front_tban(t, tok_sq):
            need("t", tok_sq)
            tok_stat = None
            for cc in range(DC):
                tok_stat = op("t", lambda e, cc=cc: e.matmul(
                    self.tban[0:1, :], self.ones_bf[:, 0:1], self.xsq[:, cc * T:(cc + 1) * T],
                    start=(cc == 0), stop=(cc == DC - 1)),
                    inc=("p" if cc == DC - 1 else None))
            statq[0] = tok_stat
            return tok_stat

        def front_rows(t, tok_stat):
            par = t % 2
            xa = self.xa[par]
            tok_x = self.tok_xa[t]
            rows_tok = self.ln_rows(self.sban[0:1, 0:512], self.tban[0:1, 0:512], tok_stat, S["rows_free"])
            tok_xr_last = None
            if t < NTB:
                op("v", lambda e, t=t: e.tensor_copy(self.a_row[0:1, t * T:(t + 1) * T], self.rowa[0:1, :]))
                rows_tok = op("v", lambda e, t=t: e.tensor_copy(self.b_row[0:1, t * T:(t + 1) * T], self.rowb[0:1, :]), inc="v", fence=True)
                need("v", tok_x)
                for cc in range(DC):
                    tok_xr_last = op("v", lambda e, t=t, cc=cc, xa=xa: e.tensor_copy(
                        self.xr[:, cc * NB + t * T: cc * NB + (t + 1) * T], xa[:, cc * T:(cc + 1) * T]),
                        inc=("v" if cc == DC - 1 else None), fence=(cc == DC - 1))
                xa_free[par].append(tok_xr_last)
            rows_pending[t] = rows_tok

        def front_bcast(t):
            par = t % 2
            xa = self.xa[par]
            ln1A = self.ln1A[par]
            tok_rf = self.bcast_ab(self.sban[:, :], self.tban[:, :], rows_pending.pop(t), None)
            S["rows_free"] = tok_rf
            tok_ab = self.evac_ab(self.sban[:, :], self.tban[:, :], self.ab_sbA, tok_rf, S["ab_cons"])
            S["stats_free"] = tok_ab
            need("g", tok_ab, ln1_free[par])
            tok_ln1 = None
            for cc in range(DC):
                op("g", lambda e, cc=cc, xa=xa: e.tensor_tensor(
                    self.ptmpA[:, :], xa[:, cc * T:(cc + 1) * T], self.ab_sbA[:, 0:512], ALU.mult), inc="g")
                tok_ln1 = op("g", lambda e, cc=cc, ln1A=ln1A: e.tensor_tensor(
                    ln1A[:, cc * T:(cc + 1) * T], self.ptmpA[:, :], self.ab_sbA[:, 512:1024], ALU.add), inc="g", fence=(cc == DC - 1))
            S["ab_cons"] = tok_ln1
            ln1_done[par] = tok_ln1
            xa_free[par].append(tok_ln1)

        def chunk_proj(t, sc):
            par = t % 2
            kpar = sc % 2
            ln1A = self.ln1A[par]
            need("t", ln1_done[par], S["kban_free"], self.tok_wk[-1])
            tok_kp = None
            for half in range(2):
                for cc in range(DC):
                    tok_kp = op("t", lambda e, sc=sc, cc=cc, half=half, ln1A=ln1A: e.matmul(
                        self.kban[:, half * 512:(half + 1) * 512],
                        ln1A[:, cc * T + sc * 128: cc * T + (sc + 1) * 128],
                        self.wkt_bf[:, cc * D + half * 512: cc * D + (half + 1) * 512],
                        start=(cc == 0), stop=(cc == DC - 1)),
                        inc=("p" if (half == 1 and cc == DC - 1) else None))
            need("t", S["vban_free"], self.tok_wv[-1])
            tok_vp = None
            for half in range(2):
                for cc in range(DC):
                    tok_vp = op("t", lambda e, sc=sc, cc=cc, half=half, ln1A=ln1A: e.matmul(
                        self.vban[:, half * 512:(half + 1) * 512],
                        ln1A[:, cc * T + sc * 128: cc * T + (sc + 1) * 128],
                        self.wvt_bf[:, cc * D + half * 512: cc * D + (half + 1) * 512],
                        start=(cc == 0), stop=(cc == DC - 1)),
                        inc=("p" if (half == 1 and cc == DC - 1) else None))
            need("v", tok_kp, S["ksb_free"][kpar])
            tok_kpre = op("v", lambda e: e.scalar_tensor_tensor(
                self.kpre[:, :], self.kban[:, :], 0.0, self.bkb[:, :], ALU.add, ALU.add), inc="v", fence=True)
            S["kban_free"] = tok_kpre
            tok_m0 = op("v", lambda e: e.tensor_scalar(
                self.m0_sb[:, :], self.kpre[:, :], 0.0, None, ALU.min, ALU.bypass), inc="v", fence=True)
            need("s", tok_m0, S["eA_free"])
            tok_e = self.op("s", lambda e: e.activation(self.e_sbA[:, :], self.m0_sb[:, :], AF.Exp, bias=0.0, scale=1.0), inc="s")
            need("v", tok_vp, S["vsb_free"][kpar])
            tok_v = None
            for half in range(2):
                src = self.vban[:, half * 512:(half + 1) * 512].rearrange("p (h m) -> p h m", m=64)
                bsl = self.bvb[:, half * 512:(half + 1) * 512].rearrange("p (h m) -> p h m", m=64)
                dst = self.v_sb[kpar][:, half * 520:(half + 1) * 520].rearrange("p (h m) -> p h m", m=65)[:, :, 0:64]
                tok_v = op("v", lambda e, s=src, b=bsl, d=dst: e.scalar_tensor_tensor(
                    d, s, 0.0, b, ALU.add, ALU.add), inc=("v" if half == 1 else None), fence=(half == 1))
            S["vban_free"] = tok_v
            need("v", tok_e)
            tok_k = op("v", lambda e, kpar=kpar: e.scalar_tensor_tensor(
                self.k_sb[kpar][:, :], self.kpre[:, :], 0.0, self.e_sbA[:, :], ALU.max, ALU.add), inc="v", fence=True)
            S["eA_free"] = tok_k
            if t == 3 and sc == 0:
                tap_k = self.tap("ksb", self.k_sb[kpar][:, :], tok_k, [128, D], BF16)
                tap_v = self.tap("vsb", self.v_sb[kpar][:, :], tok_v, [128, H * 65], BF16)
            elif t == 0 and sc == 0:
                tap_k = self.tap("ksb0", self.k_sb[kpar][:, :], tok_k, [128, D], BF16)
                tap_v = self.tap("vsb0", self.v_sb[kpar][:, :], tok_v, [128, H * 65], BF16)
            else:
                tap_k = tap_v = None
            kv_pending.append((t, sc, tok_k, tok_v, tap_k, tap_v))
            if sc == SCH - 1:
                ln1_free[par] = [tok_vp, tap_ln1A if t == 0 else None]

        def kv_emit():
            t, sc, tok_k, tok_v, tap_k, tap_v = kv_pending.pop(0)
            kpar = sc % 2
            need("t", tok_k, tok_v)
            for h in range(H):
                q = h // 2
                bank = self.kvA if q < 4 else self.kvB
                qq = q % 4
                side = 0 if q < 4 else 1
                first = not kv_started[side]
                kv_started[side] = True
                last = (t == NTA - 1 and sc == SCH - 1 and h == (7 if side == 0 else 15))
                S["kv_last"] = op("t", lambda e, h=h, qq=qq, bank=bank, first=first, last=last, kpar=kpar: e.matmul(
                    bank[(h % 2) * 64:(h % 2) * 64 + 64, qq * 65: qq * 65 + 65],
                    self.k_sb[kpar][:, h * DH:(h + 1) * DH],
                    self.v_sb[kpar][:, h * 65: h * 65 + 65],
                    start=first, stop=last), inc=("p" if h == H - 1 else None))
            S["ksb_free"][kpar] = [S["kv_last"], tap_k]
            S["vsb_free"][kpar] = [S["kv_last"], tap_v]

        front_sums(0)
        tok_sq0 = front_sq(0)
        tok_stat0 = front_tban(0, tok_sq0)
        front_rows(0, tok_stat0)
        front_bcast(0)
        tap_ln1A = self.tap("ln1A", self.ln1A[0][:, :], ln1_done[0], [128, DC * T], BF16)

        for t in range(NTA):
            par = t % 2
            self.emit_big_cast()
            if t >= NTA - 4:
                while self.big_casts:
                    self.emit_big_cast()
            if t + 2 < NTA:
                need("g", xa_free[par])
                self.tok_xa.append(self.load_xa(t + 2))
            nxt = t + 1 < NTA
            if nxt:
                front_sums(t + 1)
                tok_sq = front_sq(t + 1)
            chunk_proj(t, 0)
            while len(kv_pending) > 1:
                kv_emit()
            if nxt:
                tok_stat = front_tban(t + 1, tok_sq)
                front_rows(t + 1, tok_stat)
            chunk_proj(t, 1)
            kv_emit()
            if nxt:
                front_bcast(t + 1)
            chunk_proj(t, 2)
            kv_emit()
            chunk_proj(t, 3)
            kv_emit()
        kv_emit()
        return dict(kv=S["kv_last"], kban=S["kban_free"], vban=S["vban_free"],
                    stats=S["stats_free"], rows=S["rows_free"], ab_cons=S["ab_cons"])

    def exchange(self, st):
        op, need = self.op, self.need
        need("v", st["kv"])
        op("v", lambda e: e.tensor_copy(self.kv_acc[:, 0:260], self.kvA[:, 0:260]), inc="v")
        tok_evac = op("v", lambda e: e.tensor_copy(self.kv_acc[:, 260:520], self.kvB[:, 0:260]), inc="v", fence=True)

        need("g", tok_evac, *self.tok_excfg)
        sh = self.semh

        def gseq(e):
            with e.register("myslot") as myslot, e.register("peerslot") as peerslot:
                e.load(myslot, self.side_sb[0:1, 0:1])
                e.load(peerslot, self.side_sb[0:1, 1:2])
                with e.If_eq(myslot, 0):
                    e.dma_start(out=self.shpay.ap()[0, :, :], in_=self.kv_acc[:, :]).then_inc(sh["ge"], 16)
                with e.Else():
                    e.dma_start(out=self.shpay.ap()[1, :, :], in_=self.kv_acc[:, :]).then_inc(sh["ge"], 16)
                e.wait_ge(sh["ge"], 16)
                with e.If_eq(myslot, 0):
                    e.dma_start(out=self.shflag.ap()[0:1, 0:2], in_=self.nonce_sb[0:1, 0:2]).then_inc(sh["ge"], 16)
                with e.Else():
                    e.dma_start(out=self.shflag.ap()[1:2, 0:2], in_=self.nonce_sb[0:1, 0:2]).then_inc(sh["ge"], 16)
                e.wait_ge(sh["ge"], 32)
                with (
                    e.register("f0") as f0,
                    e.register("f1") as f1,
                    e.register("n0") as n0,
                    e.register("n1") as n1,
                    e.register("bad") as bad,
                ):
                    e.load(n0, self.nonce_sb[0:1, 0:1])
                    e.load(n1, self.nonce_sb[0:1, 1:2])
                    e.reg_mov(bad, 1)
                    with e.While(bad):
                        with e.If_eq(peerslot, 0):
                            e.load(f0, self.shflag.ap()[0:1, 0:1])
                            e.load(f1, self.shflag.ap()[0:1, 1:2])
                        with e.Else():
                            e.load(f0, self.shflag.ap()[1:2, 0:1])
                            e.load(f1, self.shflag.ap()[1:2, 1:2])
                        e.reg_sub(f0, f0, n0)
                        e.reg_sub(f1, f1, n1)
                        e.reg_alu(bad, f0, f1, ALU.bitwise_or)
                with e.If_eq(peerslot, 0):
                    e.dma_start(out=self.kv_peer[:, :], in_=self.shpay.ap()[0, :, :]).then_inc(sh["ge"], 16)
                with e.Else():
                    e.dma_start(out=self.kv_peer[:, :], in_=self.shpay.ap()[1, :, :]).then_inc(sh["ge"], 16)
                e.wait_ge(sh["ge"], 48)

        op("g", lambda e: e.dma_start(out=self.wqt_bf[:, :], in_=self.wqt, max_dma_last_dim=8192), inc="gq", n=16)
        op("g", lambda e: e.dma_start(out=self.wot_bf[:, :], in_=self.wot, max_dma_last_dim=8192), inc="gq", n=16)
        self.tok_wq = ("gq", self.cnt["gq"])
        self.tok_wo = ("gq", self.cnt["gq"])
        self.qs["g"].append(gseq)
        self.cnt["ge"] = 48
        tok_peer = ("ge", 48)

        tap_kvown = self.tap("kvown", self.kv_acc[:, :], tok_evac, [128, 520], F32)
        self.tap("arow", self.a_row[0:1, :], tok_evac, [1, NB], BF16)
        self.tap("brow", self.b_row[0:1, :], tok_evac, [1, NB], BF16)
        need("v", tok_peer, tap_kvown)
        op("v", lambda e: e.tensor_tensor(self.kv_acc[:, :], self.kv_acc[:, :], self.kv_peer[:, :], ALU.add), inc="v")
        self.tok_kvbf = op("v", lambda e: e.tensor_copy(self.kv_bf[:, :], self.kv_acc[:, :]), inc="v", fence=True)
        op("v", lambda e: e.memset(self.bd_bf[:, :], 0.0), inc="v")
        self.vfence()
        tok_bd = None
        for h in range(H):
            cc = h // 2
            r0 = (h % 2) * 64
            tok_bd = op("v", lambda e, h=h, cc=cc, r0=r0: e.tensor_copy(
                self.bd_bf[r0:r0 + 64, cc * 16 + h: cc * 16 + h + 1],
                self.kv_acc[r0:r0 + 64, cc * 65 + 64: cc * 65 + 65]), inc="v", fence=(h == H - 1))
        self.tap("kvbf", self.kv_bf[:, :], self.tok_kvbf, [128, 520], BF16)
        self.tap("bdbf", self.bd_bf[:, :], tok_bd, [128, 128], BF16)
        st["bd"] = tok_bd
        st["kv_evac"] = tok_evac
        return st

    def phase_b(self, st):
        op, need = self.op, self.need
        tok_sban_free = st["stats"]
        tok_rows_free = st["rows"]
        tok_ab_cons = st["ab_cons"]
        tok_kban_free = [st["kban"], st["kban"]]
        tok_vban_free = st["vban"]
        tok_attn_bank_free = st["kv_evac"]
        tok_qraw_free = None
        tok_qz_free = None
        tok_attn_sb_free = None
        tok_y_free = None
        tok_ln1B_free = None
        tok_x1 = None

        for t in range(NTB):
            need("t", tok_sban_free)
            op("t", lambda e, t=t: e.matmul(self.sban[:, :], self.ones_bf[0:1, 0:128],
                                            self.a_row[0:1, t * T:(t + 1) * T], start=True, stop=True), inc="p")
            pe_ab = op("t", lambda e, t=t: e.matmul(self.tban[:, :], self.ones_bf[0:1, 0:128],
                                                    self.b_row[0:1, t * T:(t + 1) * T], start=True, stop=True), inc="p")
            tok_ab = self.evac_ab(self.sban[:, :], self.tban[:, :], self.ab_sbB, pe_ab, tok_ab_cons)
            tok_sban_free = tok_ab
            need("g", tok_ab, tok_ln1B_free)
            tok_ln1 = None
            for cc in range(DC):
                op("g", lambda e, cc=cc, t=t: e.tensor_tensor(
                    self.ptmpB[:, :], self.xr[:, cc * NB + t * T: cc * NB + (t + 1) * T],
                    self.ab_sbB[:, 0:512], ALU.mult))
                tok_ln1 = op("g", lambda e, cc=cc: e.tensor_tensor(
                    self.ln1B[:, cc * T:(cc + 1) * T], self.ptmpB[:, :], self.ab_sbB[:, 512:1024], ALU.add),
                    inc=("g" if cc == DC - 1 else None), fence=(cc == DC - 1))
            tok_ab_cons = tok_ln1
            tap_ln1B = self.tap("ln1B", self.ln1B[:, :], tok_ln1, [128, DC * T], BF16) if t == 0 else None
            if tap_ln1B is not None:
                tok_ab_cons = [tok_ln1, tap_ln1B]

            tok_q = None
            tok_q_half = [None, None]
            for cc in range(DC):
                half = cc % 2
                need("t", self.tok_wq, tok_ln1, tok_kban_free[half], tok_qraw_free)
                tok_qp = None
                for dd in range(DC):
                    tok_qp = op("t", lambda e, cc=cc, dd=dd, half=half: e.matmul(
                        self.kban[:, half * 512:(half + 1) * 512],
                        self.wqt_bf[:, dd * D + cc * 128: dd * D + cc * 128 + 128],
                        self.ln1B[:, dd * T:(dd + 1) * T],
                        start=(dd == 0), stop=(dd == DC - 1)),
                        inc=("p" if dd == DC - 1 else None))
                need("s", tok_qp, tok_q_half[half])
                ps = self.kban[:, half * 512:(half + 1) * 512]
                self.op("s", lambda e, ps=ps, cc=cc: e.activation(
                    self.relu1[:, :], ps, AF.Relu, bias=self.nbqc[:, cc:cc + 1], scale=-1.0))
                self.op("s", lambda e, half=half: e.activation(
                    self.e_sbB[half][:, :], self.relu1[:, :], AF.Exp, bias=0.0, scale=-1.0))
                tok_p2 = self.op("s", lambda e, ps=ps, cc=cc, half=half: e.activation(
                    self.p2_sb[half][:, :], ps, AF.Relu, bias=self.bqc[:, cc:cc + 1], scale=1.0), inc="s")
                tok_kban_free[half] = tok_p2
                need("v", tok_p2)
                tok_q = op("v", lambda e, cc=cc, half=half: e.tensor_tensor(
                    self.qraw[:, cc * T:(cc + 1) * T], self.p2_sb[half][:, :], self.e_sbB[half][:, :], ALU.add), inc="v", fence=True)
                tok_q_half[half] = tok_q
            tap_qraw = self.tap("qraw", self.qraw[:, :], tok_q, [128, DC * T], BF16) if t == 0 else None
            tok_ln1B_free = [tok_q, tap_ln1B]

            need("t", st["bd"], tok_q, tok_vban_free)
            tok_den = None
            for cc in range(DC):
                tok_den = op("t", lambda e, cc=cc: e.matmul(
                    self.vban[0:16, 0:512], self.bd_bf[:, cc * 16:(cc + 1) * 16],
                    self.qraw[:, cc * T:(cc + 1) * T], start=(cc == 0), stop=(cc == DC - 1)),
                    inc=("p" if cc == DC - 1 else None))
            need("v", tok_den)
            op("v", lambda e: e.tensor_scalar(self.zden[0:16, :], self.vban[0:16, 0:512], EPS_Z, None, ALU.add, ALU.bypass))
            op("v", lambda e: e.reciprocal(self.zbuf[0:16, :], self.zden[0:16, :]), inc="v")
            self.vfence()
            tok_z = op("v", lambda e: e.tensor_copy(self.zbf[0:16, :], self.zbuf[0:16, :]), inc="v", fence=True)
            tap_z = self.tap("zbf", self.zbf[0:16, :], tok_z, [16, T], BF16) if t == 0 else None
            tok_zb_free = [tok_z, tap_z]

            tok_ae = None
            for cc in range(DC):
                need("t", tok_z, tok_zb_free, tok_qz_free)
                tok_zb = op("t", lambda e, cc=cc: e.matmul(
                    self.vban[:, 512:1024], self.sel_bf[0:16, cc * 128:(cc + 1) * 128],
                    self.zbf[0:16, :], start=True, stop=True), inc="p")
                need("v", tok_zb)
                tok_qz = op("v", lambda e, cc=cc: e.tensor_tensor(
                    self.qz[:, cc * T:(cc + 1) * T], self.qraw[:, cc * T:(cc + 1) * T],
                    self.vban[:, 512:1024], ALU.mult), inc="v", fence=True)
                tok_zb_free = tok_qz
                need("t", tok_qz, self.tok_kvbf, tok_attn_bank_free)
                op("t", lambda e, cc=cc: e.matmul(
                    self.kvA[0:64, 0:512], self.kv_bf[0:64, cc * 65: cc * 65 + 64],
                    self.qz[0:64, cc * T:(cc + 1) * T], start=True, stop=True))
                tok_at = op("t", lambda e, cc=cc: e.matmul(
                    self.kvB[0:64, 0:512], self.kv_bf[64:128, cc * 65: cc * 65 + 64],
                    self.qz[64:128, cc * T:(cc + 1) * T], start=True, stop=True), inc="p")
                need("s", tok_at, tok_attn_sb_free)
                self.op("s", lambda e, cc=cc: e.activation(
                    self.attn_sb[0:64, cc * T:(cc + 1) * T], self.kvA[0:64, 0:512], AF.Copy, bias=0.0, scale=1.0))
                tok_ae = self.op("s", lambda e, cc=cc: e.activation(
                    self.attn_sb[64:128, cc * T:(cc + 1) * T], self.kvB[0:64, 0:512], AF.Copy, bias=0.0, scale=1.0), inc="s", fence=True)
                tok_attn_bank_free = tok_ae
            tap_attn = self.tap("attnsb", self.attn_sb[:, :], tok_ae, [128, DC * T], BF16) if t == 0 else None
            tok_qz_free = tok_ae
            tok_qraw_free = [tok_ae, tap_qraw]

            tok_ye = None
            ysq_q = [None, None]
            tok_stats_last = None
            tok_woj = None
            for j in range(DC):
                bank = (self.sban, self.tban)[j % 2]
                need("t", self.tok_wo, tok_ae, tok_sban_free)
                for cc in range(DC):
                    tok_woj = op("t", lambda e, j=j, cc=cc, bank=bank: e.matmul(
                        bank[:, :], self.wot_bf[:, cc * D + j * 128: cc * D + j * 128 + 128],
                        self.attn_sb[:, cc * T:(cc + 1) * T],
                        start=(cc == 0), stop=(cc == DC - 1)),
                        inc=("p" if cc == DC - 1 else None))
                need("s", tok_woj, tok_y_free)
                tok_ye = self.op("s", lambda e, j=j, bank=bank: e.activation(
                    self.y_sb[:, j * T:(j + 1) * T], bank[:, :], AF.Identity, bias=self.boc[:, j:j + 1], scale=1.0), inc="s", fence=True)
                tok_sban_free = tok_ye
                need("t", tok_ye)
                op("t", lambda e, j=j: e.matmul(
                    self.vban[0:1, 0:512], self.ones_bf[:, 0:1], self.y_sb[:, j * T:(j + 1) * T],
                    start=(j == 0), stop=(j == DC - 1)), inc="p")
                yq = self.ysq[j % 2]
                need("g", tok_ye, ysq_q[j % 2])
                tok_yq = op("g", lambda e, j=j, yq=yq: e.tensor_tensor(
                    yq[:, :], self.y_sb[:, j * T:(j + 1) * T], self.y_sb[:, j * T:(j + 1) * T], ALU.mult), inc="g", fence=True)
                need("t", tok_yq)
                tok_stats_last = op("t", lambda e, j=j, yq=yq: e.matmul(
                    self.vban[0:1, 512:1024], self.ones_bf[:, 0:1], yq[:, :],
                    start=(j == 0), stop=(j == DC - 1)), inc="p")
                ysq_q[j % 2] = tok_stats_last
            rows_tok = self.ln_rows(self.vban[0:1, 0:512], self.vban[0:1, 512:1024], tok_stats_last, tok_rows_free)
            tok_rows_free = self.bcast_ab(self.vban[:, 0:512], self.vban[:, 512:1024], rows_tok, None)
            tok_ab2 = self.evac_ab(self.vban[:, 0:512], self.vban[:, 512:1024], self.ab_sbB, tok_rows_free, tok_ab_cons)
            tok_vban_free = tok_ab2
            need("v", tok_ab2, tok_ye)
            for cc in range(DC):
                op("v", lambda e, cc=cc: e.tensor_tensor(
                    self.t1_sbB[:, :], self.y_sb[:, cc * T:(cc + 1) * T], self.ab_sbB[:, 0:512], ALU.mult))
                op("v", lambda e: e.tensor_tensor(
                    self.t2_sbB[:, :], self.t1_sbB[:, :], self.ab_sbB[:, 512:1024], ALU.add))
                tok_x1 = op("v", lambda e, cc=cc, t=t: e.tensor_tensor(
                    self.xr[:, cc * NB + t * T: cc * NB + (t + 1) * T],
                    self.xr[:, cc * NB + t * T: cc * NB + (t + 1) * T], self.t2_sbB[:, :], ALU.add),
                    inc=("v" if cc == DC - 1 else None), fence=(cc == DC - 1))
            tap_y = self.tap("ysb", self.y_sb[:, :], tok_ye, [128, DC * T], BF16) if t == 0 else None
            tok_ab_cons = tok_x1
            tok_y_free = [tok_x1, tap_y]
            tok_attn_sb_free = [tok_woj, tap_attn]
        self.tap("xr", self.xr[:, :], tok_x1, [128, DC * NB], BF16)
        return dict(b_done=tok_x1, sban=tok_sban_free, vban=tok_vban_free,
                    kban=tok_kban_free, rows=tok_rows_free, ab_cons=tok_ab_cons,
                    ln3=st["kv_evac"], attn_bank=tok_attn_bank_free)

    def phase_c(self, st):
        op, need = self.op, self.need
        tok_bdone = st["b_done"]
        tok_sban_free = st["sban"]
        tok_rows_free = st["rows"]
        tok_ab_cons = st["ab_cons"]
        tok_kban_free = list(st["kban"])
        tok_vban_free = [st["vban"], st["vban"]]
        tok_ln3_bank_free = st["attn_bank"]
        tok_ln2_free = None
        tok_fsq_free = None
        tok_hid_w2_done = None
        tok_fout_cons = None
        tok_outs_free = [None, None]
        tok_sdcd_free = [None, None]
        tok_out_last = None

        sh = self.semh

        def load_w1(fb, par):
            need("y", self.tok_w1bf[fb], tok_bdone)
            prior = self.cnt["yw"]
            if prior >= 16:
                self.qs["y"].append(lambda e, v=prior: e.wait_ge(sh["yw"], v))
            return self.op("y", lambda e, fb=fb, par=par: e.dma_start(
                out=self.w1b[par][:, :], in_=self.w1bf.ap()[fb, :, :]), inc="yw", n=16)

        def load_w2(j, par):
            need("y", self.tok_w2bf[j], tok_bdone)
            prior = self.cnt["yw2"]
            if prior >= 16:
                self.qs["y"].append(lambda e, v=prior: e.wait_ge(sh["yw2"], v))
            return self.op("y", lambda e, j=j, par=par: e.dma_start(
                out=self.w2b[par][:, :], in_=self.w2bf.ap()[j, :, :]), inc="yw2", n=16)

        tok_w1 = [load_w1(0, 0), load_w1(1, 1)]
        tok_w2 = [load_w2(0, 0), load_w2(1, 1)]

        for t in range(NTB):
            need("t", tok_sban_free, tok_bdone)
            for cc in range(DC):
                op("t", lambda e, cc=cc, t=t: e.matmul(
                    self.sban[0:1, :], self.ones_bf[:, 0:1],
                    self.xr[:, cc * NB + t * T: cc * NB + (t + 1) * T],
                    start=(cc == 0), stop=(cc == DC - 1)),
                    inc=("p" if cc == DC - 1 else None))
            tok_stat = None
            fsq_q = [None, None]
            for cc in range(DC):
                fq = self.fsq[cc % 2]
                need("g", tok_bdone, fsq_q[cc % 2], tok_fsq_free)
                tok_fq = op("g", lambda e, cc=cc, t=t, fq=fq: e.tensor_tensor(
                    fq[:, :], self.xr[:, cc * NB + t * T: cc * NB + (t + 1) * T],
                    self.xr[:, cc * NB + t * T: cc * NB + (t + 1) * T], ALU.mult), inc="g", fence=True)
                need("t", tok_fq)
                tok_stat = op("t", lambda e, cc=cc, fq=fq: e.matmul(
                    self.tban[0:1, :], self.ones_bf[:, 0:1], fq[:, :],
                    start=(cc == 0), stop=(cc == DC - 1)), inc="p")
                fsq_q[cc % 2] = tok_stat
            rows_tok = self.ln_rows(self.sban[0:1, 0:512], self.tban[0:1, 0:512], tok_stat, tok_rows_free)
            tok_rows_free = self.bcast_ab(self.sban[:, :], self.tban[:, :], rows_tok, None)
            tok_ab = self.evac_ab(self.sban[:, :], self.tban[:, :], self.ab_sbC, tok_rows_free, tok_ab_cons)
            tok_sban_free = tok_ab
            need("g", tok_ab, tok_ln2_free)
            tok_ln2 = None
            for cc in range(DC):
                op("g", lambda e, cc=cc, t=t: e.tensor_tensor(
                    self.ptmpC[:, :], self.xr[:, cc * NB + t * T: cc * NB + (t + 1) * T],
                    self.ab_sbC[:, 0:512], ALU.mult))
                tok_ln2 = op("g", lambda e, cc=cc: e.tensor_tensor(
                    self.ln2C[:, cc * T:(cc + 1) * T], self.ptmpC[:, :], self.ab_sbC[:, 512:1024], ALU.add),
                    inc=("g" if cc == DC - 1 else None), fence=(cc == DC - 1))
            tok_ab_cons = tok_ln2
            tap_ln2 = self.tap("ln2C", self.ln2C[:, :], tok_ln2, [128, DC * T], BF16) if t == 0 else None

            tok_hid = None
            tap_j30 = None
            for j in range(FC):
                fb = j // 4
                par = fb % 2
                half = j % 2
                if j % 4 == 0:
                    need("t", tok_w1[par])
                need("t", tok_ln2, tok_kban_free[half])
                tok_w1m = None
                for cc in range(DC):
                    tok_w1m = op("t", lambda e, j=j, cc=cc, half=half, par=par: e.matmul(
                        self.kban[:, half * 512:(half + 1) * 512],
                        self.w1b[par][:, cc * 512 + (j % 4) * 128: cc * 512 + (j % 4) * 128 + 128],
                        self.ln2C[:, cc * T:(cc + 1) * T],
                        start=(cc == 0), stop=(cc == DC - 1)),
                        inc=("p" if cc == DC - 1 else None))
                if j % 4 == 3:
                    nxt = fb + 2
                    if nxt < DC:
                        need("y", tok_w1m)
                        tok_w1[nxt % 2] = load_w1(nxt, nxt % 2)
                        if t == 0 and nxt == 7:
                            self.tap("w1b1fb7", self.w1b[1][:, :], tok_w1[1], [128, FF], BF16)
                    elif t + 1 < NTB and nxt - DC < 2:
                        need("y", tok_w1m)
                        tok_w1[nxt % 2] = load_w1(nxt - DC, nxt % 2)
                ps = self.kban[:, half * 512:(half + 1) * 512]
                sp = j % 2
                need("s", tok_w1m, tok_sdcd_free[sp])
                self.op("s", lambda e, ps=ps, j=j, sp=sp: e.activation(
                    self.sd_sb[sp][:, :], ps, AF.Sin, bias=self.b1s1[:, j:j + 1], scale=0.1))
                self.op("s", lambda e, ps=ps, j=j, sp=sp: e.activation(
                    self.cd_sb[sp][:, :], ps, AF.Sin, bias=self.b1s2[:, j:j + 1], scale=0.1))
                tok_h = self.op("s", lambda e, ps=ps, j=j, sp=sp: e.activation(
                    self.hbf_sb[sp][:, :], ps, AF.Identity, bias=self.b1c[:, j:j + 1], scale=1.0), inc="s", fence=True)
                tok_kban_free[half] = tok_h
                if t == 0 and j >= 28:
                    tap_j30 = [
                        self.tap("hbf%d" % j, self.hbf_sb[sp][:, :], tok_h, [128, T], BF16),
                        self.tap("sd%d" % j, self.sd_sb[sp][:, :], tok_h, [128, T], BF16),
                    ]
                else:
                    tap_j30 = None
                need("v", tok_h, tok_hid_w2_done if j < 2 else None)
                op("v", lambda e, j=j, sp=sp: e.scalar_tensor_tensor(
                    self.t12_sb[sp][:, :], self.sd_sb[sp][:, :], self.qB[:, j:j + 1], self.hbf_sb[sp][:, :], ALU.mult, ALU.add))
                tok_u = op("v", lambda e, j=j, sp=sp: e.scalar_tensor_tensor(
                    self.u_sb[sp][:, :], self.cd_sb[sp][:, :], self.qA[:, j:j + 1], self.t12_sb[sp][:, :], ALU.mult, ALU.add), inc="v")
                tok_sdcd_free[sp] = [tok_u, tap_j30]
                tok_hid = op("v", lambda e, j=j, sp=sp: e.tensor_scalar(
                    self.hidden[:, j * T:(j + 1) * T], self.u_sb[sp][:, :], 0.0, None, ALU.max, ALU.bypass), inc="v", fence=True)
            tap_hid = self.tap("hidden", self.hidden[:, :], tok_hid, [128, FC * T], BF16) if t == 0 else None
            tok_ln2_free = [tok_hid, tap_ln2]

            tok_stats_last = None
            tok_fe = None
            tok_w2m = None
            for j in range(DC):
                par = j % 2
                need("t", tok_w2[par], tok_hid, tok_vban_free[par])
                for i in range(FC):
                    tok_w2m = op("t", lambda e, j=j, i=i, par=par: e.matmul(
                        self.vban[:, par * 512:(par + 1) * 512],
                        self.w2b[par][:, i * 128:(i + 1) * 128],
                        self.hidden[:, i * T:(i + 1) * T],
                        start=(i == 0), stop=(i == FC - 1)),
                        inc=("p" if i == FC - 1 else None))
                nxt = j + 2
                if nxt < DC:
                    need("y", tok_w2m)
                    tok_w2[nxt % 2] = load_w2(nxt, nxt % 2)
                elif t + 1 < NTB and nxt - DC < 2:
                    need("y", tok_w2m)
                    tok_w2[nxt % 2] = load_w2(nxt - DC, nxt % 2)
                need("s", tok_w2m, tok_fout_cons if j < 2 else None)
                tok_fe = self.op("s", lambda e, j=j, par=par: e.activation(
                    self.fout[:, j * T:(j + 1) * T], self.vban[:, par * 512:(par + 1) * 512],
                    AF.Identity, bias=self.b2c[:, j:j + 1], scale=1.0), inc="s", fence=True)
                tok_vban_free[par] = tok_fe
                need("t", tok_fe, tok_ln3_bank_free if j == 0 else None)
                op("t", lambda e, j=j: e.matmul(
                    self.kvA[0:1, :], self.ones_bf[:, 0:1], self.fout[:, j * T:(j + 1) * T],
                    start=(j == 0), stop=(j == DC - 1)), inc="p")
                fq = self.fsq[j % 2]
                need("g", tok_fe, fsq_q[j % 2])
                tok_fq = op("g", lambda e, j=j, fq=fq: e.tensor_tensor(
                    fq[:, :], self.fout[:, j * T:(j + 1) * T], self.fout[:, j * T:(j + 1) * T], ALU.mult), inc="g")
                need("t", tok_fq)
                tok_stats_last = op("t", lambda e, j=j, fq=fq: e.matmul(
                    self.kvB[0:1, :], self.ones_bf[:, 0:1], fq[:, :],
                    start=(j == 0), stop=(j == DC - 1)), inc="p")
                fsq_q[j % 2] = tok_stats_last
            tap_fout = self.tap("fout", self.fout[:, :], tok_fe, [128, DC * T], BF16) if t == 0 else None
            tok_hid_w2_done = [tok_w2m, tap_hid]
            rows_tok = self.ln_rows(self.kvA[0:1, 0:512], self.kvB[0:1, 0:512], tok_stats_last, tok_rows_free)
            tok_rows_free = self.bcast_ab(self.kvA[:, :], self.kvB[:, :], rows_tok, None)
            tok_ab3 = self.evac_ab(self.kvA[:, :], self.kvB[:, :], self.ab_sbC, tok_rows_free, tok_ab_cons)
            tok_ln3_bank_free = tok_ab3
            tok_ab_cons = None
            need("v", tok_ab3, tok_fe)
            tok_of = None
            for cc in range(DC):
                par = cc % 2
                need("v", tok_outs_free[par])
                op("v", lambda e, cc=cc: e.tensor_tensor(
                    self.t1_sbC[:, :], self.fout[:, cc * T:(cc + 1) * T], self.ab_sbC[:, 0:512], ALU.mult))
                op("v", lambda e: e.tensor_tensor(
                    self.t2f_sb[:, :], self.t1_sbC[:, :], self.ab_sbC[:, 512:1024], ALU.add))
                tok_of = op("v", lambda e, cc=cc, t=t, par=par: e.tensor_tensor(
                    self.outs[par][:, :], self.xr[:, cc * NB + t * T: cc * NB + (t + 1) * T],
                    self.t2f_sb[:, :], ALU.add), inc="v", fence=True)
                need("y", tok_of)
                prior = self.cnt["yo"]
                if prior >= 16:
                    self.qs["y"].append(lambda e, v=prior: self.semh and e.wait_ge(self.semh["yo"], v))
                tok_out_last = self.op("y", lambda e, cc=cc, t=t, par=par: e.dma_start(
                    out=self.out_d[cc * 128:(cc + 1) * 128, t * T:(t + 1) * T], in_=self.outs[par][:, :]), inc="yo", n=16)
                tok_outs_free[par] = tok_out_last
            tok_fout_cons = [tok_of, tap_fout]
            tok_fsq_free = tok_stats_last
            tok_ab_cons = tok_of

        need("y", tok_out_last)


_BUILT = None


def _get_built():
    global _BUILT
    if _BUILT is None:
        kb = KB()
        nc = kb.build()
        _BUILT = (kb, nc)
    return _BUILT


def _prepT(W):
    return np.ascontiguousarray(W.T.reshape(DC, 128, D).transpose(1, 0, 2).reshape(128, DC * D))


def kernel(**inputs):
    x = np.asarray(inputs["L0_attn_hidden_states"], np.float32)
    Wq = np.asarray(inputs["L0_attn_Wq"], np.float32)
    bq = np.asarray(inputs["L0_attn_bq"], np.float32)
    Wk = np.asarray(inputs["L0_attn_Wk"], np.float32)
    bk = np.asarray(inputs["L0_attn_bk"], np.float32)
    Wv = np.asarray(inputs["L0_attn_Wv"], np.float32)
    bv = np.asarray(inputs["L0_attn_bv"], np.float32)
    Wo = np.asarray(inputs["L0_attn_Wo"], np.float32)
    bo = np.asarray(inputs["L0_attn_bo"], np.float32)
    W1 = np.asarray(inputs["L0_ffn_W1"], np.float32)
    b1 = np.asarray(inputs["L0_ffn_b1"], np.float32)
    th = np.asarray(inputs["L0_ffn_quantum_phases"], np.float32)
    wr = np.asarray(inputs["L0_ffn_quantum_weights_real"], np.float32)
    wi = np.asarray(inputs["L0_ffn_quantum_weights_imag"], np.float32)
    W2 = np.asarray(inputs["L0_ffn_W2"], np.float32)
    b2 = np.asarray(inputs["L0_ffn_b2"], np.float32)

    wqt, wkt, wvt, wot = _prepT(Wq), _prepT(Wk), _prepT(Wv), _prepT(Wo)
    w1p = np.ascontiguousarray(W1.reshape(DC, 128, 8, 512).transpose(2, 1, 0, 3).reshape(DC, 128, FF))
    w2p = np.ascontiguousarray(W2.reshape(FC, 128, DC, 128).transpose(2, 1, 0, 3).reshape(DC, 128, FF))

    cols = lambda vec, n: np.ascontiguousarray(vec.reshape(n, 128).T)
    sel = np.zeros((16, DC * 128), np.float32)
    for cc in range(DC):
        sel[2 * cc, cc * 128: cc * 128 + 64] = 1.0
        sel[2 * cc + 1, cc * 128 + 64: cc * 128 + 128] = 1.0
    nonce = np.random.randint(1, 2**31 - 1, size=(1, 2)).astype(np.int32)

    kb, nc = _get_built()
    in_maps = []
    for c in range(8):
        b = c // 4
        sA = (c % 2) * NA
        sB = sA + ((c % 4) // 2) * NB
        xT = np.ascontiguousarray(x[b].T)
        rest = sA + NB if sB == sA else sA
        xA = np.concatenate([xT[:, sB:sB + NB], xT[:, rest:rest + NB]], axis=1)
        in_maps.append(dict(
            xA=np.ascontiguousarray(xA),
            wqt=wqt, wkt=wkt, wvt=wvt, wot=wot, w1p=w1p, w2p=w2p,
            bkr=bk.reshape(1, D), bvr=bv.reshape(1, D),
            bqc_i=cols(bq, DC), boc_i=cols(bo, DC), b1c_i=cols(b1, FC), b2c_i=cols(b2, DC),
            th_i=cols(th, FC), wr_i=cols(wr, FC), wi_i=cols(wi, FC),
            ones_i=np.ones((128, 128), np.float32),
            sel_i=sel,
            nonce_i=nonce,
            side_i=np.array([[c & 1, 1 - (c & 1)]], np.int32),
        ))
    res = run_bass_kernel_spmd(nc, in_maps, core_ids=list(range(8)))
    out = np.empty((2, 16384, D), np.float32)
    for c in range(8):
        b = c // 4
        sB = (c % 2) * NA + ((c % 4) // 2) * NB
        out[b, sB:sB + NB, :] = res.results[c]["out"].T
    return out


# version 20 (final)
# speedup vs baseline: 10.7021x; 2.0166x over previous
import numpy as np
import concourse.bass as bass
import concourse.mybir as mybir
from concourse.bass_utils import run_bass_kernel_spmd
from contextlib import ExitStack

F32 = mybir.dt.float32
BF16 = mybir.dt.bfloat16
I32 = mybir.dt.int32
AF = mybir.ActivationFunctionType
ALU = mybir.AluOpType
PI = float(np.pi)

D, DC = 1024, 8
FF, FC = 4096, 32
H, DH = 16, 64
T = 512
NA, NB = 8192, 4096
NTA, NTB = NA // T, NB // T
SCH = T // 128
EPS_LN = 1e-5
EPS_Z = 1e-6

SB_BASE = 16512
SB_TOP = 229344


class Lay:
    def __init__(self, nc, base):
        self.nc = nc
        self.off = base
        self.end = base

    def al(self, name, shape, dtype):
        per = int(np.prod(shape[1:])) * mybir.dt.size(dtype)
        self.off = -(-self.off // 32) * 32
        h = self.nc.alloc_sbuf_tensor_at(name, list(shape), dtype, offset=self.off)
        self.off += per
        self.end = max(self.end, self.off)
        return h


class KB:
    ENGS = ("t", "v", "s", "g", "y")

    def __init__(self, debug=False):
        self.debug = debug
        self.nc = bass.Bass("TRN2", target_bir_lowering=False)
        self.qs = {e: [] for e in self.ENGS}
        self.cnt = {k: 0 for k in ("p", "v", "s", "g", "gs", "gw", "gc", "gq", "gx", "ge", "yw", "yw2", "yo", "od")}
        self.obs = {e: {} for e in self.ENGS}
        self.semh = {}

    def op(self, eng, fn, inc=None, n=1, fence=False):
        if fence != "never" and inc in ("v", "s", "g"):
            fence = True
        if fence is True and inc is not None:
            self.qs[eng].append(fn)
            if eng == "v":
                ffn = lambda e: e.tensor_copy(self.fen_v[0:1, 0:8], self.fen_v[0:1, 8:16])
            elif eng == "s":
                ffn = lambda e: e.activation(self.fen_s[0:1, 0:8], self.fen_s[0:1, 8:16], AF.Copy, bias=0.0, scale=1.0)
            else:
                ffn = lambda e: e.tensor_copy(self.fen_g[0:1, 0:8], self.fen_g[0:1, 8:16])
            return self.op(eng, ffn, inc=inc, n=n, fence="never")
        if inc is None:
            self.qs[eng].append(fn)
            return None
        self.cnt[inc] += n
        val = self.cnt[inc]
        sh = self.semh

        def wrapped(e, fn=fn, inc=inc, n=n):
            fn(e).then_inc(sh[inc], n)

        self.qs[eng].append(wrapped)
        if inc in ("p", "v", "s", "g"):
            ob = self.obs[eng]
            ob[inc] = max(ob.get(inc, 0), val)
        return (inc, val)

    def vfence(self):
        tok = self.op("v", lambda e: e.tensor_copy(self.fen_v[0:1, 0:8], self.fen_v[0:1, 8:16]), inc="v", fence="never")
        sh = self.semh
        self.qs["v"].append(lambda e, v=tok[1]: e.wait_ge(sh["v"], v))
        return tok

    def tap(self, name, src_ap, tok, shape, dtype):
        if not self.debug:
            return None
        d = self.nc.dram_tensor("dbg_" + name, list(shape), dtype, kind="ExternalOutput").ap()
        self.need("y", tok)
        return self.op("y", lambda e, d=d, s=src_ap: e.dma_start(out=d, in_=s), inc="od", n=16)

    def need(self, eng, *tokens):
        sh = self.semh
        for tok in tokens:
            if tok is None:
                continue
            if isinstance(tok, list):
                self.need(eng, *tok)
                continue
            sem, val = tok
            ob = self.obs[eng]
            if ob.get(sem, 0) >= val:
                continue
            ob[sem] = val
            self.qs[eng].append(lambda e, sem=sem, val=val: e.wait_ge(sh[sem], val))

    def alloc(self):
        nc = self.nc
        self.xA = nc.dram_tensor("xA", [D, NA], F32, kind="ExternalInput").ap()
        self.wqt = nc.dram_tensor("wqt", [128, DC * D], F32, kind="ExternalInput").ap()
        self.wkt = nc.dram_tensor("wkt", [128, DC * D], F32, kind="ExternalInput").ap()
        self.wvt = nc.dram_tensor("wvt", [128, DC * D], F32, kind="ExternalInput").ap()
        self.wot = nc.dram_tensor("wot", [128, DC * D], F32, kind="ExternalInput").ap()
        self.w1p = nc.dram_tensor("w1p", [DC, 128, FF], F32, kind="ExternalInput").ap()
        self.w2p = nc.dram_tensor("w2p", [DC, 128, FF], F32, kind="ExternalInput").ap()
        self.bkr = nc.dram_tensor("bkr", [1, D], F32, kind="ExternalInput").ap()
        self.bvr = nc.dram_tensor("bvr", [1, D], F32, kind="ExternalInput").ap()
        self.bqc_i = nc.dram_tensor("bqc_i", [128, DC], F32, kind="ExternalInput").ap()
        self.boc_i = nc.dram_tensor("boc_i", [128, DC], F32, kind="ExternalInput").ap()
        self.b1c_i = nc.dram_tensor("b1c_i", [128, FC], F32, kind="ExternalInput").ap()
        self.b2c_i = nc.dram_tensor("b2c_i", [128, DC], F32, kind="ExternalInput").ap()
        self.th_i = nc.dram_tensor("th_i", [128, FC], F32, kind="ExternalInput").ap()
        self.wr_i = nc.dram_tensor("wr_i", [128, FC], F32, kind="ExternalInput").ap()
        self.wi_i = nc.dram_tensor("wi_i", [128, FC], F32, kind="ExternalInput").ap()
        self.ones_i = nc.dram_tensor("ones_i", [128, 128], F32, kind="ExternalInput").ap()
        self.sel_i = nc.dram_tensor("sel_i", [16, DC * 128], F32, kind="ExternalInput").ap()
        self.nonce_i = nc.dram_tensor("nonce_i", [1, 2], I32, kind="ExternalInput").ap()
        self.side_i = nc.dram_tensor("side_i", [1, 2], I32, kind="ExternalInput").ap()
        self.out_d = nc.dram_tensor("out", [D, NB], F32, kind="ExternalOutput").ap()
        self.w1bf = nc.dram_tensor("w1bf", [DC, 128, FF], BF16)
        self.w2bf = nc.dram_tensor("w2bf", [DC, 128, FF], BF16)
        self.shpay = nc.dram_tensor("shpay", [2, 128, 520], F32, addr_space="Shared")
        self.shflag = nc.dram_tensor("shflag", [2, 16], I32, addr_space="Shared")

        st = Lay(nc, SB_BASE)
        self.xr = st.al("xr", [128, DC * NB], BF16)
        self.kv_acc = st.al("kv_acc", [128, 520], F32)
        self.kv_peer = st.al("kv_peer", [128, 520], F32)
        self.kv_bf = st.al("kv_bf", [128, 520], BF16)
        self.bd_bf = st.al("bd_bf", [128, 128], BF16)
        self.ones_bf = st.al("ones_bf", [128, 128], BF16)
        self.bqc = st.al("bqc", [128, DC], F32)
        self.nbqc = st.al("nbqc", [128, DC], F32)
        self.boc = st.al("boc", [128, DC], F32)
        self.b1c = st.al("b1c", [128, FC], F32)
        self.b1s1 = st.al("b1s1", [128, FC], F32)
        self.b1s2 = st.al("b1s2", [128, FC], F32)
        self.b2c = st.al("b2c", [128, DC], F32)
        self.qA = st.al("qA", [128, FC], F32)
        self.qB = st.al("qB", [128, FC], F32)
        self.sel_bf = st.al("sel_bf", [16, DC * 128], BF16)
        self.a_row = st.al("a_row", [1, NB], BF16)
        self.b_row = st.al("b_row", [1, NB], BF16)
        self.nonce_sb = st.al("nonce_sb", [1, 16], I32)
        self.side_sb = st.al("side_sb", [1, 2], I32)
        self.rowm = st.al("rowm", [1, 512], F32)
        self.rowv = st.al("rowv", [1, 512], F32)
        self.rowt = st.al("rowt", [1, 512], F32)
        self.rowa = st.al("rowa", [1, 512], BF16)
        self.rowb = st.al("rowb", [1, 512], BF16)
        self.qs1 = st.al("qs1", [128, FC], F32)
        self.qs2 = st.al("qs2", [128, FC], F32)
        self.qs3 = st.al("qs3", [128, FC], F32)
        self.th_c = st.al("th_c", [128, FC], F32)
        self.wr_c = st.al("wr_c", [128, FC], F32)
        self.wi_c = st.al("wi_c", [128, FC], F32)
        self.fen_v = st.al("fen_v", [1, 16], F32)
        self.fen_s = st.al("fen_s", [1, 16], F32)
        self.fen_g = st.al("fen_g", [1, 16], F32)
        rb = -(-st.off // 64) * 64

        A = Lay(nc, rb)
        self.wkt_bf = A.al("wkt_bf", [128, DC * D], BF16)
        self.wvt_bf = A.al("wvt_bf", [128, DC * D], BF16)
        self.xa0 = A.al("xa0", [128, DC * T], BF16)
        self.xa1 = A.al("xa1", [128, DC * T], BF16)
        self.ln1A = [A.al("ln1A0", [128, DC * T], BF16), A.al("ln1A1", [128, DC * T], BF16)]
        self.k_sb = [A.al("k_sb0", [128, D], BF16), A.al("k_sb1", [128, D], BF16)]
        self.v_sb = [A.al("v_sb0", [128, H * 65], BF16), A.al("v_sb1", [128, H * 65], BF16)]
        self.ab_sbA = A.al("ab_sbA", [128, 1024], F32)
        self.bkb = A.al("bkb", [128, D], F32)
        self.bvb = A.al("bvb", [128, D], F32)
        self.kpre = A.al("kpre", [128, D], F32)
        self.m0_sb = A.al("m0_sb", [128, D], F32)
        self.e_sbA = A.al("e_sbA", [128, D], F32)
        self.xsq = A.al("xsq", [128, DC * T], BF16)
        self.ptmpA = A.al("ptmpA", [128, T], BF16)
        self.bkr_bf = A.al("bkr_bf", [1, D], BF16)
        self.bvr_bf = A.al("bvr_bf", [1, D], BF16)

        B = Lay(nc, rb)
        self.wqt_bf = B.al("wqt_bf", [128, DC * D], BF16)
        self.wot_bf = B.al("wot_bf", [128, DC * D], BF16)
        self.ln1B = [B.al("ln1B0", [128, DC * T], BF16), B.al("ln1B1", [128, DC * T], BF16)]
        self.qraw = B.al("qraw", [128, DC * T], BF16)
        self.qz = B.al("qz", [128, DC * T], BF16)
        self.attn_sb = B.al("attn_sb", [128, DC * T], BF16)
        self.y_sb = B.al("y_sb", [128, DC * T], BF16)
        self.ab_sbB = B.al("ab_sbB", [128, 1024], F32)
        self.ab_sbB2 = B.al("ab_sbB2", [128, 1024], F32)
        self.relu1 = B.al("relu1", [128, T], F32)
        self.e_sbB = [B.al("e_sbB0", [128, T], F32), B.al("e_sbB1", [128, T], F32)]
        self.p2_sb = [B.al("p2_sb0", [128, T], F32), B.al("p2_sb1", [128, T], F32)]
        self.ysq = [B.al("ysq0", [128, T], BF16), B.al("ysq1", [128, T], BF16)]
        self.t1_sbB = B.al("t1_sbB", [128, T], BF16)
        self.t2_sbB = B.al("t2_sbB", [128, T], BF16)
        self.zden = B.al("zden", [16, T], F32)
        self.zbuf = B.al("zbuf", [16, T], F32)
        self.zbf = B.al("zbf", [16, T], BF16)
        self.ptmpB = B.al("ptmpB", [128, T], BF16)

        C = Lay(nc, rb)
        self.w1b = [C.al("w1b0", [128, FF], BF16), C.al("w1b1", [128, FF], BF16)]
        self.w2b = [C.al("w2b0", [128, FF], BF16), C.al("w2b1", [128, FF], BF16)]
        self.ln2C = C.al("ln2C", [128, DC * T], BF16)
        self.hidden = C.al("hidden", [128, FC * T], BF16)
        self.fout = C.al("fout", [128, DC * T], BF16)
        self.ab_sbC = C.al("ab_sbC", [128, 1024], F32)
        self.ab_sbC2 = C.al("ab_sbC2", [128, 1024], F32)
        self.sd_sb = [C.al("sd_sb0", [128, T], BF16), C.al("sd_sb1", [128, T], BF16)]
        self.cd_sb = [C.al("cd_sb0", [128, T], BF16), C.al("cd_sb1", [128, T], BF16)]
        self.hbf_sb = [C.al("hbf0", [128, T], BF16), C.al("hbf1", [128, T], BF16)]
        self.t12_sb = [C.al("t12_sb0", [128, T], BF16), C.al("t12_sb1", [128, T], BF16)]
        self.t1_sbC = C.al("t1_sbC", [128, T], BF16)
        self.t2f_sb = C.al("t2f_sb", [128, T], BF16)
        self.u_sb = [C.al("u_sb0", [128, T], F32), C.al("u_sb1", [128, T], F32)]
        self.fsq = [C.al("fsq0", [128, T], BF16), C.al("fsq1", [128, T], BF16)]
        self.outs = [C.al("outs0", [128, T], F32), C.al("outs1", [128, T], F32)]
        self.ptmpC = C.al("ptmpC", [128, T], BF16)

        assert max(A.end, B.end, C.end) <= SB_TOP, (st.off, A.end, B.end, C.end)

        self.kban = nc.alloc_psum_tensor("kban", [128, 1024], F32)
        self.vban = nc.alloc_psum_tensor("vban", [128, 1024], F32)
        self.sban = nc.alloc_psum_tensor("sban", [128, 512], F32)
        self.tban = nc.alloc_psum_tensor("tban", [128, 512], F32)
        self.kvA = nc.alloc_psum_tensor("kvA", [128, 512], F32)
        self.kvB = nc.alloc_psum_tensor("kvB", [128, 512], F32)

    def build(self):
        self.alloc()
        with ExitStack() as stack:
            E = stack.enter_context
            for name in ("p", "v", "s", "g", "gs", "gw", "gc", "gq", "gx", "ge", "yw", "yw2", "yo", "od"):
                self.semh[name] = E(self.nc.semaphore(name))
            block = E(self.nc.Block())
            self.emit()
            for eng, sect in (("t", block.tensor), ("v", block.vector), ("s", block.scalar),
                              ("g", block.gpsimd), ("y", block.sync)):
                def body(e, eng=eng):
                    for fn in self.qs[eng]:
                        fn(e)
                sect(body)
        return self.nc

    def emit(self):
        self.setup()
        st = self.phase_a()
        st = self.exchange(st)
        st = self.phase_b(st)
        self.phase_c(st)

    def setup(self):
        op, need = self.op, self.need
        toks = {}
        for nm, dst, src in (
            ("ones", self.ones_bf, self.ones_i), ("sel", self.sel_bf, self.sel_i),
            ("bqc", self.bqc, self.bqc_i), ("boc", self.boc, self.boc_i),
            ("b1c", self.b1c, self.b1c_i), ("b2c", self.b2c, self.b2c_i),
            ("th", self.th_c, self.th_i), ("wr", self.wr_c, self.wr_i),
            ("wi", self.wi_c, self.wi_i),
            ("bkr", self.bkr_bf, self.bkr), ("bvr", self.bvr_bf, self.bvr),
        ):
            toks[nm] = op("g", lambda e, d=dst, s=src: e.dma_start(out=d[:, :], in_=s), inc="gs", n=16)
        toks["nonce"] = op("g", lambda e: e.dma_start(out=self.nonce_sb[0:1, 0:2], in_=self.nonce_i), inc="gs", n=16)
        toks["side"] = op("g", lambda e: e.dma_start(out=self.side_sb[0:1, :], in_=self.side_i), inc="gs", n=16)
        tok_gs_all = ("gs", self.cnt["gs"])
        for nm in toks:
            toks[nm] = tok_gs_all
        self.tok_excfg = [tok_gs_all]

        self.xa = [self.xa0, self.xa1]
        self.tok_xa = [self.load_xa(0), self.load_xa(1)]

        self.tok_wk, self.tok_wv = [], []
        for w_bf, w_f, lst in ((self.wkt_bf, self.wkt, self.tok_wk), (self.wvt_bf, self.wvt, self.tok_wv)):
            for q4 in range(4):
                c0, c1 = q4 * (DC * D // 4), (q4 + 1) * (DC * D // 4)
                lst.append(op("g", lambda e, d=w_bf, s=w_f, c0=c0, c1=c1:
                              e.dma_start(out=d[:, c0:c1], in_=s[:, c0:c1], max_dma_last_dim=8192),
                              inc="gw", n=16))
        tok_gw_all = ("gw", self.cnt["gw"])
        self.tok_wk = [tok_gw_all]
        self.tok_wv = [tok_gw_all]
        self.big_casts = [("w1", j) for j in range(DC)] + [("w2", j) for j in range(DC)]
        self.tok_w1bf = [None] * DC
        self.tok_w2bf = [None] * DC

        op("v", lambda e: e.memset(self.v_sb[0][:, :], 1.0), inc="v")
        self.tok_vones = op("v", lambda e: e.memset(self.v_sb[1][:, :], 1.0), inc="v")

        need("v", toks["th"], toks["wr"], toks["wi"], toks["b1c"], toks["bqc"])
        v = lambda fn: op("v", fn, inc="v")
        v(lambda e: e.tensor_scalar(self.qs1[:, :], self.th_c[:, :], 1.0 / (2 * PI), None, ALU.mult, ALU.bypass))
        self.vfence()
        qs2_i = self.qs2.bitcast(I32)
        v(lambda e, q=qs2_i: e.tensor_copy(q[:, :], self.qs1[:, :]))
        self.vfence()
        v(lambda e, q=qs2_i: e.tensor_copy(self.qs1[:, :], q[:, :]))
        self.vfence()
        tok_thred = v(lambda e: e.scalar_tensor_tensor(self.qs1[:, :], self.qs1[:, :], -2 * PI, self.th_c[:, :], ALU.mult, ALU.add))
        need("s", tok_thred)
        self.op("s", lambda e: e.activation(self.qs2[:, :], self.qs1[:, :], AF.Sin, bias=0.0, scale=1.0), inc="s")
        tok_s2 = self.op("s", lambda e: e.activation(self.qs3[:, :], self.qs1[:, :], AF.Sin, bias=0.0, scale=0.5), inc="s")
        need("v", tok_s2)
        v(lambda e: e.tensor_tensor(self.qs1[:, :], self.qs3[:, :], self.qs3[:, :], ALU.mult))
        v(lambda e: e.tensor_scalar(self.qs1[:, :], self.qs1[:, :], -2.0, 1.0, ALU.mult, ALU.add))
        self.vfence()
        v(lambda e: e.tensor_tensor(self.qs3[:, :], self.qs1[:, :], self.wr_c[:, :], ALU.mult))
        v(lambda e: e.tensor_tensor(self.th_c[:, :], self.qs2[:, :], self.wi_c[:, :], ALU.mult))
        v(lambda e: e.tensor_tensor(self.qs3[:, :], self.qs3[:, :], self.th_c[:, :], ALU.add))
        v(lambda e: e.tensor_tensor(self.qs3[:, :], self.qs3[:, :], self.wi_c[:, :], ALU.mult))
        v(lambda e: e.tensor_scalar(self.qs3[:, :], self.qs3[:, :], 0.1, None, ALU.mult, ALU.bypass))
        self.vfence()
        v(lambda e: e.tensor_tensor(self.qA[:, :], self.qs3[:, :], self.qs2[:, :], ALU.mult))
        v(lambda e: e.tensor_tensor(self.qB[:, :], self.qs3[:, :], self.qs1[:, :], ALU.mult))
        v(lambda e: e.tensor_scalar(self.b1s1[:, :], self.b1c[:, :], 0.1, None, ALU.mult, ALU.bypass))
        self.vfence()
        v(lambda e: e.tensor_scalar(self.b1s2[:, :], self.b1s1[:, :], PI / 2, None, ALU.add, ALU.bypass))
        v(lambda e: e.tensor_scalar(self.nbqc[:, :], self.bqc[:, :], -1.0, None, ALU.mult, ALU.bypass))

        need("t", self.tok_vones, toks["ones"], toks["bkr"], toks["bvr"])
        op("t", lambda e: e.matmul(self.sban[:, :], self.ones_bf[0:1, 0:128], self.bkr_bf[0:1, 0:512], start=True, stop=True), inc="p")
        tk = op("t", lambda e: e.matmul(self.tban[:, :], self.ones_bf[0:1, 0:128], self.bkr_bf[0:1, 512:1024], start=True, stop=True), inc="p")
        need("s", tk)
        self.op("s", lambda e: e.activation(self.bkb[:, 0:512], self.sban[:, :], AF.Copy, bias=0.0, scale=1.0), inc="s")
        tok_kb = self.op("s", lambda e: e.activation(self.bkb[:, 512:1024], self.tban[:, :], AF.Copy, bias=0.0, scale=1.0), inc="s")
        need("t", tok_kb)
        op("t", lambda e: e.matmul(self.sban[:, :], self.ones_bf[0:1, 0:128], self.bvr_bf[0:1, 0:512], start=True, stop=True), inc="p")
        tv = op("t", lambda e: e.matmul(self.tban[:, :], self.ones_bf[0:1, 0:128], self.bvr_bf[0:1, 512:1024], start=True, stop=True), inc="p")
        need("s", tv)
        self.op("s", lambda e: e.activation(self.bvb[:, 0:512], self.sban[:, :], AF.Copy, bias=0.0, scale=1.0), inc="s")
        self.tok_setup_banks = self.op("s", lambda e: e.activation(self.bvb[:, 512:1024], self.tban[:, :], AF.Copy, bias=0.0, scale=1.0), inc="s")

    def load_xa(self, t):
        par = t % 2
        src = self.xA[:, t * T:(t + 1) * T].rearrange("(cc p) s -> p cc s", p=128)
        dst = self.xa[par][:, :].rearrange("p (cc s) -> p cc s", s=T)
        sh = self.semh
        if t >= 2:
            prior = self.cnt["gx"]
            self.qs["g"].append(lambda e, v=prior: e.wait_ge(sh["gx"], v))
        return self.op("g", lambda e, d=dst, s=src: e.dma_start(out=d, in_=s, max_dma_last_dim=8192), inc="gx", n=16)

    def emit_big_cast(self):
        if not self.big_casts:
            return
        which, j = self.big_casts.pop(0)
        w_bf, w_f = (self.w1bf, self.w1p) if which == "w1" else (self.w2bf, self.w2p)
        self.op("g", lambda e, d=w_bf, s=w_f, j=j:
                e.dma_start(out=d.ap()[j, :, :], in_=s[j, :, :], max_dma_last_dim=8192), inc="gc", n=16)
        if not self.big_casts:
            tok = ("gc", self.cnt["gc"])
            self.tok_w1bf = [tok] * DC
            self.tok_w2bf = [tok] * DC

    def ln_rows(self, sum_ap, sq_ap, stat_tok, rows_free_tok):
        op, need = self.op, self.need
        need("v", stat_tok, rows_free_tok)
        op("v", lambda e: e.tensor_scalar(self.rowm[0:1, :], sum_ap, 1.0 / D, None, ALU.mult, ALU.bypass))
        op("v", lambda e: e.tensor_scalar(self.rowv[0:1, :], sq_ap, 1.0 / D, None, ALU.mult, ALU.bypass))
        op("v", lambda e: e.tensor_tensor(self.rowt[0:1, :], self.rowm[0:1, :], self.rowm[0:1, :], ALU.mult))
        op("v", lambda e: e.tensor_tensor(self.rowv[0:1, :], self.rowv[0:1, :], self.rowt[0:1, :], ALU.subtract))
        tok_v = op("v", lambda e: e.tensor_scalar(self.rowv[0:1, :], self.rowv[0:1, :], EPS_LN, None, ALU.add, ALU.bypass), inc="v", fence=True)
        need("s", tok_v)
        op("s", lambda e: e.activation(self.rowt[0:1, :], self.rowv[0:1, :], AF.Ln, bias=0.0, scale=1.0))
        tok_a = op("s", lambda e: e.activation(self.rowa[0:1, :], self.rowt[0:1, :], AF.Exp, bias=0.0, scale=-0.5), inc="s", fence=True)
        need("v", tok_a)
        tok_b = op("v", lambda e: e.scalar_tensor_tensor(self.rowb[0:1, :], self.rowm[0:1, :], -1.0, self.rowa[0:1, :], ALU.mult, ALU.mult), inc="v", fence=True)
        return tok_b

    def bcast_ab(self, a_ps, b_ps, rows_tok, bank_free_tok):
        op, need = self.op, self.need
        need("t", rows_tok, bank_free_tok)
        op("t", lambda e: e.matmul(a_ps, self.ones_bf[0:1, 0:128], self.rowa[0:1, :], start=True, stop=True))
        return op("t", lambda e: e.matmul(b_ps, self.ones_bf[0:1, 0:128], self.rowb[0:1, :], start=True, stop=True), inc="p")

    def evac_ab(self, a_ps, b_ps, ab_sb, pe_tok, consumer_tok):
        op, need = self.op, self.need
        need("s", pe_tok, consumer_tok)
        op("s", lambda e: e.activation(ab_sb[:, 0:512], a_ps, AF.Copy, bias=0.0, scale=1.0))
        return op("s", lambda e: e.activation(ab_sb[:, 512:1024], b_ps, AF.Copy, bias=0.0, scale=1.0), inc="s", fence=True)

    def phase_a(self):
        op, need = self.op, self.need
        S = dict(stats_free=self.tok_setup_banks, ab_cons=None, rows_free=None,
                 kban_free=None, vban_free=None, eA_free=None, kv_last=None,
                 ksb_free=[None, None], vsb_free=[self.tok_vones, self.tok_vones])
        statq = [None]
        rows_pending = {}
        ln1_done = [None, None]
        ln1_free = [None, None]
        xa_free = [None, None]
        kv_started = [False, False]
        kv_pending = []

        def front_sums(t):
            par = t % 2
            xa = self.xa[par]
            need("t", self.tok_xa[t], S["stats_free"])
            for cc in range(DC):
                op("t", lambda e, cc=cc, xa=xa: e.matmul(
                    self.sban[0:1, :], self.ones_bf[:, 0:1], xa[:, cc * T:(cc + 1) * T],
                    start=(cc == 0), stop=(cc == DC - 1)),
                    inc=("p" if cc == DC - 1 else None))

        def front_sq(t):
            par = t % 2
            xa = self.xa[par]
            need("s", self.tok_xa[t], statq[0])
            tok_sq = None
            for cc in range(DC):
                tok_sq = self.op("s", lambda e, cc=cc, xa=xa: e.activation(
                    self.xsq[:, cc * T:(cc + 1) * T], xa[:, cc * T:(cc + 1) * T],
                    AF.Square, bias=0.0, scale=1.0),
                    inc=("s" if cc == DC - 1 else None), fence=(cc == DC - 1))
            xa_free[par] = [tok_sq]
            return tok_sq

        def front_tban(t, tok_sq):
            need("t", tok_sq)
            tok_stat = None
            for cc in range(DC):
                tok_stat = op("t", lambda e, cc=cc: e.matmul(
                    self.tban[0:1, :], self.ones_bf[:, 0:1], self.xsq[:, cc * T:(cc + 1) * T],
                    start=(cc == 0), stop=(cc == DC - 1)),
                    inc=("p" if cc == DC - 1 else None))
            statq[0] = tok_stat
            return tok_stat

        def front_rows(t, tok_stat):
            par = t % 2
            xa = self.xa[par]
            tok_x = self.tok_xa[t]
            rows_tok = self.ln_rows(self.sban[0:1, 0:512], self.tban[0:1, 0:512], tok_stat, S["rows_free"])
            tok_xr_last = None
            if t < NTB:
                op("v", lambda e, t=t: e.tensor_copy(self.a_row[0:1, t * T:(t + 1) * T], self.rowa[0:1, :]))
                rows_tok = op("v", lambda e, t=t: e.tensor_copy(self.b_row[0:1, t * T:(t + 1) * T], self.rowb[0:1, :]), inc="v", fence=True)
                need("v", tok_x)
                for cc in range(DC):
                    tok_xr_last = op("v", lambda e, t=t, cc=cc, xa=xa: e.tensor_copy(
                        self.xr[:, cc * NB + t * T: cc * NB + (t + 1) * T], xa[:, cc * T:(cc + 1) * T]),
                        inc=("v" if cc == DC - 1 else None), fence=(cc == DC - 1))
                xa_free[par].append(tok_xr_last)
            rows_pending[t] = rows_tok

        def front_bcast(t):
            par = t % 2
            xa = self.xa[par]
            ln1A = self.ln1A[par]
            tok_rf = self.bcast_ab(self.sban[:, :], self.tban[:, :], rows_pending.pop(t), None)
            S["rows_free"] = tok_rf
            tok_ab = self.evac_ab(self.sban[:, :], self.tban[:, :], self.ab_sbA, tok_rf, S["ab_cons"])
            S["stats_free"] = tok_ab
            need("g", tok_ab, ln1_free[par])
            tok_ln1 = None
            for cc in range(DC):
                op("g", lambda e, cc=cc, xa=xa: e.tensor_tensor(
                    self.ptmpA[:, :], xa[:, cc * T:(cc + 1) * T], self.ab_sbA[:, 0:512], ALU.mult), inc="g")
                tok_ln1 = op("g", lambda e, cc=cc, ln1A=ln1A: e.tensor_tensor(
                    ln1A[:, cc * T:(cc + 1) * T], self.ptmpA[:, :], self.ab_sbA[:, 512:1024], ALU.add), inc="g", fence=(cc == DC - 1))
            S["ab_cons"] = tok_ln1
            ln1_done[par] = tok_ln1
            xa_free[par].append(tok_ln1)

        def chunk_proj(t, sc):
            par = t % 2
            kpar = sc % 2
            ln1A = self.ln1A[par]
            need("t", ln1_done[par], S["kban_free"], self.tok_wk[-1])
            tok_kp = None
            for half in range(2):
                for cc in range(DC):
                    tok_kp = op("t", lambda e, sc=sc, cc=cc, half=half, ln1A=ln1A: e.matmul(
                        self.kban[:, half * 512:(half + 1) * 512],
                        ln1A[:, cc * T + sc * 128: cc * T + (sc + 1) * 128],
                        self.wkt_bf[:, cc * D + half * 512: cc * D + (half + 1) * 512],
                        start=(cc == 0), stop=(cc == DC - 1)),
                        inc=("p" if (half == 1 and cc == DC - 1) else None))
            need("t", S["vban_free"], self.tok_wv[-1])
            tok_vp = None
            for half in range(2):
                for cc in range(DC):
                    tok_vp = op("t", lambda e, sc=sc, cc=cc, half=half, ln1A=ln1A: e.matmul(
                        self.vban[:, half * 512:(half + 1) * 512],
                        ln1A[:, cc * T + sc * 128: cc * T + (sc + 1) * 128],
                        self.wvt_bf[:, cc * D + half * 512: cc * D + (half + 1) * 512],
                        start=(cc == 0), stop=(cc == DC - 1)),
                        inc=("p" if (half == 1 and cc == DC - 1) else None))
            need("v", tok_kp, S["ksb_free"][kpar])
            tok_kpre = op("v", lambda e: e.scalar_tensor_tensor(
                self.kpre[:, :], self.kban[:, :], 0.0, self.bkb[:, :], ALU.add, ALU.add), inc="v", fence=True)
            S["kban_free"] = tok_kpre
            tok_m0 = op("v", lambda e: e.tensor_scalar(
                self.m0_sb[:, :], self.kpre[:, :], 0.0, None, ALU.min, ALU.bypass), inc="v", fence=True)
            need("s", tok_m0, S["eA_free"])
            tok_e = self.op("s", lambda e: e.activation(self.e_sbA[:, :], self.m0_sb[:, :], AF.Exp, bias=0.0, scale=1.0), inc="s")
            need("v", tok_vp, S["vsb_free"][kpar])
            tok_v = None
            for half in range(2):
                src = self.vban[:, half * 512:(half + 1) * 512].rearrange("p (h m) -> p h m", m=64)
                bsl = self.bvb[:, half * 512:(half + 1) * 512].rearrange("p (h m) -> p h m", m=64)
                dst = self.v_sb[kpar][:, half * 520:(half + 1) * 520].rearrange("p (h m) -> p h m", m=65)[:, :, 0:64]
                tok_v = op("v", lambda e, s=src, b=bsl, d=dst: e.scalar_tensor_tensor(
                    d, s, 0.0, b, ALU.add, ALU.add), inc=("v" if half == 1 else None), fence=(half == 1))
            S["vban_free"] = tok_v
            need("v", tok_e)
            tok_k = op("v", lambda e, kpar=kpar: e.scalar_tensor_tensor(
                self.k_sb[kpar][:, :], self.kpre[:, :], 0.0, self.e_sbA[:, :], ALU.max, ALU.add), inc="v", fence=True)
            S["eA_free"] = tok_k
            if t == 3 and sc == 0:
                tap_k = self.tap("ksb", self.k_sb[kpar][:, :], tok_k, [128, D], BF16)
                tap_v = self.tap("vsb", self.v_sb[kpar][:, :], tok_v, [128, H * 65], BF16)
            elif t == 0 and sc == 0:
                tap_k = self.tap("ksb0", self.k_sb[kpar][:, :], tok_k, [128, D], BF16)
                tap_v = self.tap("vsb0", self.v_sb[kpar][:, :], tok_v, [128, H * 65], BF16)
            else:
                tap_k = tap_v = None
            kv_pending.append((t, sc, tok_k, tok_v, tap_k, tap_v))
            if sc == SCH - 1:
                ln1_free[par] = [tok_vp, tap_ln1A if t == 0 else None]

        def kv_emit():
            t, sc, tok_k, tok_v, tap_k, tap_v = kv_pending.pop(0)
            kpar = sc % 2
            need("t", tok_k, tok_v)
            for h in range(H):
                q = h // 2
                bank = self.kvA if q < 4 else self.kvB
                qq = q % 4
                side = 0 if q < 4 else 1
                first = not kv_started[side]
                kv_started[side] = True
                last = (t == NTA - 1 and sc == SCH - 1 and h == (7 if side == 0 else 15))
                S["kv_last"] = op("t", lambda e, h=h, qq=qq, bank=bank, first=first, last=last, kpar=kpar: e.matmul(
                    bank[(h % 2) * 64:(h % 2) * 64 + 64, qq * 65: qq * 65 + 65],
                    self.k_sb[kpar][:, h * DH:(h + 1) * DH],
                    self.v_sb[kpar][:, h * 65: h * 65 + 65],
                    start=first, stop=last), inc=("p" if h == H - 1 else None))
            S["ksb_free"][kpar] = [S["kv_last"], tap_k]
            S["vsb_free"][kpar] = [S["kv_last"], tap_v]

        front_sums(0)
        tok_sq0 = front_sq(0)
        tok_stat0 = front_tban(0, tok_sq0)
        front_rows(0, tok_stat0)
        front_bcast(0)
        tap_ln1A = self.tap("ln1A", self.ln1A[0][:, :], ln1_done[0], [128, DC * T], BF16)

        for t in range(NTA):
            par = t % 2
            if t + 2 < NTA:
                need("g", xa_free[par])
                self.tok_xa.append(self.load_xa(t + 2))
            self.emit_big_cast()
            nxt = t + 1 < NTA
            if nxt:
                front_sums(t + 1)
                tok_sq = front_sq(t + 1)
            chunk_proj(t, 0)
            while len(kv_pending) > 1:
                kv_emit()
            if nxt:
                tok_stat = front_tban(t + 1, tok_sq)
                front_rows(t + 1, tok_stat)
            chunk_proj(t, 1)
            kv_emit()
            if nxt:
                front_bcast(t + 1)
            chunk_proj(t, 2)
            kv_emit()
            chunk_proj(t, 3)
            kv_emit()
        kv_emit()
        return dict(kv=S["kv_last"], kban=S["kban_free"], vban=S["vban_free"],
                    stats=S["stats_free"], rows=S["rows_free"], ab_cons=S["ab_cons"])

    def exchange(self, st):
        op, need = self.op, self.need
        need("v", st["kv"])
        op("v", lambda e: e.tensor_copy(self.kv_acc[:, 0:260], self.kvA[:, 0:260]), inc="v")
        tok_evac = op("v", lambda e: e.tensor_copy(self.kv_acc[:, 260:520], self.kvB[:, 0:260]), inc="v", fence=True)

        need("g", tok_evac, *self.tok_excfg)
        sh = self.semh

        def gseq(e):
            with e.register("myslot") as myslot, e.register("peerslot") as peerslot:
                e.load(myslot, self.side_sb[0:1, 0:1])
                e.load(peerslot, self.side_sb[0:1, 1:2])
                with e.If_eq(myslot, 0):
                    e.dma_start(out=self.shpay.ap()[0, :, :], in_=self.kv_acc[:, :]).then_inc(sh["ge"], 16)
                with e.Else():
                    e.dma_start(out=self.shpay.ap()[1, :, :], in_=self.kv_acc[:, :]).then_inc(sh["ge"], 16)
                e.wait_ge(sh["ge"], 16)
                with e.If_eq(myslot, 0):
                    e.dma_start(out=self.shflag.ap()[0:1, 0:2], in_=self.nonce_sb[0:1, 0:2]).then_inc(sh["ge"], 16)
                with e.Else():
                    e.dma_start(out=self.shflag.ap()[1:2, 0:2], in_=self.nonce_sb[0:1, 0:2]).then_inc(sh["ge"], 16)
                e.wait_ge(sh["ge"], 32)
                with (
                    e.register("f0") as f0,
                    e.register("f1") as f1,
                    e.register("n0") as n0,
                    e.register("n1") as n1,
                    e.register("bad") as bad,
                ):
                    e.load(n0, self.nonce_sb[0:1, 0:1])
                    e.load(n1, self.nonce_sb[0:1, 1:2])
                    e.reg_mov(bad, 1)
                    with e.While(bad):
                        with e.If_eq(peerslot, 0):
                            e.load(f0, self.shflag.ap()[0:1, 0:1])
                            e.load(f1, self.shflag.ap()[0:1, 1:2])
                        with e.Else():
                            e.load(f0, self.shflag.ap()[1:2, 0:1])
                            e.load(f1, self.shflag.ap()[1:2, 1:2])
                        e.reg_sub(f0, f0, n0)
                        e.reg_sub(f1, f1, n1)
                        e.reg_alu(bad, f0, f1, ALU.bitwise_or)
                with e.If_eq(peerslot, 0):
                    e.dma_start(out=self.kv_peer[:, :], in_=self.shpay.ap()[0, :, :]).then_inc(sh["ge"], 16)
                with e.Else():
                    e.dma_start(out=self.kv_peer[:, :], in_=self.shpay.ap()[1, :, :]).then_inc(sh["ge"], 16)
                e.wait_ge(sh["ge"], 48)

        op("g", lambda e: e.dma_start(out=self.wqt_bf[:, :], in_=self.wqt, max_dma_last_dim=8192), inc="gq", n=16)
        op("g", lambda e: e.dma_start(out=self.wot_bf[:, :], in_=self.wot, max_dma_last_dim=8192), inc="gq", n=16)
        self.tok_wq = ("gq", self.cnt["gq"])
        self.tok_wo = ("gq", self.cnt["gq"])
        self.qs["g"].append(gseq)
        self.cnt["ge"] = 48
        tok_peer = ("ge", 48)

        tap_kvown = self.tap("kvown", self.kv_acc[:, :], tok_evac, [128, 520], F32)
        self.tap("arow", self.a_row[0:1, :], tok_evac, [1, NB], BF16)
        self.tap("brow", self.b_row[0:1, :], tok_evac, [1, NB], BF16)
        need("v", tok_peer, tap_kvown)
        op("v", lambda e: e.tensor_tensor(self.kv_acc[:, :], self.kv_acc[:, :], self.kv_peer[:, :], ALU.add), inc="v")
        self.tok_kvbf = op("v", lambda e: e.tensor_copy(self.kv_bf[:, :], self.kv_acc[:, :]), inc="v", fence=True)
        op("v", lambda e: e.memset(self.bd_bf[:, :], 0.0), inc="v")
        self.vfence()
        tok_bd = None
        for h in range(H):
            cc = h // 2
            r0 = (h % 2) * 64
            tok_bd = op("v", lambda e, h=h, cc=cc, r0=r0: e.tensor_copy(
                self.bd_bf[r0:r0 + 64, cc * 16 + h: cc * 16 + h + 1],
                self.kv_acc[r0:r0 + 64, cc * 65 + 64: cc * 65 + 65]), inc="v", fence=(h == H - 1))
        self.tap("kvbf", self.kv_bf[:, :], self.tok_kvbf, [128, 520], BF16)
        self.tap("bdbf", self.bd_bf[:, :], tok_bd, [128, 128], BF16)
        st["bd"] = tok_bd
        st["kv_evac"] = tok_evac
        return st

    def phase_b(self, st):
        op, need = self.op, self.need
        tok_sban_free = st["stats"]
        tok_rows_free = st["rows"]
        tok_ab_cons = st["ab_cons"]
        tok_kban_free = [st["kban"], st["kban"]]
        tok_vban_free = st["vban"]
        tok_attn_bank_free = st["kv_evac"]
        tok_qraw_free = None
        tok_qz_free = None
        tok_attn_sb_free = None
        tok_y_free = None
        tok_x1 = None
        ln1B_done = [None, None]
        ln1B_free = [None, None]
        ab2_cons = [None]
        self.tap_ln1B = None

        def ln1_front(t, bank_free):
            par = t % 2
            need("t", bank_free)
            op("t", lambda e, t=t: e.matmul(self.sban[:, :], self.ones_bf[0:1, 0:128],
                                            self.a_row[0:1, t * T:(t + 1) * T], start=True, stop=True), inc="p")
            pe_ab = op("t", lambda e, t=t: e.matmul(self.tban[:, :], self.ones_bf[0:1, 0:128],
                                                    self.b_row[0:1, t * T:(t + 1) * T], start=True, stop=True), inc="p")
            tok_ab = self.evac_ab(self.sban[:, :], self.tban[:, :], self.ab_sbB2, pe_ab, ab2_cons[0])
            need("g", tok_ab, ln1B_free[par])
            tok_ln1 = None
            for cc in range(DC):
                op("g", lambda e, cc=cc, t=t: e.tensor_tensor(
                    self.ptmpB[:, :], self.xr[:, cc * NB + t * T: cc * NB + (t + 1) * T],
                    self.ab_sbB2[:, 0:512], ALU.mult))
                tok_ln1 = op("g", lambda e, cc=cc, par=par: e.tensor_tensor(
                    self.ln1B[par][:, cc * T:(cc + 1) * T], self.ptmpB[:, :], self.ab_sbB2[:, 512:1024], ALU.add),
                    inc=("g" if cc == DC - 1 else None), fence=(cc == DC - 1))
            ab2_cons[0] = tok_ln1
            ln1B_done[par] = tok_ln1
            if t == 0:
                self.tap_ln1B = self.tap("ln1B", self.ln1B[0][:, :], tok_ln1, [128, DC * T], BF16)
            return tok_ab

        tok_sban_free = ln1_front(0, tok_sban_free)

        for t in range(NTB):
            par = t % 2
            tok_ln1 = ln1B_done[par]
            ln1B = self.ln1B[par]
            tok_q = None
            tok_q_half = [None, None]
            for cc in range(DC):
                half = cc % 2
                need("t", self.tok_wq, tok_ln1, tok_kban_free[half], tok_qraw_free)
                tok_qp = None
                for dd in range(DC):
                    tok_qp = op("t", lambda e, cc=cc, dd=dd, half=half, ln1B=ln1B: e.matmul(
                        self.kban[:, half * 512:(half + 1) * 512],
                        self.wqt_bf[:, dd * D + cc * 128: dd * D + cc * 128 + 128],
                        ln1B[:, dd * T:(dd + 1) * T],
                        start=(dd == 0), stop=(dd == DC - 1)),
                        inc=("p" if dd == DC - 1 else None))
                need("s", tok_qp, tok_q_half[half])
                ps = self.kban[:, half * 512:(half + 1) * 512]
                self.op("s", lambda e, ps=ps, cc=cc: e.activation(
                    self.relu1[:, :], ps, AF.Relu, bias=self.nbqc[:, cc:cc + 1], scale=-1.0))
                self.op("s", lambda e, half=half: e.activation(
                    self.e_sbB[half][:, :], self.relu1[:, :], AF.Exp, bias=0.0, scale=-1.0))
                tok_p2 = self.op("s", lambda e, ps=ps, cc=cc, half=half: e.activation(
                    self.p2_sb[half][:, :], ps, AF.Relu, bias=self.bqc[:, cc:cc + 1], scale=1.0), inc="s")
                tok_kban_free[half] = tok_p2
                need("v", tok_p2)
                tok_q = op("v", lambda e, cc=cc, half=half: e.tensor_tensor(
                    self.qraw[:, cc * T:(cc + 1) * T], self.p2_sb[half][:, :], self.e_sbB[half][:, :], ALU.add), inc="v", fence=True)
                tok_q_half[half] = tok_q
            tap_qraw = self.tap("qraw", self.qraw[:, :], tok_q, [128, DC * T], BF16) if t == 0 else None
            ln1B_free[par] = [tok_q, self.tap_ln1B if t == 0 else None]
            if t + 1 < NTB:
                tok_sban_free = ln1_front(t + 1, tok_sban_free)

            need("t", st["bd"], tok_q, tok_vban_free)
            tok_den = None
            for cc in range(DC):
                tok_den = op("t", lambda e, cc=cc: e.matmul(
                    self.vban[0:16, 0:512], self.bd_bf[:, cc * 16:(cc + 1) * 16],
                    self.qraw[:, cc * T:(cc + 1) * T], start=(cc == 0), stop=(cc == DC - 1)),
                    inc=("p" if cc == DC - 1 else None))
            need("v", tok_den)
            op("v", lambda e: e.tensor_scalar(self.zden[0:16, :], self.vban[0:16, 0:512], EPS_Z, None, ALU.add, ALU.bypass))
            op("v", lambda e: e.reciprocal(self.zbuf[0:16, :], self.zden[0:16, :]), inc="v")
            self.vfence()
            tok_z = op("v", lambda e: e.tensor_copy(self.zbf[0:16, :], self.zbuf[0:16, :]), inc="v", fence=True)
            tap_z = self.tap("zbf", self.zbf[0:16, :], tok_z, [16, T], BF16) if t == 0 else None
            tok_zb_free = [tok_z, tap_z]

            tok_ae = None
            for cc in range(DC):
                need("t", tok_z, tok_zb_free, tok_qz_free)
                tok_zb = op("t", lambda e, cc=cc: e.matmul(
                    self.vban[:, 512:1024], self.sel_bf[0:16, cc * 128:(cc + 1) * 128],
                    self.zbf[0:16, :], start=True, stop=True), inc="p")
                need("v", tok_zb)
                tok_qz = op("v", lambda e, cc=cc: e.tensor_tensor(
                    self.qz[:, cc * T:(cc + 1) * T], self.qraw[:, cc * T:(cc + 1) * T],
                    self.vban[:, 512:1024], ALU.mult), inc="v", fence=True)
                tok_zb_free = tok_qz
                need("t", tok_qz, self.tok_kvbf, tok_attn_bank_free)
                op("t", lambda e, cc=cc: e.matmul(
                    self.kvA[0:64, 0:512], self.kv_bf[0:64, cc * 65: cc * 65 + 64],
                    self.qz[0:64, cc * T:(cc + 1) * T], start=True, stop=True))
                tok_at = op("t", lambda e, cc=cc: e.matmul(
                    self.kvB[0:64, 0:512], self.kv_bf[64:128, cc * 65: cc * 65 + 64],
                    self.qz[64:128, cc * T:(cc + 1) * T], start=True, stop=True), inc="p")
                need("s", tok_at, tok_attn_sb_free)
                self.op("s", lambda e, cc=cc: e.activation(
                    self.attn_sb[0:64, cc * T:(cc + 1) * T], self.kvA[0:64, 0:512], AF.Copy, bias=0.0, scale=1.0))
                tok_ae = self.op("s", lambda e, cc=cc: e.activation(
                    self.attn_sb[64:128, cc * T:(cc + 1) * T], self.kvB[0:64, 0:512], AF.Copy, bias=0.0, scale=1.0), inc="s", fence=True)
                tok_attn_bank_free = tok_ae
            tap_attn = self.tap("attnsb", self.attn_sb[:, :], tok_ae, [128, DC * T], BF16) if t == 0 else None
            tok_qz_free = tok_ae
            tok_qraw_free = [tok_ae, tap_qraw]

            tok_ye = None
            ysq_q = [None, None]
            tok_stats_last = None
            tok_woj = None
            for j in range(DC):
                bank = (self.sban, self.tban)[j % 2]
                need("t", self.tok_wo, tok_ae, tok_sban_free)
                for cc in range(DC):
                    tok_woj = op("t", lambda e, j=j, cc=cc, bank=bank: e.matmul(
                        bank[:, :], self.wot_bf[:, cc * D + j * 128: cc * D + j * 128 + 128],
                        self.attn_sb[:, cc * T:(cc + 1) * T],
                        start=(cc == 0), stop=(cc == DC - 1)),
                        inc=("p" if cc == DC - 1 else None))
                need("s", tok_woj, tok_y_free)
                tok_ye = self.op("s", lambda e, j=j, bank=bank: e.activation(
                    self.y_sb[:, j * T:(j + 1) * T], bank[:, :], AF.Identity, bias=self.boc[:, j:j + 1], scale=1.0), inc="s", fence=True)
                tok_sban_free = tok_ye
                need("t", tok_ye)
                op("t", lambda e, j=j: e.matmul(
                    self.vban[0:1, 0:512], self.ones_bf[:, 0:1], self.y_sb[:, j * T:(j + 1) * T],
                    start=(j == 0), stop=(j == DC - 1)), inc="p")
                yq = self.ysq[j % 2]
                need("v", tok_ye, ysq_q[j % 2])
                tok_yq = op("v", lambda e, j=j, yq=yq: e.tensor_tensor(
                    yq[:, :], self.y_sb[:, j * T:(j + 1) * T], self.y_sb[:, j * T:(j + 1) * T], ALU.mult), inc="v", fence=True)
                need("t", tok_yq)
                tok_stats_last = op("t", lambda e, j=j, yq=yq: e.matmul(
                    self.vban[0:1, 512:1024], self.ones_bf[:, 0:1], yq[:, :],
                    start=(j == 0), stop=(j == DC - 1)), inc="p")
                ysq_q[j % 2] = tok_stats_last
            rows_tok = self.ln_rows(self.vban[0:1, 0:512], self.vban[0:1, 512:1024], tok_stats_last, tok_rows_free)
            tok_rows_free = self.bcast_ab(self.vban[:, 0:512], self.vban[:, 512:1024], rows_tok, None)
            tok_ab2 = self.evac_ab(self.vban[:, 0:512], self.vban[:, 512:1024], self.ab_sbB, tok_rows_free, tok_ab_cons)
            tok_vban_free = tok_ab2
            need("v", tok_ab2, tok_ye)
            for cc in range(DC):
                op("v", lambda e, cc=cc: e.tensor_tensor(
                    self.t1_sbB[:, :], self.y_sb[:, cc * T:(cc + 1) * T], self.ab_sbB[:, 0:512], ALU.mult))
                op("v", lambda e: e.tensor_tensor(
                    self.t2_sbB[:, :], self.t1_sbB[:, :], self.ab_sbB[:, 512:1024], ALU.add))
                tok_x1 = op("v", lambda e, cc=cc, t=t: e.tensor_tensor(
                    self.xr[:, cc * NB + t * T: cc * NB + (t + 1) * T],
                    self.xr[:, cc * NB + t * T: cc * NB + (t + 1) * T], self.t2_sbB[:, :], ALU.add),
                    inc=("v" if cc == DC - 1 else None), fence=(cc == DC - 1))
            tap_y = self.tap("ysb", self.y_sb[:, :], tok_ye, [128, DC * T], BF16) if t == 0 else None
            tok_ab_cons = tok_x1
            tok_y_free = [tok_x1, tap_y]
            tok_attn_sb_free = [tok_woj, tap_attn]
        self.tap("xr", self.xr[:, :], tok_x1, [128, DC * NB], BF16)
        return dict(b_done=tok_x1, sban=tok_sban_free, vban=tok_vban_free,
                    kban=tok_kban_free, rows=tok_rows_free, ab_cons=tok_ab_cons,
                    ln3=st["kv_evac"], attn_bank=tok_attn_bank_free)

    def phase_c(self, st):
        op, need = self.op, self.need
        tok_bdone = st["b_done"]
        tok_sban_free = st["sban"]
        tok_rows_free = st["rows"]
        tok_ab_cons = st["ab_cons"]
        tok_kban_free = list(st["kban"])
        tok_vban_free = [st["vban"], st["vban"]]
        tok_ln3_bank_free = st["attn_bank"]
        tok_ln2_free = None
        tok_fsq_free = None
        tok_hid_w2_done = None
        tok_fout_cons = None
        tok_outs_free = [None, None]
        tok_sdcd_free = [None, None]
        tok_out_last = None

        sh = self.semh

        def load_w1(fb, par):
            need("y", self.tok_w1bf[fb], tok_bdone)
            prior = self.cnt["yw"]
            if prior >= 16:
                self.qs["y"].append(lambda e, v=prior: e.wait_ge(sh["yw"], v))
            return self.op("y", lambda e, fb=fb, par=par: e.dma_start(
                out=self.w1b[par][:, :], in_=self.w1bf.ap()[fb, :, :]), inc="yw", n=16)

        def load_w2(j, par):
            need("y", self.tok_w2bf[j], tok_bdone)
            prior = self.cnt["yw2"]
            if prior >= 16:
                self.qs["y"].append(lambda e, v=prior: e.wait_ge(sh["yw2"], v))
            return self.op("y", lambda e, j=j, par=par: e.dma_start(
                out=self.w2b[par][:, :], in_=self.w2bf.ap()[j, :, :]), inc="yw2", n=16)

        tok_w1 = [load_w1(0, 0), load_w1(1, 1)]
        tok_w2 = [load_w2(0, 0), load_w2(1, 1)]

        fsq_q = [None, None]
        ln2_done = [None]
        ab2C_cons = [None]
        sbanC = [tok_sban_free]

        def ln2_front(t, w1_gate):
            nonlocal tok_rows_free
            need("t", sbanC[0], tok_bdone)
            for cc in range(DC):
                op("t", lambda e, cc=cc, t=t: e.matmul(
                    self.sban[0:1, :], self.ones_bf[:, 0:1],
                    self.xr[:, cc * NB + t * T: cc * NB + (t + 1) * T],
                    start=(cc == 0), stop=(cc == DC - 1)),
                    inc=("p" if cc == DC - 1 else None))
            tok_stat = None
            for cc in range(DC):
                fq = self.fsq[cc % 2]
                need("v", tok_bdone, fsq_q[cc % 2])
                tok_fq = op("v", lambda e, cc=cc, t=t, fq=fq: e.tensor_tensor(
                    fq[:, :], self.xr[:, cc * NB + t * T: cc * NB + (t + 1) * T],
                    self.xr[:, cc * NB + t * T: cc * NB + (t + 1) * T], ALU.mult), inc="v", fence=True)
                need("t", tok_fq)
                tok_stat = op("t", lambda e, cc=cc, fq=fq: e.matmul(
                    self.tban[0:1, :], self.ones_bf[:, 0:1], fq[:, :],
                    start=(cc == 0), stop=(cc == DC - 1)), inc="p")
                fsq_q[cc % 2] = tok_stat
            rows_tok = self.ln_rows(self.sban[0:1, 0:512], self.tban[0:1, 0:512], tok_stat, tok_rows_free)
            tok_rows_free = self.bcast_ab(self.sban[:, :], self.tban[:, :], rows_tok, None)
            tok_ab = self.evac_ab(self.sban[:, :], self.tban[:, :], self.ab_sbC2, tok_rows_free, ab2C_cons[0])
            sbanC[0] = tok_ab
            need("g", tok_ab, w1_gate)
            tok_ln2 = None
            for cc in range(DC):
                op("g", lambda e, cc=cc, t=t: e.tensor_tensor(
                    self.ptmpC[:, :], self.xr[:, cc * NB + t * T: cc * NB + (t + 1) * T],
                    self.ab_sbC2[:, 0:512], ALU.mult))
                tok_ln2 = op("g", lambda e, cc=cc: e.tensor_tensor(
                    self.ln2C[:, cc * T:(cc + 1) * T], self.ptmpC[:, :], self.ab_sbC2[:, 512:1024], ALU.add),
                    inc=("g" if cc == DC - 1 else None), fence=(cc == DC - 1))
            ab2C_cons[0] = tok_ln2
            ln2_done[0] = tok_ln2

        ln2_front(0, None)

        for t in range(NTB):
            tok_ln2 = ln2_done[0]
            tap_ln2 = self.tap("ln2C", self.ln2C[:, :], tok_ln2, [128, DC * T], BF16) if t == 0 else None

            tok_hid = None
            tap_j30 = None
            for j in range(FC):
                fb = j // 4
                par = fb % 2
                half = j % 2
                if j % 4 == 0:
                    need("t", tok_w1[par])
                need("t", tok_ln2, tok_kban_free[half])
                tok_w1m = None
                for cc in range(DC):
                    tok_w1m = op("t", lambda e, j=j, cc=cc, half=half, par=par: e.matmul(
                        self.kban[:, half * 512:(half + 1) * 512],
                        self.w1b[par][:, cc * 512 + (j % 4) * 128: cc * 512 + (j % 4) * 128 + 128],
                        self.ln2C[:, cc * T:(cc + 1) * T],
                        start=(cc == 0), stop=(cc == DC - 1)),
                        inc=("p" if cc == DC - 1 else None))
                if j % 4 == 3:
                    nxt = fb + 2
                    if nxt < DC:
                        need("y", tok_w1m)
                        tok_w1[nxt % 2] = load_w1(nxt, nxt % 2)
                        if t == 0 and nxt == 7:
                            self.tap("w1b1fb7", self.w1b[1][:, :], tok_w1[1], [128, FF], BF16)
                    elif t + 1 < NTB and nxt - DC < 2:
                        need("y", tok_w1m)
                        tok_w1[nxt % 2] = load_w1(nxt - DC, nxt % 2)
                ps = self.kban[:, half * 512:(half + 1) * 512]
                sp = j % 2
                need("s", tok_w1m, tok_sdcd_free[sp])
                self.op("s", lambda e, ps=ps, j=j, sp=sp: e.activation(
                    self.sd_sb[sp][:, :], ps, AF.Sin, bias=self.b1s1[:, j:j + 1], scale=0.1))
                self.op("s", lambda e, ps=ps, j=j, sp=sp: e.activation(
                    self.cd_sb[sp][:, :], ps, AF.Sin, bias=self.b1s2[:, j:j + 1], scale=0.1))
                tok_h = self.op("s", lambda e, ps=ps, j=j, sp=sp: e.activation(
                    self.hbf_sb[sp][:, :], ps, AF.Identity, bias=self.b1c[:, j:j + 1], scale=1.0), inc="s", fence=True)
                tok_kban_free[half] = tok_h
                if t == 0 and j >= 28:
                    tap_j30 = [
                        self.tap("hbf%d" % j, self.hbf_sb[sp][:, :], tok_h, [128, T], BF16),
                        self.tap("sd%d" % j, self.sd_sb[sp][:, :], tok_h, [128, T], BF16),
                    ]
                else:
                    tap_j30 = None
                need("v", tok_h, tok_hid_w2_done if j < 2 else None)
                op("v", lambda e, j=j, sp=sp: e.scalar_tensor_tensor(
                    self.t12_sb[sp][:, :], self.sd_sb[sp][:, :], self.qB[:, j:j + 1], self.hbf_sb[sp][:, :], ALU.mult, ALU.add))
                tok_u = op("v", lambda e, j=j, sp=sp: e.scalar_tensor_tensor(
                    self.u_sb[sp][:, :], self.cd_sb[sp][:, :], self.qA[:, j:j + 1], self.t12_sb[sp][:, :], ALU.mult, ALU.add), inc="v")
                tok_sdcd_free[sp] = [tok_u, tap_j30]
                tok_hid = op("v", lambda e, j=j, sp=sp: e.tensor_scalar(
                    self.hidden[:, j * T:(j + 1) * T], self.u_sb[sp][:, :], 0.0, None, ALU.max, ALU.bypass), inc="v", fence=True)
            tap_hid = self.tap("hidden", self.hidden[:, :], tok_hid, [128, FC * T], BF16) if t == 0 else None
            if t + 1 < NTB:
                ln2_front(t + 1, [tok_w1m, tap_ln2])

            tok_stats_last = None
            tok_fe = None
            tok_w2m = None
            for j in range(DC):
                par = j % 2
                need("t", tok_w2[par], tok_hid, tok_vban_free[par])
                for i in range(FC):
                    tok_w2m = op("t", lambda e, j=j, i=i, par=par: e.matmul(
                        self.vban[:, par * 512:(par + 1) * 512],
                        self.w2b[par][:, i * 128:(i + 1) * 128],
                        self.hidden[:, i * T:(i + 1) * T],
                        start=(i == 0), stop=(i == FC - 1)),
                        inc=("p" if i == FC - 1 else None))
                nxt = j + 2
                if nxt < DC:
                    need("y", tok_w2m)
                    tok_w2[nxt % 2] = load_w2(nxt, nxt % 2)
                elif t + 1 < NTB and nxt - DC < 2:
                    need("y", tok_w2m)
                    tok_w2[nxt % 2] = load_w2(nxt - DC, nxt % 2)
                need("s", tok_w2m, tok_fout_cons if j < 2 else None)
                tok_fe = self.op("s", lambda e, j=j, par=par: e.activation(
                    self.fout[:, j * T:(j + 1) * T], self.vban[:, par * 512:(par + 1) * 512],
                    AF.Identity, bias=self.b2c[:, j:j + 1], scale=1.0), inc="s", fence=True)
                tok_vban_free[par] = tok_fe
                need("t", tok_fe, tok_ln3_bank_free if j == 0 else None)
                op("t", lambda e, j=j: e.matmul(
                    self.kvA[0:1, :], self.ones_bf[:, 0:1], self.fout[:, j * T:(j + 1) * T],
                    start=(j == 0), stop=(j == DC - 1)), inc="p")
                fq = self.fsq[j % 2]
                need("v", tok_fe, fsq_q[j % 2])
                tok_fq = op("v", lambda e, j=j, fq=fq: e.tensor_tensor(
                    fq[:, :], self.fout[:, j * T:(j + 1) * T], self.fout[:, j * T:(j + 1) * T], ALU.mult), inc="v")
                need("t", tok_fq)
                tok_stats_last = op("t", lambda e, j=j, fq=fq: e.matmul(
                    self.kvB[0:1, :], self.ones_bf[:, 0:1], fq[:, :],
                    start=(j == 0), stop=(j == DC - 1)), inc="p")
                fsq_q[j % 2] = tok_stats_last
            tap_fout = self.tap("fout", self.fout[:, :], tok_fe, [128, DC * T], BF16) if t == 0 else None
            tok_hid_w2_done = [tok_w2m, tap_hid]
            rows_tok = self.ln_rows(self.kvA[0:1, 0:512], self.kvB[0:1, 0:512], tok_stats_last, tok_rows_free)
            tok_rows_free = self.bcast_ab(self.kvA[:, :], self.kvB[:, :], rows_tok, None)
            tok_ab3 = self.evac_ab(self.kvA[:, :], self.kvB[:, :], self.ab_sbC, tok_rows_free, tok_ab_cons)
            tok_ln3_bank_free = tok_ab3
            tok_ab_cons = None
            need("v", tok_ab3, tok_fe)
            tok_of = None
            for cc in range(DC):
                par = cc % 2
                need("v", tok_outs_free[par])
                op("v", lambda e, cc=cc: e.tensor_tensor(
                    self.t1_sbC[:, :], self.fout[:, cc * T:(cc + 1) * T], self.ab_sbC[:, 0:512], ALU.mult))
                op("v", lambda e: e.tensor_tensor(
                    self.t2f_sb[:, :], self.t1_sbC[:, :], self.ab_sbC[:, 512:1024], ALU.add))
                tok_of = op("v", lambda e, cc=cc, t=t, par=par: e.tensor_tensor(
                    self.outs[par][:, :], self.xr[:, cc * NB + t * T: cc * NB + (t + 1) * T],
                    self.t2f_sb[:, :], ALU.add), inc="v", fence=True)
                need("y", tok_of)
                prior = self.cnt["yo"]
                if prior >= 16:
                    self.qs["y"].append(lambda e, v=prior: self.semh and e.wait_ge(self.semh["yo"], v))
                tok_out_last = self.op("y", lambda e, cc=cc, t=t, par=par: e.dma_start(
                    out=self.out_d[cc * 128:(cc + 1) * 128, t * T:(t + 1) * T], in_=self.outs[par][:, :]), inc="yo", n=16)
                tok_outs_free[par] = tok_out_last
            tok_fout_cons = [tok_of, tap_fout]
            tok_fsq_free = tok_stats_last
            tok_ab_cons = tok_of

        need("y", tok_out_last)


_BUILT = None


def _get_built():
    global _BUILT
    if _BUILT is None:
        kb = KB()
        nc = kb.build()
        _BUILT = (kb, nc)
    return _BUILT


def _prepT(W):
    return np.ascontiguousarray(W.T.reshape(DC, 128, D).transpose(1, 0, 2).reshape(128, DC * D))


def kernel(**inputs):
    x = np.asarray(inputs["L0_attn_hidden_states"], np.float32)
    Wq = np.asarray(inputs["L0_attn_Wq"], np.float32)
    bq = np.asarray(inputs["L0_attn_bq"], np.float32)
    Wk = np.asarray(inputs["L0_attn_Wk"], np.float32)
    bk = np.asarray(inputs["L0_attn_bk"], np.float32)
    Wv = np.asarray(inputs["L0_attn_Wv"], np.float32)
    bv = np.asarray(inputs["L0_attn_bv"], np.float32)
    Wo = np.asarray(inputs["L0_attn_Wo"], np.float32)
    bo = np.asarray(inputs["L0_attn_bo"], np.float32)
    W1 = np.asarray(inputs["L0_ffn_W1"], np.float32)
    b1 = np.asarray(inputs["L0_ffn_b1"], np.float32)
    th = np.asarray(inputs["L0_ffn_quantum_phases"], np.float32)
    wr = np.asarray(inputs["L0_ffn_quantum_weights_real"], np.float32)
    wi = np.asarray(inputs["L0_ffn_quantum_weights_imag"], np.float32)
    W2 = np.asarray(inputs["L0_ffn_W2"], np.float32)
    b2 = np.asarray(inputs["L0_ffn_b2"], np.float32)

    wqt, wkt, wvt, wot = _prepT(Wq), _prepT(Wk), _prepT(Wv), _prepT(Wo)
    w1p = np.ascontiguousarray(W1.reshape(DC, 128, 8, 512).transpose(2, 1, 0, 3).reshape(DC, 128, FF))
    w2p = np.ascontiguousarray(W2.reshape(FC, 128, DC, 128).transpose(2, 1, 0, 3).reshape(DC, 128, FF))

    cols = lambda vec, n: np.ascontiguousarray(vec.reshape(n, 128).T)
    sel = np.zeros((16, DC * 128), np.float32)
    for cc in range(DC):
        sel[2 * cc, cc * 128: cc * 128 + 64] = 1.0
        sel[2 * cc + 1, cc * 128 + 64: cc * 128 + 128] = 1.0
    nonce = np.random.randint(1, 2**31 - 1, size=(1, 2)).astype(np.int32)

    kb, nc = _get_built()
    in_maps = []
    for c in range(8):
        b = c // 4
        sA = (c % 2) * NA
        sB = sA + ((c % 4) // 2) * NB
        xT = np.ascontiguousarray(x[b].T)
        rest = sA + NB if sB == sA else sA
        xA = np.concatenate([xT[:, sB:sB + NB], xT[:, rest:rest + NB]], axis=1)
        in_maps.append(dict(
            xA=np.ascontiguousarray(xA),
            wqt=wqt, wkt=wkt, wvt=wvt, wot=wot, w1p=w1p, w2p=w2p,
            bkr=bk.reshape(1, D), bvr=bv.reshape(1, D),
            bqc_i=cols(bq, DC), boc_i=cols(bo, DC), b1c_i=cols(b1, FC), b2c_i=cols(b2, DC),
            th_i=cols(th, FC), wr_i=cols(wr, FC), wi_i=cols(wi, FC),
            ones_i=np.ones((128, 128), np.float32),
            sel_i=sel,
            nonce_i=nonce,
            side_i=np.array([[c & 1, 1 - (c & 1)]], np.int32),
        ))
    res = run_bass_kernel_spmd(nc, in_maps, core_ids=list(range(8)))
    out = np.empty((2, 16384, D), np.float32)
    for c in range(8):
        b = c // 4
        sB = (c % 2) * NA + ((c % 4) // 2) * NB
        out[b, sB:sB + NB, :] = res.results[c]["out"].T
    return out
